# version 07
# baseline (speedup 1.0000x reference)
import numpy as np
import ml_dtypes
from contextlib import ExitStack

import concourse.bass as bass
import concourse.mybir as mybir
from concourse.ap import AP as APc
from concourse.bass_utils import run_bass_kernel_spmd

F32 = mybir.dt.float32
BF16 = mybir.dt.bfloat16
AF = mybir.ActivationFunctionType
ALU = mybir.AluOpType
AX = mybir.AxisListType

N_CORES = 8
B_LOC = 2
T_SEQ = 2048
T_LOC = B_LOC * T_SEQ
D = 2048

ENGS = ("sync", "gpsimd", "vector", "scalar", "tensor")


def _rname(x):
    if hasattr(x, "tensor"):
        return x.tensor.name
    return x.name


_KB_ID = [0]

DMA_QUEUES = ("sync", "gpsimd", "scalar")
DMA_POOL_DEPTH = 6


class DmaPool:

    def __init__(self, nc, ctx):
        self.sems = {}
        self.cnt = {}
        self.idx = {q: 0 for q in DMA_QUEUES}
        for q in DMA_QUEUES:
            for i in range(DMA_POOL_DEPTH):
                key = f"dq_{q}{i}"
                self.sems[key] = ctx.enter_context(nc.semaphore(key))
                self.cnt[key] = 0

    def slot(self, q):
        i = self.idx[q] % DMA_POOL_DEPTH
        self.idx[q] += 1
        return f"dq_{q}{i}"

    def snapshot(self):
        return [(k, v) for k, v in self.cnt.items() if v > 0]


class KB:

    def __init__(self, nc, ctx, dpool=None):
        self.nc = nc
        _KB_ID[0] += 1
        kid = _KB_ID[0]
        self.sems = {e: ctx.enter_context(nc.semaphore(f"kb{kid}_{e}")) for e in ENGS}
        self.dpool = dpool if dpool is not None else DmaPool(nc, ctx)
        self.cnt = {e: 0 for e in ENGS}
        self.items = {e: [] for e in ENGS}
        self.waited = {e: {} for e in ENGS}
        self.res = {}
        self.bar = []
        self.finals = None

    def _sem(self, key):
        return self.sems[key] if key in self.sems else self.dpool.sems[key]

    def _resource(self, name):
        if name not in self.res:
            self.res[name] = {"w": None, "r": {}}
        return self.res[name]

    def barrier(self):
        self.bar = ([(e, self.cnt[e]) for e in ENGS if self.cnt[e] > 0]
                    + self.dpool.snapshot())

    def op(self, eng, fn, reads=(), writes=(), deps=(), inc=1, _dma_key=None):
        dep_list = list(deps) + self.bar
        rr = [self._resource(_rname(x)) for x in reads]
        ww = [self._resource(_rname(x)) for x in writes]
        for r in rr:
            if r["w"] is not None:
                dep_list.append(r["w"])
        for w in ww:
            if w["w"] is not None:
                dep_list.append(w["w"])
            for e2, v in w["r"].items():
                dep_list.append((e2, v))
        waits = []
        for (e2, v) in dep_list:
            if e2 == eng == "tensor":
                continue
            if v <= self.waited[eng].get(e2, 0):
                continue
            self.waited[eng][e2] = v
            waits.append((e2, v))
        if _dma_key is None:
            self.cnt[eng] += inc
            tok = (eng, self.cnt[eng])
            inc_sem = (eng, inc)
        else:
            self.dpool.cnt[_dma_key] += 16
            tok = (_dma_key, self.dpool.cnt[_dma_key])
            inc_sem = (_dma_key, 16)
        self.items[eng].append((waits, fn, inc_sem))
        for r in rr:
            if r["r"].get(tok[0], 0) < tok[1]:
                r["r"][tok[0]] = tok[1]
        for w in ww:
            w["w"] = tok
            w["r"] = {}
        return tok

    def dma(self, queue, out, in_, reads=(), writes=(), deps=(), **kw):
        key = self.dpool.slot(queue)
        slot_pred = [(key, self.dpool.cnt[key])] if self.dpool.cnt[key] > 0 else []
        return self.op(queue, lambda e: e.dma_start(out=out, in_=in_, **kw),
                       reads=list(reads) + [in_], writes=list(writes) + [out],
                       deps=list(deps) + slot_pred, _dma_key=key)

    def seal(self):
        self.finals = ([(e, self.cnt[e]) for e in ENGS if self.cnt[e] > 0]
                       + self.dpool.snapshot())

    def finalize(self, block):
        if self.finals is None:
            self.seal()
        finals = self.finals
        items = self.items
        ksems = self._sem

        def section(eng_name):
            def _(eng):
                for (waits, fn, inc_sem) in items[eng_name]:
                    for (k, v) in waits:
                        eng.wait_ge(ksems(k), v)
                    ins = fn(eng)
                    ins.then_inc(ksems(inc_sem[0]), inc_sem[1])
                for (k, v) in finals:
                    eng.wait_ge(ksems(k), v)
            return _
        block.sync(section("sync"))
        block.gpsimd(section("gpsimd"))
        block.vector(section("vector"))
        block.scalar(section("scalar"))
        block.tensor(section("tensor"))



def hadamard(n):
    H = np.array([[1.0]])
    while H.shape[0] < n:
        H = np.block([[H, H], [H, -H]])
    return H


CONST_DEFS = {
    "id_bf": ((128, 128), BF16),
    "id_f32": ((128, 128), F32),
    "h128_bf": ((128, 128), BF16),
    "h64_bf": ((128, 64), BF16),
    "ones_bf": ((128, 128), BF16),
    "ut_bf": ((128, 128), BF16),
    "mask_lows": ((128, 128), F32),
    "mask_upi": ((128, 128), F32),
    "mask_ups": ((128, 128), F32),
    "ones_f32": ((128, 128), F32),
    "mq": ((1536, 16), BF16),
    "mqt": ((16, 1536), BF16),
    "sel16": ((16, 2048), F32),
}


def make_consts():
    i = np.arange(128)
    return {
        "id_bf": np.eye(128, dtype=ml_dtypes.bfloat16),
        "id_f32": np.eye(128, dtype=np.float32),
        "h128_bf": (hadamard(128) / np.sqrt(128.0)).astype(ml_dtypes.bfloat16),
        "h64_bf": np.concatenate([hadamard(64) / 8.0] * 2, 0).astype(ml_dtypes.bfloat16),
        "ones_bf": np.ones((128, 128), dtype=ml_dtypes.bfloat16),
        "ut_bf": np.triu(np.ones((128, 128))).astype(ml_dtypes.bfloat16),
        "mask_lows": np.where(i[:, None] > i[None, :], 0.0, -1e30).astype(np.float32),
        "mask_upi": np.where(i[:, None] <= i[None, :], 0.0, -1e30).astype(np.float32),
        "mask_ups": np.where(i[:, None] < i[None, :], 0.0, -1e30).astype(np.float32),
        "ones_f32": np.ones((128, 128), dtype=np.float32),
        "mq": (np.arange(1536)[:, None] // 96 == np.arange(16)[None, :]).astype(ml_dtypes.bfloat16),
        "mqt": (np.arange(1536)[None, :] // 96 == np.arange(16)[:, None]).astype(ml_dtypes.bfloat16),
        "sel16": (np.arange(2048)[None, :] // 128 == np.arange(16)[:, None]).astype(np.float32),
    }


def make_cvals(kb, ctx, values):
    nc = kb.nc
    t = ctx.enter_context(nc.sbuf_tensor("cvals", [128, max(1, len(values))], F32))
    out = {}
    for i, v in enumerate(values):
        kb.op("vector", lambda e, i=i, v=v: e.memset(t[:, i:i + 1], v),
              writes=[t])
        out[v] = t[:, i:i + 1]
    out["_tile"] = t
    return out


def declare_and_load_consts(kb, ctx, names):
    nc = kb.nc
    out = {}
    for n in names:
        shp, dt = CONST_DEFS[n]
        d = nc.dram_tensor("c_" + n, list(shp), dt, kind="ExternalInput").ap()
        t = ctx.enter_context(nc.sbuf_tensor("sb_" + n, list(shp), dt))
        kb.dma("sync", t[:], d)
        out[n] = t
    return out




def pack_rows(ap, row0, n_rows, C):
    return APc(ap.tensor, row0 * C, [[C, 128], [128 * C, n_rows // 128], [1, C]])

def emit_cast_pass(kb, ctx, src, wbf, R, C, tag, rg_rows=512, bufs=2):
    nc = kb.nc
    a = rg_rows // 128
    cb = [ctx.enter_context(nc.sbuf_tensor(f"{tag}_cb{i}", [128, a * C], BF16))
          for i in range(bufs)]
    toks = []
    for rg in range(R // rg_rows):
        buf = cb[rg % bufs]
        sv = pack_rows(src, rg * rg_rows, rg_rows, C)
        dv = pack_rows(wbf, rg * rg_rows, rg_rows, C)
        t_ld = kb.dma("gpsimd", buf[:], sv, reads=[src], writes=[buf])
        toks.append(kb.dma("sync", dv, buf[:], reads=[buf], writes=[wbf]))
    return toks


def emit_tern_cast_pass(kb, ctx, src, wbf, R, C, tag, thr, rg_rows=512):
    nc = kb.nc
    a = rg_rows // 128
    tf = ctx.enter_context(nc.sbuf_tensor(f"{tag}_tf", [128, a * C], F32))
    tg = ctx.enter_context(nc.sbuf_tensor(f"{tag}_tg", [128, a * C], F32))
    cb = ctx.enter_context(nc.sbuf_tensor(f"{tag}_cb", [128, a * C], BF16))
    toks = []
    for rg in range(R // rg_rows):
        sv = pack_rows(src, rg * rg_rows, rg_rows, C)
        dv = pack_rows(wbf, rg * rg_rows, rg_rows, C)
        kb.dma("sync", tf[:], sv, reads=[src], writes=[tf])
        kb.op("vector", lambda e: e.tensor_scalar(
            out=tg[:], in0=tf[:], scalar1=thr[:, 0:1], scalar2=None, op0=ALU.is_gt),
            reads=[tf, thr], writes=[tg])
        kb.op("vector", lambda e: e.tensor_scalar(
            out=tf[:], in0=tf[:], scalar1=thr[:, 1:2], scalar2=None, op0=ALU.is_lt),
            reads=[tf, thr], writes=[tf])
        kb.op("vector", lambda e: e.tensor_tensor(
            out=cb[:], in0=tg[:], in1=tf[:], op=ALU.subtract),
            reads=[tg, tf], writes=[cb])
        toks.append(kb.dma("sync", dv, cb[:], reads=[cb], writes=[wbf]))
    return toks


def emit_transpose_tile_lhsT(kb, ctx, wbf, wtt, H, Dd, tag):
    nc = kb.nc
    nd = Dd // 128
    tb = [ctx.enter_context(nc.sbuf_tensor(f"{tag}_tb{i}", [128, H], BF16,
                                           is_transpose_dest=True)) for i in range(2)]
    toks = []
    for c in range(nd):
        buf = tb[c % 2]
        t_ld = kb.dma("sync", buf[:], wbf[:, c * 128:(c + 1) * 128], transpose=True)
        out_ap = APc(wtt.tensor, c * 128,
                     [[nd * 128, 128], [128 * nd * 128, H // 128], [1, 128]])
        toks.append(kb.dma("sync", out_ap, buf[:], writes=[wtt]))
    return toks


def lhsT_block_ap(wtt, hb, nd):
    return APc(wtt.tensor, hb * 128 * nd * 128, [[nd * 128, 128], [1, nd * 128]])


def emit_transpose_tile_rhsT(kb, ctx, wbf, wtt, Dd, Hf, tag, panel=512):
    nc = kb.nc
    HB = Hf // 128
    n_pan = Dd // panel
    tb = [ctx.enter_context(nc.sbuf_tensor(f"{tag}_tb{i}", [128, Dd], BF16,
                                           is_transpose_dest=True)) for i in range(2)]
    toks = []
    for hc in range(HB):
        buf = tb[hc % 2]
        t_ld = kb.dma("sync", buf[:], wbf[:, hc * 128:(hc + 1) * 128], transpose=True)
        out_ap = APc(wtt.tensor, hc * 128 * panel,
                     [[panel, 128], [HB * 128 * panel, n_pan], [1, panel]])
        toks.append(kb.dma("sync", out_ap, buf[:], writes=[wtt]))
    return toks


def emit_absmean_thr(kb, ctx, src, H, Dd, tag, csts, psum_pool, rg_rows=512):
    nc = kb.nc
    RG = rg_rows
    a = RG // 128
    n_rg = H // RG
    thr = ctx.enter_context(nc.sbuf_tensor(f"{tag}_thr", [128, 4], F32))
    kb.op("vector", lambda e: e.memset(thr[:], 0.0), writes=[thr])
    sctx = ctx
    tf = [sctx.enter_context(nc.sbuf_tensor(f"{tag}_af{i}", [128, a * Dd], F32))
          for i in range(2)]
    parts = sctx.enter_context(nc.sbuf_tensor(f"{tag}_parts", [128, n_rg], F32))
    red = sctx.enter_context(nc.sbuf_tensor(f"{tag}_red", [128, 1], F32))
    redT = sctx.enter_context(nc.sbuf_tensor(f"{tag}_redT", [1, 128], F32))
    thr_d = nc.dram_tensor(f"{tag}_thrd", [1, 4], F32, kind="Internal").ap()
    for rg in range(n_rg):
        buf = tf[rg % 2]
        kb.dma("sync", buf[:], pack_rows(src, rg * RG, RG, Dd),
               reads=[src], writes=[buf])
        kb.op("vector", lambda e, b=buf, i=rg: e.tensor_reduce(
            out=parts[:, i:i + 1], in_=b[:], axis=AX.X, op=ALU.add,
            apply_absolute_value=True), reads=[buf], writes=[parts])
    kb.op("vector", lambda e: e.tensor_reduce(
        out=red[:], in_=parts[:], axis=AX.X, op=ALU.add),
        reads=[parts], writes=[red])
    pT = psum_pool[0]
    kb.op("tensor", lambda e: e.transpose(pT[0:1, 0:128], red[:], csts["id_f32"][:]),
          reads=[red, csts["id_f32"]], writes=[pT])
    kb.op("vector", lambda e: e.tensor_copy(out=redT[:], in_=pT[0:1, 0:128]),
          reads=[pT], writes=[redT])
    kb.op("vector", lambda e: e.tensor_reduce(
        out=thr[0:1, 2:3], in_=redT[:], axis=AX.X, op=ALU.add),
        reads=[redT], writes=[thr])
    kb.op("vector", lambda e: e.tensor_scalar(
        out=thr[0:1, 2:3], in0=thr[0:1, 2:3], scalar1=1.0 / (H * Dd), scalar2=1e-5,
        op0=ALU.mult, op1=ALU.max), reads=[thr], writes=[thr])
    kb.op("vector", lambda e: e.tensor_scalar(
        out=thr[0:1, 0:1], in0=thr[0:1, 2:3], scalar1=0.7, scalar2=None,
        op0=ALU.mult), reads=[thr], writes=[thr])
    kb.op("vector", lambda e: e.tensor_scalar(
        out=thr[0:1, 1:2], in0=thr[0:1, 0:1], scalar1=-1.0, scalar2=None,
        op0=ALU.mult), reads=[thr], writes=[thr])
    kb.dma("sync", thr_d, thr[0:1, :])
    kb.dma("sync", thr[:], APc(thr_d.tensor, 0, [[0, 128], [1, 4]]))
    return thr


def emit_fold_h128(kb, ctx, w2pp, csts, psum_pool, tag="w2h"):
    nc = kb.nc
    h128 = csts["h128_bf"]
    bi = [ctx.enter_context(nc.sbuf_tensor(f"{tag}_i{i}", [128, 2048], BF16))
          for i in range(2)]
    bo = [ctx.enter_context(nc.sbuf_tensor(f"{tag}_o{i}", [128, 2048], BF16))
          for i in range(2)]
    for g in range(64):
        fi = bi[g % 2]
        fo = bo[g % 2]
        in_ap = APc(w2pp.tensor, g * 4 * 128 * 512, [[512, 128], [128 * 512, 4], [1, 512]])
        kb.dma("sync", fi[:], in_ap, reads=[w2pp], writes=[fi])
        for b4 in range(4):
            bank = psum_pool[4 + b4 % 2]
            kb.op("tensor", lambda e, b=bank, f=fi, b4=b4: e.matmul(
                b[:], h128[:], f[:, b4 * 512:(b4 + 1) * 512], start=True, stop=True),
                reads=[fi, h128], writes=[bank])
            eng = "vector" if b4 % 2 == 0 else "scalar"
            def drain(e, b=bank, f=fo, b4=b4, eng=eng):
                if eng == "vector":
                    return e.tensor_copy(out=f[:, b4 * 512:(b4 + 1) * 512], in_=b[:])
                return e.copy(out=f[:, b4 * 512:(b4 + 1) * 512], in_=b[:])
            kb.op(eng, drain, reads=[bank], writes=[fo])
        out_ap = APc(w2pp.tensor, g * 4 * 128 * 512, [[512, 128], [128 * 512, 4], [1, 512]])
        kb.dma("sync", out_ap, fo[:], reads=[fo], writes=[w2pp])


def emit_fold_w2(kb, ctx, w2bfT, w2pp, csts, psum_pool, tag="w2f"):
    nc = kb.nc
    Dd = 2048
    fin = [ctx.enter_context(nc.sbuf_tensor(f"{tag}_fi{i}", [128, Dd], BF16))
           for i in range(2)]
    fout = [ctx.enter_context(nc.sbuf_tensor(f"{tag}_fo{i}", [128, Dd], BF16))
            for i in range(2)]
    h64 = csts["h64_bf"]
    toks = []
    for j0 in range(64):
        fi = fin[j0 % 2]
        fo = fout[j0 % 2]
        kb.dma("sync", fi[0:64, :],
               APc(w2bfT.tensor, j0 * Dd, [[128 * Dd, 64], [1, Dd]]))
        kb.dma("sync", fi[64:128, :],
               APc(w2bfT.tensor, (j0 + 64) * Dd, [[128 * Dd, 64], [1, Dd]]))
        for half in range(2):
            for ds in range(4):
                bank = psum_pool[half * 4 + ds]
                kb.op("tensor", lambda e, b=bank, f=fi, h=half, d=ds: e.matmul(
                    b[0:64, :], h64[h * 64:(h + 1) * 64, 0:64],
                    f[h * 64:(h + 1) * 64, d * 512:(d + 1) * 512],
                    start=True, stop=True), reads=[fi, h64], writes=[bank])
                eng = "vector" if ds % 2 == 0 else "scalar"
                def drain(e, b=bank, f=fo, h=half, d=ds, eng=eng):
                    if eng == "vector":
                        return e.tensor_copy(
                            out=f[h * 64:(h + 1) * 64, d * 512:(d + 1) * 512],
                            in_=b[0:64, :])
                    return e.copy(
                        out=f[h * 64:(h + 1) * 64, d * 512:(d + 1) * 512],
                        in_=b[0:64, :])
                kb.op(eng, drain, reads=[bank], writes=[fo])
        sA = kb.dma("sync",
                    APc(w2pp.tensor, j0 * 512,
                        [[128 * 512, 64], [64 * 128 * 512, 4], [1, 512]]),
                    fo[0:64, :], writes=[w2pp])
        sB = kb.dma("sync",
                    APc(w2pp.tensor, (j0 + 64) * 512,
                        [[128 * 512, 64], [64 * 128 * 512, 4], [1, 512]]),
                    fo[64:128, :], writes=[w2pp])
        toks += [sA, sB]
    return toks



def emit_ln_rowtile(kb, bufs, xin_rows, dest_fn, csts, pbanks, par=0):
    ln_in = bufs["ln_in"][par] if isinstance(bufs["ln_in"], list) else bufs["ln_in"]
    lnbf = bufs["lnbf"][par] if isinstance(bufs["lnbf"], list) else bufs["lnbf"]
    dump = lnbf
    stat0 = bufs["stat"]
    stat = stat0[:, par * 8:par * 8 + 8] if isinstance(bufs["ln_in"], list) else stat0
    kb.dma("sync", ln_in[:], xin_rows)
    kb.op("vector", lambda e: e.tensor_reduce(
        out=stat[:, 0:1], in_=ln_in[:], axis=AX.X, op=ALU.add),
        reads=[ln_in], writes=[stat0])
    kb.op("vector", lambda e: e.scalar_tensor_tensor(
        out=dump[:], in0=ln_in[:], scalar=0.0, in1=ln_in[:],
        op0=ALU.add, op1=ALU.mult, accum_out=stat[:, 1:2]),
        reads=[ln_in], writes=[dump, stat0])
    kb.op("vector", lambda e: e.tensor_scalar(
        out=stat[:, 2:3], in0=stat[:, 0:1], scalar1=1.0 / D, scalar2=None,
        op0=ALU.mult), reads=[stat0], writes=[stat0])
    kb.op("vector", lambda e: e.tensor_tensor(
        out=stat[:, 3:4], in0=stat[:, 2:3], in1=stat[:, 2:3], op=ALU.mult),
        reads=[stat0], writes=[stat0])
    kb.op("vector", lambda e: e.tensor_scalar(
        out=stat[:, 4:5], in0=stat[:, 1:2], scalar1=1.0 / D, scalar2=None,
        op0=ALU.mult), reads=[stat0], writes=[stat0])
    kb.op("vector", lambda e: e.tensor_tensor(
        out=stat[:, 4:5], in0=stat[:, 4:5], in1=stat[:, 3:4], op=ALU.subtract),
        reads=[stat0], writes=[stat0])
    kb.op("vector", lambda e: e.tensor_scalar(
        out=stat[:, 4:5], in0=stat[:, 4:5], scalar1=1e-5, scalar2=None,
        op0=ALU.add), reads=[stat0], writes=[stat0])
    kb.op("scalar", lambda e: e.activation(
        out=stat[:, 5:6], in_=stat[:, 4:5], func=AF.Ln, bias=0.0, scale=1.0),
        reads=[stat0], writes=[stat0])
    kb.op("scalar", lambda e: e.activation(
        out=stat[:, 6:7], in_=stat[:, 5:6], func=AF.Exp, bias=0.0, scale=-0.5),
        reads=[stat0], writes=[stat0])
    kb.op("vector", lambda e: e.tensor_scalar(
        out=lnbf[:], in0=ln_in[:], scalar1=stat[:, 2:3], scalar2=stat[:, 6:7],
        op0=ALU.subtract, op1=ALU.mult), reads=[ln_in, stat0], writes=[lnbf])
    id_bf = csts["id_bf"]
    for k in range(16):
        pb = pbanks[k % 2]
        pb_bf = pb[:, 0:256].bitcast(BF16)[:, 0:128]
        kb.op("tensor", lambda e, k=k, pb_bf=pb_bf: e.transpose(
            pb_bf, lnbf[:, k * 128:(k + 1) * 128], id_bf[:]),
            reads=[lnbf, id_bf], writes=[pb])
        dst_t, col0 = dest_fn(k)
        kb.op("vector", lambda e, pb_bf=pb_bf, dst_t=dst_t, col0=col0: e.tensor_copy(
            out=dst_t[:, col0:col0 + 128], in_=pb_bf),
            reads=[pb], writes=[dst_t])



def emit_ffn(kb, ctx, xin, xout, wtt1, wtt3, wpp2, Hf, csts, psum_pool,
             tag="ffn", relu2=False, scale_tile=None):
    nc = kb.nc
    HB = Hf // 128
    TT = 1024
    sb = lambda *a, **k: ctx.enter_context(nc.sbuf_tensor(*a, **k))
    Xp = sb(f"{tag}_Xp", [128, 16 * TT], BF16)
    hid = sb(f"{tag}_hid", [128, HB * TT], BF16)
    _lnbf = sb(f"{tag}_lnb", [128, D], BF16)
    ln_bufs = dict(ln_in=[sb(f"{tag}_lni{i}", [128, D], F32) for i in range(2)],
                   stat=sb(f"{tag}_st", [128, 16], F32),
                   lnbf=[_lnbf, _lnbf])
    w1b = [sb(f"{tag}_w1b{i}", [128, D], BF16) for i in range(2)]
    _w3b = sb(f"{tag}_w3b0", [128, D], BF16) if wtt3 is not None else None
    w3b = [_w3b, _w3b] if wtt3 is not None else None
    w2b = [sb(f"{tag}_w2b{i}", [128, 2 * 512], BF16) for i in range(2)]
    sg = [sb(f"{tag}_sg{i}", [128, 512], F32) for i in range(2)]
    res = [sb(f"{tag}_res{i}", [128, 512], F32) for i in range(2)]

    for tile in range(T_LOC // TT):
        row0 = tile * TT
        for r in range(8):
            emit_ln_rowtile(kb, ln_bufs, xin[row0 + r * 128:row0 + (r + 1) * 128, :],
                            lambda k, r=r: (Xp, k * TT + r * 128), csts,
                            (psum_pool[6], psum_pool[7]), par=r % 2)
        for hb in range(HB):
            wb = w1b[hb % 2]
            kb.dma("sync", wb[:], lhsT_block_ap(wtt1, hb, 16), reads=[wtt1], writes=[wb])
            mats = [(0, wb)]
            if wtt3 is not None:
                w3 = w3b[hb % 2]
                kb.dma("sync", w3[:], lhsT_block_ap(wtt3, hb, 16), reads=[wtt3], writes=[w3])
                mats.append((1, w3))
            for (mi, wtile) in mats:
                for chj in range(TT // 512):
                    bank = psum_pool[mi * 2 + chj]
                    for c in range(16):
                        kb.op("tensor", lambda e, b=bank, w=wtile, c=c, chj=chj:
                              e.matmul(b[:], w[:, c * 128:(c + 1) * 128],
                                       Xp[:, c * TT + chj * 512:c * TT + chj * 512 + 512],
                                       start=(c == 0), stop=(c == 15)),
                              reads=[wtile, Xp], writes=[bank])
            for chj in range(TT // 512):
                bankg = psum_pool[0 + chj]
                sgb = sg[chj]
                col = hb * TT + chj * 512
                if relu2:
                    kb.op("scalar", lambda e, b=bankg, s=sgb: e.activation(
                        out=s[:], in_=b[:], func=AF.Relu, bias=0.0, scale=1.0),
                        reads=[bankg], writes=[sgb])
                    kb.op("vector", lambda e, s=sgb, col=col: e.tensor_tensor(
                        out=hid[:, col:col + 512], in0=s[:], in1=s[:], op=ALU.mult),
                        reads=[sgb], writes=[hid])
                else:
                    kb.op("scalar", lambda e, b=bankg, s=sgb: e.activation(
                        out=s[:], in_=b[:], func=AF.Silu, bias=0.0, scale=1.0),
                        reads=[bankg], writes=[sgb])
                    bankl = psum_pool[2 + chj]
                    kb.op("vector", lambda e, s=sgb, b=bankl, col=col: e.tensor_tensor(
                        out=hid[:, col:col + 512], in0=s[:], in1=b[:], op=ALU.mult),
                        reads=[sgb, bankl], writes=[hid])
        for pn in range(D // 512):
            for hbg in range((HB + 1) // 2):
                nb4 = min(2, HB - hbg * 2)
                wb = w2b[hbg % 2]
                in_ap = APc(wpp2.tensor, (pn * HB + hbg * 2) * 128 * 512,
                            [[512, 128], [128 * 512, nb4], [1, 512]])
                kb.dma("sync", wb[:, 0:nb4 * 512], in_ap, reads=[wpp2], writes=[wb])
                for b4 in range(nb4):
                    hb = hbg * 2 + b4
                    for tg in range(TT // 128):
                        bank = psum_pool[tg]
                        kb.op("tensor", lambda e, b=bank, hb=hb, tg=tg, b4=b4, w=wb:
                              e.matmul(b[:], hid[:, hb * TT + tg * 128: hb * TT + tg * 128 + 128],
                                       w[:, b4 * 512:(b4 + 1) * 512],
                                       start=(hb == 0), stop=(hb == HB - 1)),
                              reads=[wb, hid], writes=[bank])
            for tg in range(TT // 128):
                bank = psum_pool[tg]
                rs = res[tg % 2]
                kb.dma("scalar", rs[:],
                       xin[row0 + tg * 128:row0 + (tg + 1) * 128,
                           pn * 512:(pn + 1) * 512], writes=[rs])
                if relu2 and scale_tile is not None:
                    kb.op("vector", lambda e, b=bank, rs=rs: e.scalar_tensor_tensor(
                        out=rs[:], in0=b[:], scalar=scale_tile[:, 0:1], in1=rs[:],
                        op0=ALU.mult, op1=ALU.add),
                        reads=[bank, scale_tile, rs], writes=[rs])
                else:
                    kb.op("vector", lambda e, b=bank, rs=rs: e.scalar_tensor_tensor(
                        out=rs[:], in0=b[:], scalar=0.5, in1=rs[:],
                        op0=ALU.mult, op1=ALU.add), reads=[bank, rs], writes=[rs])
                kb.dma("sync",
                       xout[row0 + tg * 128:row0 + (tg + 1) * 128,
                            pn * 512:(pn + 1) * 512], rs[:], reads=[rs])



def build_prep(nprep):
    nc = bass.Bass("TRN2", target_bir_lowering=False)
    W1 = nc.dram_tensor("W1", [8192, 2048], F32, kind="ExternalInput").ap()
    W3 = nc.dram_tensor("W3", [8192, 2048], F32, kind="ExternalInput").ap()
    W2 = nc.dram_tensor("W2", [2048, 8192], F32, kind="ExternalInput").ap()
    WFC = nc.dram_tensor("WFC", [5120, 2048], F32, kind="ExternalInput").ap()
    WPJ = nc.dram_tensor("WPJ", [2048, 5120], F32, kind="ExternalInput").ap()

    w1tt = nc.dram_tensor("w1tt", [64 * 2048, 128], BF16, kind="ExternalOutput").ap()
    w3tt = nc.dram_tensor("w3tt", [64 * 2048, 128], BF16, kind="ExternalOutput").ap()
    w2pp = nc.dram_tensor("w2pp", [4 * 64 * 128, 512], BF16, kind="ExternalOutput").ap()
    wfctt = nc.dram_tensor("wfctt", [40 * 2048, 128], BF16, kind="ExternalOutput").ap()
    wpjtt = nc.dram_tensor("wpjtt", [4 * 40 * 128, 512], BF16, kind="ExternalOutput").ap()
    scale_o = nc.dram_tensor("scale_o", [1, 1], F32, kind="ExternalOutput").ap()

    w1bf = nc.dram_tensor("w1bf", [8192, 2048], BF16, kind="Internal").ap()
    w3bf = nc.dram_tensor("w3bf", [8192, 2048], BF16, kind="Internal").ap()
    w2bf = nc.dram_tensor("w2bf", [2048, 8192], BF16, kind="Internal").ap()
    w2bfT = nc.dram_tensor("w2bfT", [8192, 2048], BF16, kind="Internal").ap()
    wfcbf = nc.dram_tensor("wfcbf", [5120, 2048], BF16, kind="Internal").ap()
    wpjbf = nc.dram_tensor("wpjbf", [2048, 5120], BF16, kind="Internal").ap()

    with ExitStack() as ctx:
        kb = KB(nc, ctx)
        psum_pool = [ctx.enter_context(nc.psum_tensor(f"pp{i}", [128, 512], F32))
                     for i in range(8)]
        csts = declare_and_load_consts(kb, ctx, ["id_f32", "h64_bf"])

        stile_am = [None, None]
        with ExitStack() as sctx:
            emit_cast_pass(kb, sctx, W1, w1bf, 8192, 2048, "w1c")
            emit_transpose_tile_lhsT(kb, sctx, w1bf, w1tt, 8192, 2048, "w1t")
            emit_cast_pass(kb, sctx, W3, w3bf, 8192, 2048, "w3c")
            emit_transpose_tile_lhsT(kb, sctx, w3bf, w3tt, 8192, 2048, "w3t")
            emit_cast_pass(kb, sctx, W2, w2bf, 2048, 8192, "w2c", rg_rows=256, bufs=1)
            emit_transpose_tile_rhsT_natural(kb, sctx, w2bf, w2bfT, "w2t")
            emit_fold_w2(kb, sctx, w2bfT, w2pp, csts, psum_pool)
        kb.barrier()
        am_fc = ctx.enter_context(nc.sbuf_tensor("pam_fc", [1, 1], F32))
        am_pj = ctx.enter_context(nc.sbuf_tensor("pam_pj", [1, 1], F32))
        with ExitStack() as sctx:
            thr_fc = emit_absmean_thr(kb, sctx, WFC, 5120, 2048, "amf", csts,
                                      psum_pool, rg_rows=256)
            emit_tern_cast_pass(kb, sctx, WFC, wfcbf, 5120, 2048, "tfc", thr_fc,
                                rg_rows=256)
            emit_transpose_tile_lhsT(kb, sctx, wfcbf, wfctt, 5120, 2048, "wft")
            kb.op("vector", lambda e, t=thr_fc: e.tensor_copy(out=am_fc[:], in_=t[0:1, 2:3]),
                  reads=[thr_fc], writes=[am_fc])
        kb.barrier()
        with ExitStack() as sctx:
            thr_pj = emit_absmean_thr(kb, sctx, WPJ, 2048, 5120, "amp", csts,
                                      psum_pool, rg_rows=128)
            emit_tern_cast_pass(kb, sctx, WPJ, wpjbf, 2048, 5120, "tpj", thr_pj,
                                rg_rows=128)
            emit_transpose_tile_rhsT(kb, sctx, wpjbf, wpjtt, 2048, 5120, "wpt")
            kb.op("vector", lambda e, t=thr_pj: e.tensor_copy(out=am_pj[:], in_=t[0:1, 2:3]),
                  reads=[thr_pj], writes=[am_pj])
        kb.barrier()
        thr_fc, thr_pj = am_fc, am_pj
        stile = ctx.enter_context(nc.sbuf_tensor("stile", [1, 1], F32))
        kb.op("vector", lambda e, t=thr_fc: e.tensor_tensor(
            out=stile[:], in0=t[0:1, 0:1], in1=t[0:1, 0:1], op=ALU.mult),
            reads=[thr_fc], writes=[stile])
        kb.op("vector", lambda e, t=thr_pj: e.tensor_tensor(
            out=stile[:], in0=stile[:], in1=t[0:1, 0:1], op=ALU.mult),
            reads=[stile, thr_pj], writes=[stile])
        kb.dma("sync", scale_o, stile[:], reads=[stile])

        block = ctx.enter_context(nc.Block())
        kb.finalize(block)
    return nc


def emit_transpose_tile_rhsT_natural(kb, ctx, wbf, wT, tag):
    nc = kb.nc
    tb = [ctx.enter_context(nc.sbuf_tensor(f"{tag}_tb{i}", [128, 2048], BF16,
                                           is_transpose_dest=True)) for i in range(2)]
    for hc in range(64):
        buf = tb[hc % 2]
        kb.dma("sync", buf[:], wbf[:, hc * 128:(hc + 1) * 128], transpose=True)
        kb.dma("sync", wT[hc * 128:(hc + 1) * 128, :], buf[:])


def build_ffn_stage(Hf, relu2):
    nc = bass.Bass("TRN2", target_bir_lowering=False)
    HB = Hf // 128
    xin = nc.dram_tensor("xin", [T_LOC, D], F32, kind="ExternalInput").ap()
    xout = nc.dram_tensor("xout", [T_LOC, D], F32, kind="ExternalOutput").ap()
    if relu2:
        wtt1 = nc.dram_tensor("wfctt", [HB * 2048, 128], BF16, kind="ExternalInput").ap()
        wpp2 = nc.dram_tensor("wpjtt", [4 * HB * 128, 512], BF16, kind="ExternalInput").ap()
        scale_i = nc.dram_tensor("scale_i", [1, 1], F32, kind="ExternalInput").ap()
        wtt3 = None
    else:
        wtt1 = nc.dram_tensor("w1tt", [HB * 2048, 128], BF16, kind="ExternalInput").ap()
        wtt3 = nc.dram_tensor("w3tt", [HB * 2048, 128], BF16, kind="ExternalInput").ap()
        wpp2 = nc.dram_tensor("w2pp", [4 * HB * 128, 512], BF16, kind="ExternalInput").ap()
    with ExitStack() as ctx:
        kb = KB(nc, ctx)
        psum_pool = [ctx.enter_context(nc.psum_tensor(f"pp{i}", [128, 512], F32))
                     for i in range(8)]
        names = ["id_bf"] + ([] if relu2 else ["h128_bf"])
        csts = declare_and_load_consts(kb, ctx, names)
        scale_tile = None
        if relu2:
            scale_tile = ctx.enter_context(nc.sbuf_tensor("sct", [128, 1], F32))
            kb.dma("sync", scale_tile[:], APc(scale_i.tensor, 0, [[0, 128], [1, 1]]))
        emit_ffn(kb, ctx, xin, xout, wtt1, wtt3, wpp2, Hf, csts, psum_pool,
                 tag="f", relu2=relu2, scale_tile=scale_tile)
        block = ctx.enter_context(nc.Block())
        kb.finalize(block)
    return nc





def emit_l1attn(kb, ctx, xin, xout, W, bvec, qsp, qsp2, csts, psum_pool, tag="a1"):
    nc = kb.nc
    sb = lambda *a, **k: ctx.enter_context(nc.sbuf_tensor(*a, **k))
    TT = T_LOC
    Xp = sb(f"{tag}_Xp", [128, 16 * TT], BF16)
    ln_bufs = dict(ln_in=[sb(f"{tag}_lni{i}", [128, D], F32) for i in range(2)],
                   stat=sb(f"{tag}_st", [128, 16], F32),
                   lnbf=[sb(f"{tag}_lnb{i}", [128, D], BF16) for i in range(2)])
    for r in range(32):
        emit_ln_rowtile(kb, ln_bufs, xin[r * 128:(r + 1) * 128, :],
                        lambda k, r=r: (Xp, k * TT + r * 128), csts,
                        (psum_pool[6], psum_pool[7]), par=r % 2)

    brow = sb(f"{tag}_brow", [48, 128], F32)
    ball = sb(f"{tag}_ball", [128, 48], F32)
    kb.dma("sync", brow[:], APc(bvec.tensor, 0, [[128, 48], [1, 128]]))
    kb.op("tensor", lambda e: e.transpose(psum_pool[5][0:128, 0:48], brow[:],
                                          csts["id_f32"][0:48, 0:48]),
          reads=[brow, csts["id_f32"]], writes=[psum_pool[5]])
    kb.op("vector", lambda e: e.tensor_copy(out=ball[:], in_=psum_pool[5][0:128, 0:48]),
          reads=[psum_pool[5]], writes=[ball])

    wq = [sb(f"{tag}_wq{i}", [128, 16 * 128], BF16) for i in range(2)]
    wv = [sb(f"{tag}_wv{i}", [128, 16 * 128], BF16) for i in range(2)]
    qf = sb(f"{tag}_qf", [128, 512], F32)
    vf = sb(f"{tag}_vf", [128, 512], F32)
    dmp = sb(f"{tag}_dq", [128, 512], BF16)
    qfull = sb(f"{tag}_qfl", [128, TT], BF16)
    parts = sb(f"{tag}_pts", [128, 24], F32)
    sqq = sb(f"{tag}_sqq", [128, 32], F32)
    sqk = sb(f"{tag}_sqk", [128, 32], F32)
    skv = sb(f"{tag}_skv", [128, 32], F32)
    s_all = sb(f"{tag}_sal", [128, 32], F32)

    def wload(buf, ft):
        in_ap = APc(W.tensor, ft * 128, [[6144, 128], [6144 * 128, 16], [1, 128]])
        return kb.dma("gpsimd", buf[:], in_ap, reads=[W], writes=[buf])

    def project(buf, ft_local, tc, bank):
        for c in range(16):
            kb.op("tensor", lambda e, b=bank, w=buf, c=c, tc=tc:
                  e.matmul(b[:], w[:, c * 128:(c + 1) * 128],
                           Xp[:, c * TT + tc * 512:c * TT + tc * 512 + 512],
                           start=(c == 0), stop=(c == 15)),
                  reads=[buf, Xp], writes=[bank])

    for ft in range(16):
        buf = wq[ft % 2]
        wload(buf, ft)
        for tc in range(8):
            bank = psum_pool[tc % 2]
            project(buf, ft, tc, bank)
            kb.op("vector", lambda e, b=bank, ft=ft: e.tensor_scalar(
                out=qf[:], in0=b[:], scalar1=ball[:, ft:ft + 1], scalar2=None,
                op0=ALU.add), reads=[bank, ball], writes=[qf])
            kb.op("vector", lambda e, tc=tc: e.scalar_tensor_tensor(
                out=dmp[:], in0=qf[:], scalar=0.0, in1=qf[:], op0=ALU.add,
                op1=ALU.mult, accum_out=parts[:, tc:tc + 1]),
                reads=[qf], writes=[dmp, parts])
            kb.op("scalar", lambda e, tc=tc: e.copy(
                out=qfull[:, tc * 512:(tc + 1) * 512], in_=qf[:]),
                reads=[qf], writes=[qfull])
        for b2 in range(2):
            kb.op("vector", lambda e, ft=ft, b2=b2: e.tensor_reduce(
                out=sqq[:, ft * 2 + b2:ft * 2 + b2 + 1], in_=parts[:, b2 * 4:b2 * 4 + 4],
                axis=AX.X, op=ALU.add), reads=[parts], writes=[sqq])
        kb.dma("sync", qsp[ft * 128:(ft + 1) * 128, :], qfull[:],
               reads=[qfull], writes=[qsp])

    for ft in range(16):
        bufk = wq[ft % 2]
        bufv = wv[ft % 2]
        wload(bufk, 16 + ft)
        wload(bufv, 32 + ft)
        for tc in range(8):
            bank = psum_pool[tc % 2]
            bank2 = psum_pool[2 + tc % 2]
            project(bufk, ft, tc, bank)
            project(bufv, ft, tc, bank2)
            kb.op("vector", lambda e, b=bank, ft=ft: e.tensor_scalar(
                out=qf[:], in0=b[:], scalar1=ball[:, 16 + ft:17 + ft], scalar2=None,
                op0=ALU.add), reads=[bank, ball], writes=[qf])
            kb.op("vector", lambda e, b=bank2, ft=ft: e.tensor_scalar(
                out=vf[:], in0=b[:], scalar1=ball[:, 32 + ft:33 + ft], scalar2=None,
                op0=ALU.add), reads=[bank2, ball], writes=[vf])
            kb.op("vector", lambda e, tc=tc: e.scalar_tensor_tensor(
                out=dmp[:], in0=qf[:], scalar=0.0, in1=qf[:], op0=ALU.add,
                op1=ALU.mult, accum_out=parts[:, 8 + tc:9 + tc]),
                reads=[qf], writes=[dmp, parts])
            kb.op("vector", lambda e, tc=tc: e.scalar_tensor_tensor(
                out=dmp[:], in0=qf[:], scalar=0.0, in1=vf[:], op0=ALU.add,
                op1=ALU.mult, accum_out=parts[:, 16 + tc:17 + tc]),
                reads=[qf, vf], writes=[dmp, parts])
        for b2 in range(2):
            kb.op("vector", lambda e, ft=ft, b2=b2: e.tensor_reduce(
                out=sqk[:, ft * 2 + b2:ft * 2 + b2 + 1],
                in_=parts[:, 8 + b2 * 4:8 + b2 * 4 + 4],
                axis=AX.X, op=ALU.add), reads=[parts], writes=[sqk])
            kb.op("vector", lambda e, ft=ft, b2=b2: e.tensor_reduce(
                out=skv[:, ft * 2 + b2:ft * 2 + b2 + 1],
                in_=parts[:, 16 + b2 * 4:16 + b2 * 4 + 4],
                axis=AX.X, op=ALU.add), reads=[parts], writes=[skv])

    kb.op("vector", lambda e: e.tensor_tensor(
        out=s_all[:], in0=sqk[:], in1=sqq[:], op=ALU.mult),
        reads=[sqk, sqq], writes=[s_all])
    kb.op("scalar", lambda e: e.activation(
        out=s_all[:], in_=s_all[:], func=AF.Ln, bias=0.0, scale=1.0),
        reads=[s_all], writes=[s_all])
    kb.op("scalar", lambda e: e.activation(
        out=s_all[:], in_=s_all[:], func=AF.Exp, bias=0.0, scale=-0.5),
        reads=[s_all], writes=[s_all])
    kb.op("vector", lambda e: e.tensor_tensor(
        out=s_all[:], in0=s_all[:], in1=skv[:], op=ALU.mult),
        reads=[s_all, skv], writes=[s_all])

    for ft in range(16):
        kb.dma("sync", qfull[:], qsp[ft * 128:(ft + 1) * 128, :],
               reads=[qsp], writes=[qfull])
        for b2 in range(2):
            kb.op("vector", lambda e, ft=ft, b2=b2: e.tensor_scalar(
                out=qfull[:, b2 * T_SEQ:(b2 + 1) * T_SEQ],
                in0=qfull[:, b2 * T_SEQ:(b2 + 1) * T_SEQ],
                scalar1=s_all[:, ft * 2 + b2:ft * 2 + b2 + 1], scalar2=None,
                op0=ALU.mult), reads=[qfull, s_all], writes=[qfull])
        kb.dma("sync", qsp2[ft * 128:(ft + 1) * 128, :], qfull[:],
               reads=[qfull], writes=[qsp2])

    qt = [sb(f"{tag}_qt{i}", [128, D], BF16, is_transpose_dest=True) for i in range(2)]
    _rs = sb(f"{tag}_rs0", [128, D], F32)
    rs = [_rs, _rs]
    for rt in range(32):
        tq = qt[rt % 2]
        tr = rs[rt % 2]
        kb.dma("sync", tq[:], qsp2[:, rt * 128:(rt + 1) * 128],
               reads=[qsp2], writes=[tq], transpose=True)
        kb.dma("scalar", tr[:], xin[rt * 128:(rt + 1) * 128, :], writes=[tr])
        kb.op("vector", lambda e, tq=tq, tr=tr: e.tensor_tensor(
            out=tr[:], in0=tr[:], in1=tq[:], op=ALU.add),
            reads=[tq, tr], writes=[tr])
        kb.dma("sync", xout[rt * 128:(rt + 1) * 128, :], tr[:], reads=[tr])


def build_l1attn_stage():
    nc = bass.Bass("TRN2", target_bir_lowering=False)
    xin = nc.dram_tensor("xin", [T_LOC, D], F32, kind="ExternalInput").ap()
    W = nc.dram_tensor("W", [2048, 6144], F32, kind="ExternalInput").ap()
    bvec = nc.dram_tensor("b", [6144], F32, kind="ExternalInput").ap()
    xout = nc.dram_tensor("xout", [T_LOC, D], F32, kind="ExternalOutput").ap()
    qsp = nc.dram_tensor("qsp", [2048, T_LOC], BF16, kind="Internal").ap()
    qsp2 = nc.dram_tensor("qsp2", [2048, T_LOC], BF16, kind="Internal").ap()
    with ExitStack() as ctx:
        kb = KB(nc, ctx)
        psum_pool = [ctx.enter_context(nc.psum_tensor(f"pp{i}", [128, 512], F32))
                     for i in range(8)]
        csts = declare_and_load_consts(kb, ctx, ["id_bf", "id_f32"])
        emit_l1attn(kb, ctx, xin, xout, W, bvec, qsp, qsp2, csts, psum_pool)
        block = ctx.enter_context(nc.Block())
        kb.finalize(block)
    return nc


H_HEADS = 16
SCAN_SLOTS = 2
SCAN_RR = True
SCAN_PART = 0
SCAN_PACK = False
DK = 96
DV = 192
CH = 128
NCH = T_SEQ // CH


def emit_l0attn(kb, ctx, xin, xout, wts, scr, csts, psum_pool, cv, tag="a0"):
    nc = kb.nc
    sb = lambda *a, **k: ctx.enter_context(nc.sbuf_tensor(*a, **k))
    qsp, ksp, vsp, onsp, xsp = scr["qsp"], scr["ksp"], scr["vsp"], scr["onsp"], scr["xsp"]

    cnv = sb(f"{tag}_cnv", [128, 192], F32)
    mq_sb = sb(f"{tag}_mq", [128, 12 * 16], BF16)
    mqt_sb = sb(f"{tag}_mqt", [16, 12 * 128], BF16)
    dt_sb = sb(f"{tag}_dt", [16, 1], F32)
    nae = sb(f"{tag}_nae", [16, 1], F32)
    nw_bc = sb(f"{tag}_nw", [128, DV], F32)
    Gt_all = sb(f"{tag}_Gt", [128, 512], F32)
    EGt_all = sb(f"{tag}_EGt", [128, 512], F32)
    NBt_all = sb(f"{tag}_NBt", [128, 512], F32)
    PBt_all = sb(f"{tag}_PBt", [128, 512], F32)

    kb.dma("sync", cnv[:, 0:48],
           APc(wts["conv_q"].tensor, 0, [[4, 128], [128 * 4, 12], [1, 4]]))
    kb.dma("sync", cnv[:, 48:96],
           APc(wts["conv_k"].tensor, 0, [[4, 128], [128 * 4, 12], [1, 4]]))
    kb.dma("sync", cnv[:, 96:192],
           APc(wts["conv_v"].tensor, 0, [[4, 128], [128 * 4, 24], [1, 4]]))
    kb.dma("sync", mq_sb[:], APc(csts["mq_d"].tensor, 0, [[16, 128], [128 * 16, 12], [1, 16]]))
    kb.dma("sync", mqt_sb[:], csts["mqt_d"])
    kb.dma("sync", dt_sb[:], APc(wts["dt_bias"].tensor, 0, [[1, 16], [1, 1]]))
    kb.dma("sync", nae[:], APc(wts["A_log"].tensor, 0, [[1, 16], [1, 1]]))
    kb.op("scalar", lambda e: e.activation(out=nae[:], in_=nae[:], func=AF.Exp,
                                           bias=0.0, scale=1.0),
          reads=[nae], writes=[nae])
    kb.op("vector", lambda e: e.tensor_scalar(out=nae[:], in0=nae[:], scalar1=-1.0,
                                              scalar2=None, op0=ALU.mult),
          reads=[nae], writes=[nae])
    kb.dma("sync", nw_bc[:], APc(wts["norm_weight"].tensor, 0, [[0, 128], [1, DV]]))

    id_bf, id_f32, ones_f32 = csts["id_bf"], csts["id_f32"], csts["ones_f32"]
    mask_lows, mask_upi = csts["mask_lows"], csts["mask_upi"]

    with ExitStack() as actx:
        asb = lambda *a, **k: actx.enter_context(nc.sbuf_tensor(*a, **k))
        Xp = asb(f"{tag}_Xp", [128, 16 * T_LOC], BF16)
        with ExitStack() as lctx:
            lsb = lambda *a, **k: lctx.enter_context(nc.sbuf_tensor(*a, **k))
            ln_bufs = dict(ln_in=[lsb(f"{tag}_lni{i}", [128, D], F32) for i in range(2)],
                           stat=lsb(f"{tag}_lst", [128, 16], F32),
                           lnbf=[lsb(f"{tag}_lnb{i}", [128, D], BF16) for i in range(2)])
            for r in range(32):
                emit_ln_rowtile(kb, ln_bufs, xin[r * 128:(r + 1) * 128, :],
                                lambda k, r=r: (Xp, k * T_LOC + r * 128), csts,
                                (psum_pool[6], psum_pool[7]), par=r % 2)
        kb.barrier()

        wbq = asb(f"{tag}_wbq", [128, 16 * 16], BF16)
        wba = asb(f"{tag}_wba", [128, 16 * 16], BF16)
        kb.dma("gpsimd", wbq[:], APc(wts["Wb"].tensor, 0, [[16, 128], [128 * 16, 16], [1, 16]]),
               reads=[wts["Wb"]], writes=[wbq])
        kb.dma("gpsimd", wba[:], APc(wts["Wa"].tensor, 0, [[16, 128], [128 * 16, 16], [1, 16]]),
               reads=[wts["Wa"]], writes=[wba])
        nbfull = asb(f"{tag}_nbf", [16, T_SEQ], F32)
        gts = asb(f"{tag}_gts", [16, T_SEQ], F32)
        onesT = asb(f"{tag}_onT", [16, CH], F32)
        kb.op("vector", lambda e: e.memset(onesT[:], 1.0), writes=[onesT])
        wring = [asb(f"{tag}_wr{i}", [128, 16 * 128], BF16) for i in range(2)]
        praw = asb(f"{tag}_praw", [128, T_SEQ], F32)
        pcv = asb(f"{tag}_pcv", [128, T_SEQ], F32)
        sqacc = asb(f"{tag}_sqa", [16, T_SEQ], F32)
        tmp16 = sqacc
        qsqbf = asb(f"{tag}_qsq", [128, T_SEQ], BF16)
        rstdbf = asb(f"{tag}_rsb", [16, T_SEQ], BF16)
        spl = asb(f"{tag}_spl", [128, T_SEQ], BF16)

        def beta_g(b2):
            for (wtile, outt) in ((wbq, nbfull), (wba, gts)):
                for tc in range(T_SEQ // 512):
                    bank = psum_pool[tc % 2]
                    for c in range(16):
                        col = c * T_LOC + b2 * T_SEQ + tc * 512
                        kb.op("tensor", lambda e, b=bank, w=wtile, c=c, col=col:
                              e.matmul(b[0:16, :], w[:, c * 16:(c + 1) * 16],
                                       Xp[:, col:col + 512],
                                       start=(c == 0), stop=(c == 15)),
                              reads=[wtile, Xp], writes=[bank])
                    kb.op("scalar", lambda e, b=bank, o=outt, tc=tc: e.copy(
                        out=o[0:16, tc * 512:(tc + 1) * 512], in_=b[0:16, :]),
                        reads=[bank], writes=[outt])
            kb.op("scalar", lambda e: e.activation(out=nbfull[0:16, :], in_=nbfull[0:16, :],
                                                   func=AF.Sigmoid, bias=0.0, scale=1.0),
                  reads=[nbfull], writes=[nbfull])
            kb.op("vector", lambda e: e.tensor_scalar(out=nbfull[0:16, :], in0=nbfull[0:16, :],
                                                      scalar1=-2.0, scalar2=None, op0=ALU.mult),
                  reads=[nbfull], writes=[nbfull])
            kb.op("vector", lambda e: e.tensor_scalar(out=gts[0:16, :], in0=gts[0:16, :],
                                                      scalar1=dt_sb[:, 0:1], scalar2=None,
                                                      op0=ALU.add),
                  reads=[gts, dt_sb], writes=[gts])
            kb.op("scalar", lambda e: e.activation(out=tmp16[0:16, :], in_=gts[0:16, :],
                                                   func=AF.Abs, bias=0.0, scale=1.0),
                  reads=[gts], writes=[tmp16])
            kb.op("scalar", lambda e: e.activation(out=tmp16[0:16, :], in_=tmp16[0:16, :],
                                                   func=AF.Exp, bias=0.0, scale=-1.0),
                  reads=[tmp16], writes=[tmp16])
            kb.op("scalar", lambda e: e.activation(out=tmp16[0:16, :], in_=tmp16[0:16, :],
                                                   func=AF.Ln, bias=cv[1.0][0:16, :], scale=1.0),
                  reads=[tmp16, cv["_tile"]], writes=[tmp16])
            kb.op("vector", lambda e: e.tensor_scalar(out=gts[0:16, :], in0=gts[0:16, :],
                                                      scalar1=0.0, scalar2=None, op0=ALU.max),
                  reads=[gts], writes=[gts])
            kb.op("vector", lambda e: e.tensor_tensor(out=gts[0:16, :], in0=gts[0:16, :],
                                                      in1=tmp16[0:16, :], op=ALU.add),
                  reads=[gts, tmp16], writes=[gts])
            kb.op("vector", lambda e: e.tensor_scalar(out=gts[0:16, :], in0=gts[0:16, :],
                                                      scalar1=nae[:, 0:1], scalar2=None,
                                                      op0=ALU.mult),
                  reads=[gts, nae], writes=[gts])
            for c in range(NCH):
                colb = (b2 * NCH + c) * 16
                t0 = c * CH
                kb.op("vector", lambda e, t0=t0: e.tensor_tensor_scan(
                    out=tmp16[0:16, 0:CH], data0=onesT[0:16, :],
                    data1=gts[0:16, t0:t0 + CH], initial=0.0,
                    op0=ALU.mult, op1=ALU.add),
                    reads=[onesT, gts], writes=[tmp16])
                bank = psum_pool[c % 2]
                kb.op("tensor", lambda e, b=bank: e.transpose(
                    b[0:128, 0:16], tmp16[0:16, 0:CH], id_f32[0:16, 0:16]),
                    reads=[tmp16, id_f32], writes=[bank])
                kb.op("vector", lambda e, b=bank, colb=colb: e.tensor_copy(
                    out=Gt_all[:, colb:colb + 16], in_=b[0:128, 0:16]),
                    reads=[bank], writes=[Gt_all])
                bank2 = psum_pool[2 + c % 2]
                kb.op("tensor", lambda e, b=bank2, t0=t0: e.transpose(
                    b[0:128, 0:16], nbfull[0:16, t0:t0 + CH], id_f32[0:16, 0:16]),
                    reads=[nbfull, id_f32], writes=[bank2])
                kb.op("vector", lambda e, b=bank2, colb=colb: e.tensor_copy(
                    out=NBt_all[:, colb:colb + 16], in_=b[0:128, 0:16]),
                    reads=[bank2], writes=[NBt_all])

        def proj_conv_silu(b2, ft, Wsrc, Hf, tap0):
            buf = wring[ft % 2]
            kb.dma("gpsimd", buf[:],
                   APc(Wsrc.tensor, ft * 128, [[Hf, 128], [Hf * 128, 16], [1, 128]]),
                   reads=[Wsrc], writes=[buf])
            for tc in range(4):
                bank = psum_pool[4 + tc % 2]
                for c in range(16):
                    col = c * T_LOC + b2 * T_SEQ + tc * 512
                    kb.op("tensor", lambda e, b=bank, w=buf, c=c, col=col:
                          e.matmul(b[:], w[:, c * 128:(c + 1) * 128],
                                   Xp[:, col:col + 512], start=(c == 0), stop=(c == 15)),
                          reads=[buf, Xp], writes=[bank])
                kb.op("scalar", lambda e, b=bank, tc=tc: e.copy(
                    out=praw[:, tc * 512:(tc + 1) * 512], in_=b[:]),
                    reads=[bank], writes=[praw])
            kb.op("vector", lambda e: e.tensor_scalar(
                out=pcv[:], in0=praw[:], scalar1=cnv[:, tap0 + 3:tap0 + 4],
                scalar2=None, op0=ALU.mult), reads=[praw, cnv], writes=[pcv])
            for tau in (2, 1, 0):
                sh = 3 - tau
                kb.op("vector", lambda e, tau=tau, sh=sh: e.scalar_tensor_tensor(
                    out=pcv[:, sh:T_SEQ], in0=praw[:, 0:T_SEQ - sh],
                    scalar=cnv[:, tap0 + tau:tap0 + tau + 1],
                    in1=pcv[:, sh:T_SEQ], op0=ALU.mult, op1=ALU.add),
                    reads=[praw, cnv, pcv], writes=[pcv])
            kb.op("scalar", lambda e: e.activation(
                out=praw[:], in_=pcv[:], func=AF.Silu, bias=0.0, scale=1.0),
                reads=[pcv], writes=[praw])

        def qk_pipe(b2, Wsrc, convoff, spdst, scale_bias):
            for ft in range(12):
                proj_conv_silu(b2, ft, Wsrc, 1536, convoff + ft * 4)
                kb.op("vector", lambda e: e.tensor_tensor(
                    out=qsqbf[:], in0=praw[:], in1=praw[:], op=ALU.mult),
                    reads=[praw], writes=[qsqbf])
                kb.op("vector", lambda e: e.tensor_copy(out=spl[:], in_=praw[:]),
                      reads=[praw], writes=[spl])
                kb.dma("sync", spdst[(b2 * 12 + ft) * 128:(b2 * 12 + ft) * 128 + 128, :],
                       spl[:], reads=[spl], writes=[spdst])
                for tc in range(4):
                    bank = psum_pool[6 + tc % 2]
                    kb.op("tensor", lambda e, b=bank, ft=ft, tc=tc: e.matmul(
                        b[0:16, :], mq_sb[:, ft * 16:(ft + 1) * 16],
                        qsqbf[:, tc * 512:(tc + 1) * 512], start=True, stop=True),
                        reads=[mq_sb, qsqbf], writes=[bank])
                    if ft == 0:
                        kb.op("vector", lambda e, b=bank, tc=tc: e.tensor_copy(
                            out=sqacc[0:16, tc * 512:(tc + 1) * 512], in_=b[0:16, :]),
                            reads=[bank], writes=[sqacc])
                    else:
                        kb.op("vector", lambda e, b=bank, tc=tc: e.tensor_tensor(
                            out=sqacc[0:16, tc * 512:(tc + 1) * 512],
                            in0=sqacc[0:16, tc * 512:(tc + 1) * 512],
                            in1=b[0:16, :], op=ALU.add),
                            reads=[bank, sqacc], writes=[sqacc])
            kb.op("vector", lambda e: e.tensor_scalar(
                out=sqacc[0:16, :], in0=sqacc[0:16, :], scalar1=1e-6, scalar2=None,
                op0=ALU.add), reads=[sqacc], writes=[sqacc])
            kb.op("scalar", lambda e: e.activation(
                out=sqacc[0:16, :], in_=sqacc[0:16, :], func=AF.Ln, bias=0.0, scale=1.0),
                reads=[sqacc], writes=[sqacc])
            if scale_bias is None:
                kb.op("scalar", lambda e: e.activation(
                    out=rstdbf[0:16, :], in_=sqacc[0:16, :], func=AF.Exp, bias=0.0,
                    scale=-0.5), reads=[sqacc], writes=[rstdbf])
            else:
                kb.op("scalar", lambda e: e.activation(
                    out=rstdbf[0:16, :], in_=sqacc[0:16, :], func=AF.Exp,
                    bias=scale_bias[0:16, :], scale=-0.5),
                    reads=[sqacc, cv["_tile"]], writes=[rstdbf])
            for ft in range(12):
                kb.dma("sync", spl[:],
                       spdst[(b2 * 12 + ft) * 128:(b2 * 12 + ft) * 128 + 128, :],
                       reads=[spdst], writes=[spl])
                for tc in range(4):
                    bank = psum_pool[4 + tc % 2]
                    kb.op("tensor", lambda e, b=bank, ft=ft, tc=tc: e.matmul(
                        b[:], mqt_sb[0:16, ft * 128:(ft + 1) * 128],
                        rstdbf[0:16, tc * 512:(tc + 1) * 512], start=True, stop=True),
                        reads=[mqt_sb, rstdbf], writes=[bank])
                    kb.op("vector", lambda e, b=bank, tc=tc: e.tensor_tensor(
                        out=spl[:, tc * 512:(tc + 1) * 512],
                        in0=spl[:, tc * 512:(tc + 1) * 512],
                        in1=b[:], op=ALU.mult),
                        reads=[bank, spl], writes=[spl])
                kb.dma("sync", spdst[(b2 * 12 + ft) * 128:(b2 * 12 + ft) * 128 + 128, :],
                       spl[:], reads=[spl], writes=[spdst])

        def v_pipe(b2):
            vtt = qsqbf
            for ft in range(24):
                proj_conv_silu(b2, ft, wts["Wv"], 3072, 96 + ft * 4)
                kb.op("vector", lambda e: e.tensor_copy(out=spl[:], in_=praw[:]),
                      reads=[praw], writes=[spl])
                for tc in range(16):
                    bank = psum_pool[6 + tc % 2]
                    pb_bf = bank[:, 0:64].bitcast(BF16)
                    kb.op("tensor", lambda e, pb=pb_bf, tc=tc: e.transpose(
                        pb, spl[:, tc * 128:(tc + 1) * 128], id_bf[:]),
                        reads=[spl, id_bf], writes=[bank])
                    kb.op("vector", lambda e, pb=pb_bf, tc=tc: e.tensor_copy(
                        out=vtt[:, tc * 128:(tc + 1) * 128], in_=pb),
                        reads=[bank], writes=[vtt])
                out_ap = APc(vsp.tensor, (b2 * T_SEQ) * 3072 + ft * 128,
                             [[3072, 128], [128 * 3072, 16], [1, 128]])
                kb.dma("sync", out_ap, vtt[:], reads=[vtt], writes=[vsp])

        for b2 in range(B_LOC):
            beta_g(b2)
            qk_pipe(b2, wts["Wq"], 0, qsp, cv_key_qbias(cv))
            qk_pipe(b2, wts["Wk"], 48, ksp, None)
            v_pipe(b2)
        kb.op("scalar", lambda e: e.activation(out=EGt_all[:], in_=Gt_all[:],
                                               func=AF.Exp, bias=0.0, scale=1.0),
              reads=[Gt_all], writes=[EGt_all])
        kb.op("vector", lambda e: e.tensor_scalar(out=PBt_all[:], in0=NBt_all[:],
                                                  scalar1=-1.0, scalar2=None, op0=ALU.mult),
              reads=[NBt_all], writes=[PBt_all])
        kb.dma("sync", xsp, Xp[:], reads=[Xp], writes=[xsp])
    kb.barrier()

    with ExitStack() as bctx:
        bsb = lambda *a, **k: bctx.enter_context(nc.sbuf_tensor(*a, **k))
        sel16 = bsb(f"{tag}_sel", [16, 2048], F32)
        kb.dma("sync", sel16[:], csts["sel16_d"], writes=[sel16])
        qsb = [bsb(f"{tag}_qsb{i}", [96, 16 * CH], BF16) for i in range(2)]
        ksb = [bsb(f"{tag}_ksb{i}", [96, 16 * CH], BF16) for i in range(2)]
        vsb = [bsb(f"{tag}_vsb{i}", [128, 16 * DV], BF16) for i in range(2)]
        osb = [bsb(f"{tag}_osb{i}", [128, 16 * DV], BF16) for i in range(2)]
        S32 = [bsb(f"{tag}_S32{i}", [96, 4 * DV], F32) for i in range(4)]
        Sbf = [bsb(f"{tag}_Sbf{i}", [96, 4 * DV], BF16) for i in range(4)]
        Slo = [bsb(f"{tag}_Slo{i}", [96, 4 * DV], BF16) for i in range(4)]
        grow16 = bsb(f"{tag}_g16", [16, 128], F32)

        class Slot:
            def __init__(self, i):
                self.Dneg = bsb(f"{tag}_Dng{i}", [128, 128], F32)
                self.DS = bsb(f"{tag}_DS{i}", [128, 128], F32)
                self.DUi = bsb(f"{tag}_DUi{i}", [128, 128], F32)
                self.P0 = bsb(f"{tag}_P0{i}", [128, 128], BF16)
                self.P0T = bsb(f"{tag}_P0T{i}", [128, 128], BF16)
                self.Pn = [bsb(f"{tag}_Pn{i}{j}", [128, 128], BF16) for j in range(2)]
                self.PnT = [bsb(f"{tag}_PnT{i}{j}", [128, 128], BF16) for j in range(2)]
                self.Lt = bsb(f"{tag}_Lt{i}", [128, 128], BF16)
                self.y32 = bsb(f"{tag}_y32{i}", [128, DV], F32)
                self.ybf = bsb(f"{tag}_ybf{i}", [128, DV], BF16)
                self.o_sb = bsb(f"{tag}_oSB{i}", [128, DV], F32)
                self.o2sb = bsb(f"{tag}_o2{i}", [128, DV], F32)
                self.scl = bsb(f"{tag}_scl{i}", [128, 4], F32)
                self.egl = bsb(f"{tag}_egl{i}", [96, 1], F32)
                self.kr = bsb(f"{tag}_kr{i}", [128, 96], BF16)
                self.dmp = bsb(f"{tag}_dmp{i}", [128, DV], BF16)
        slots = [Slot(i) for i in range(4)]

        def step_gen(b2, c, h, kbuf, qb, vb, ob):
            slot = h % SCAN_SLOTS
            B = slots[slot]
            col = (b2 * NCH + c) * 16 + h
            Gcol = Gt_all[:, col:col + 1]
            EGcol = EGt_all[:, col:col + 1]
            NBcol = NBt_all[:, col:col + 1]
            PBcol = PBt_all[:, col:col + 1]
            Kt = kbuf[0:96, h * CH:(h + 1) * CH]
            Qt = qb[0:96, h * CH:(h + 1) * CH]
            Vt = vb[:, h * DV:(h + 1) * DV]
            sslot, hs = h % 4, h // 4
            S32s = S32[sslot][0:96, hs * DV:(hs + 1) * DV]
            Sbfs = Sbf[sslot][0:96, hs * DV:(hs + 1) * DV]
            Slos = Slo[sslot][0:96, hs * DV:(hs + 1) * DV]
            if SCAN_PACK:
                ba, bb = psum_pool[slot * 2], psum_pool[slot * 2 + 1]
                ap_kq = ba[:, 128:256]
                ap_raw = bb[:, 128:128 + DV]
                ap_sq2 = bb[:, 128:256]
                ap_o2 = bb[:, 0:DV]
                t_kq, t_raw, t_sq2, t_o2 = ba, bb, bb, bb
            else:
                ba, bb = psum_pool[slot * 4], psum_pool[slot * 4 + 1]
                bc, bd = psum_pool[slot * 4 + 2], psum_pool[slot * 4 + 3]
                ap_kq = bc[:, 0:128]
                ap_raw = bd[:, 0:DV]
                ap_sq2 = bc[:, 0:128]
                ap_o2 = bc[:, 0:DV]
                t_kq, t_raw, t_sq2, t_o2 = bc, bd, bc, bc
            kb.op("tensor", lambda e: e.matmul(
                ba[:, 0:128], sel16[0:16, h * 128:(h + 1) * 128], grow16[0:16, :],
                start=True, stop=True), reads=[sel16, grow16], writes=[ba])
            kb.op("tensor", lambda e: e.matmul(
                bb[:, 0:128], Kt, Kt, start=True, stop=True),
                reads=[kbuf], writes=[bb])
            kb.op("tensor", lambda e: e.matmul(
                ap_kq, Kt, Qt, start=True, stop=True),
                reads=[kbuf, qb], writes=[t_kq])
            kb.op("tensor", lambda e: e.matmul(
                ap_raw, Kt, Sbfs, start=True, stop=False),
                reads=[kbuf, Sbf[sslot]], writes=[t_raw])
            kb.op("tensor", lambda e: e.matmul(
                ap_raw, Kt, Slos, start=False, stop=True),
                reads=[kbuf, Slo[sslot]], writes=[t_raw])
            kb.op("vector", lambda e: e.tensor_scalar(
                out=B.Dneg[:], in0=ba[:, 0:128], scalar1=Gcol, scalar2=None,
                op0=ALU.subtract), reads=[ba, Gt_all], writes=[B.Dneg])
            kb.op("vector", lambda e: e.scalar_tensor_tensor(
                out=B.DS[:], in0=B.Dneg[:], scalar=-1.0, in1=mask_lows[:],
                op0=ALU.mult, op1=ALU.add),
                reads=[B.Dneg, mask_lows], writes=[B.DS])
            kb.op("scalar", lambda e: e.activation(
                out=B.DS[:], in_=B.DS[:], func=AF.Exp, bias=0.0, scale=1.0),
                reads=[B.DS], writes=[B.DS])
            kb.op("vector", lambda e: e.tensor_tensor(
                out=B.DUi[:], in0=B.Dneg[:], in1=mask_upi[:], op=ALU.add),
                reads=[B.Dneg, mask_upi], writes=[B.DUi])
            kb.op("scalar", lambda e: e.activation(
                out=B.DUi[:], in_=B.DUi[:], func=AF.Exp, bias=0.0, scale=1.0),
                reads=[B.DUi], writes=[B.DUi])
            kb.op("vector", lambda e: e.scalar_tensor_tensor(
                out=B.P0[:], in0=B.DS[:], scalar=NBcol, in1=bb[:, 0:128],
                op0=ALU.mult, op1=ALU.mult),
                reads=[B.DS, NBt_all, bb], writes=[B.P0])
            pa_bf = ba[:, 0:64].bitcast(BF16)
            kb.op("tensor", lambda e: e.transpose(
                pa_bf, B.P0[:], id_bf[:]), reads=[B.P0, id_bf], writes=[ba])
            kb.op("scalar", lambda e: e.copy(out=B.P0T[:], in_=pa_bf),
                  reads=[ba], writes=[B.P0T])
            kb.op("vector", lambda e: e.tensor_tensor(
                out=B.Lt[:], in0=B.DUi[:], in1=ap_kq, op=ALU.mult),
                reads=[B.DUi, t_kq], writes=[B.Lt])
            kb.op("vector", lambda e: e.tensor_tensor(
                out=B.scl[:, 0:1], in0=NBcol, in1=EGcol, op=ALU.mult),
                reads=[NBt_all, EGt_all], writes=[B.scl])
            kb.op("vector", lambda e: e.tensor_scalar(
                out=B.y32[:], in0=Vt, scalar1=PBcol, scalar2=None, op0=ALU.mult),
                reads=[vb, PBt_all], writes=[B.y32])
            kb.op("vector", lambda e: e.scalar_tensor_tensor(
                out=B.y32[:], in0=ap_raw, scalar=B.scl[:, 0:1], in1=B.y32[:],
                op0=ALU.mult, op1=ALU.add),
                reads=[t_raw, B.scl, B.y32], writes=[B.y32])
            kb.op("scalar", lambda e: e.copy(out=B.ybf[:], in_=B.y32[:]),
                  reads=[B.y32], writes=[B.ybf])
            kb.op("tensor", lambda e: e.matmul(
                ba[0:96, 0:1], sel16[0:16, h * 128:h * 128 + 96], grow16[0:16, 127:128],
                start=True, stop=True), reads=[sel16, grow16], writes=[ba])
            kb.op("scalar", lambda e: e.activation(
                out=B.egl[0:96, :], in_=ba[0:96, 0:1], func=AF.Exp, bias=0.0,
                scale=1.0), reads=[ba], writes=[B.egl])
            if SCAN_PART == 1:
                return
            yield
            for it in range(7):
                Pc = B.P0 if it == 0 else B.Pn[(it - 1) % 2]
                PcT = B.P0T if it == 0 else B.PnT[(it - 1) % 2]
                kb.op("tensor", lambda e, PT=PcT: e.matmul(
                    ap_raw, PT[:], B.ybf[:], start=True, stop=True),
                    reads=[PcT, B.ybf], writes=[t_raw])
                if it < 6:
                    kb.op("tensor", lambda e, P=Pc, PT=PcT: e.matmul(
                        bb[:, 0:128], PT[:], P[:], start=True, stop=True),
                        reads=[Pc, PcT], writes=[bb])
                    kb.op("tensor", lambda e, P=Pc, PT=PcT: e.matmul(
                        ap_sq2, P[:], PT[:], start=True, stop=True),
                        reads=[Pc, PcT], writes=[t_sq2])
                kb.op("vector", lambda e: e.tensor_tensor(
                    out=B.y32[:], in0=B.y32[:], in1=ap_raw, op=ALU.add),
                    reads=[B.y32, t_raw], writes=[B.y32])
                kb.op("scalar", lambda e: e.copy(out=B.ybf[:], in_=B.y32[:]),
                      reads=[B.y32], writes=[B.ybf])
                if it < 6:
                    t_pn = kb.op("vector", lambda e, P=B.Pn[it % 2]: e.tensor_copy(
                        out=P[:], in_=bb[:, 0:128]), reads=[bb], writes=[B.Pn[it % 2]])
                    kb.op("scalar", lambda e, P=B.PnT[it % 2]: e.copy(
                        out=P[:], in_=ap_sq2), reads=[t_sq2],
                        writes=[B.PnT[it % 2]], deps=[t_pn])
                yield
            if SCAN_PART == 2:
                return
            kb.op("tensor", lambda e: e.matmul(
                ba[:, 0:DV], Qt, Sbfs, start=True, stop=False),
                reads=[qb, Sbf[sslot]], writes=[ba])
            kb.op("tensor", lambda e: e.matmul(
                ba[:, 0:DV], Qt, Slos, start=False, stop=True),
                reads=[qb, Slo[sslot]], writes=[ba])
            kb.op("tensor", lambda e: e.matmul(
                ap_o2, B.Lt[:], B.ybf[:], start=True, stop=True),
                reads=[B.Lt, B.ybf], writes=[t_o2])
            kb.op("scalar", lambda e: e.copy(out=B.o2sb[:], in_=ap_o2),
                  reads=[t_o2], writes=[B.o2sb])
            kb.op("vector", lambda e: e.scalar_tensor_tensor(
                out=B.o_sb[:], in0=ba[:, 0:DV], scalar=EGcol, in1=B.o2sb[:],
                op0=ALU.mult, op1=ALU.add),
                reads=[ba, EGt_all, B.o2sb], writes=[B.o_sb])
            kb.op("vector", lambda e: e.scalar_tensor_tensor(
                out=B.dmp[:], in0=B.o_sb[:], scalar=0.0, in1=B.o_sb[:],
                op0=ALU.add, op1=ALU.mult, accum_out=B.scl[:, 1:2]),
                reads=[B.o_sb], writes=[B.dmp, B.scl])
            kb.op("vector", lambda e: e.tensor_scalar(
                out=B.scl[:, 1:2], in0=B.scl[:, 1:2], scalar1=1.0 / DV,
                scalar2=1e-6, op0=ALU.mult, op1=ALU.add),
                reads=[B.scl], writes=[B.scl])
            kb.op("scalar", lambda e: e.activation(
                out=B.scl[:, 2:3], in_=B.scl[:, 1:2], func=AF.Ln, bias=0.0,
                scale=1.0), reads=[B.scl], writes=[B.scl])
            kb.op("scalar", lambda e: e.activation(
                out=B.scl[:, 2:3], in_=B.scl[:, 2:3], func=AF.Exp, bias=0.0,
                scale=-0.5), reads=[B.scl], writes=[B.scl])
            kb.op("vector", lambda e: e.scalar_tensor_tensor(
                out=ob[:, h * DV:(h + 1) * DV], in0=B.o_sb[:], scalar=B.scl[:, 2:3],
                in1=nw_bc[:], op0=ALU.mult, op1=ALU.mult),
                reads=[B.o_sb, B.scl, nw_bc], writes=[ob])
            yield
            if SCAN_PART == 3:
                return
            pd_bf = ba[:, 0:64].bitcast(BF16)[:, 0:96]
            kb.op("tensor", lambda e: e.transpose(
                pd_bf, Kt, id_bf[0:96, 0:96]), reads=[kbuf, id_bf], writes=[ba])
            kb.op("vector", lambda e: e.tensor_scalar(
                out=B.kr[:], in0=pd_bf, scalar1=B.DUi[:, 127:128], scalar2=None,
                op0=ALU.mult), reads=[ba, B.DUi], writes=[B.kr])
            kb.op("tensor", lambda e: e.matmul(
                bb[0:96, 0:DV], B.kr[:], B.ybf[:], start=True, stop=True),
                reads=[B.kr, B.ybf], writes=[bb])
            kb.op("vector", lambda e: e.scalar_tensor_tensor(
                out=S32s, in0=S32s, scalar=B.egl[:, 0:1], in1=bb[0:96, 0:DV],
                op0=ALU.mult, op1=ALU.add),
                reads=[S32[sslot], B.egl, bb], writes=[S32[sslot]])
            kb.op("scalar", lambda e: e.copy(out=Sbfs, in_=S32s),
                  reads=[S32[sslot]], writes=[Sbf[sslot]])
            kb.op("vector", lambda e: e.tensor_tensor(
                out=Slos, in0=S32s, in1=Sbfs, op=ALU.subtract),
                reads=[S32[sslot], Sbf[sslot]], writes=[Slo[sslot]])

        for b2 in range(B_LOC):
            for sl in range(4):
                kb.op("vector", lambda e, sl=sl: e.memset(S32[sl][0:96, :], 0.0),
                      writes=[S32[sl]])
                kb.op("vector", lambda e, sl=sl: e.memset(Sbf[sl][0:96, :], 0.0),
                      writes=[Sbf[sl]])
                kb.op("vector", lambda e, sl=sl: e.memset(Slo[sl][0:96, :], 0.0),
                      writes=[Slo[sl]])
            for c in range(NCH):
                cur = c % 2
                qb, kbuf, vb, ob = qsb[cur], ksb[cur], vsb[cur], osb[cur]
                kb.dma("sync", qb[0:96, :],
                       APc(qsp.tensor, (b2 * 12 * 128) * T_SEQ + c * CH,
                           [[T_SEQ, 96], [96 * T_SEQ, 16], [1, CH]]),
                       reads=[qsp], writes=[qb])
                kb.dma("sync", kbuf[0:96, :],
                       APc(ksp.tensor, (b2 * 12 * 128) * T_SEQ + c * CH,
                           [[T_SEQ, 96], [96 * T_SEQ, 16], [1, CH]]),
                       reads=[ksp], writes=[kbuf])
                kb.dma("sync", vb[:],
                       vsp[b2 * T_SEQ + c * CH:b2 * T_SEQ + c * CH + CH, :],
                       reads=[vsp], writes=[vb])
                colb = (b2 * NCH + c) * 16
                bg = psum_pool[0]
                kb.op("tensor", lambda e, colb=colb: e.transpose(
                    bg[0:16, 0:128], Gt_all[:, colb:colb + 16], id_f32[:]),
                    reads=[Gt_all, id_f32], writes=[bg])
                kb.op("vector", lambda e: e.tensor_copy(
                    out=grow16[:], in_=bg[0:16, 0:128]),
                    reads=[bg], writes=[grow16])
                NS = SCAN_SLOTS
                for hq in range(H_HEADS // NS):
                    gens = [step_gen(b2, c, NS * hq + i, kbuf, qb, vb, ob)
                            for i in range(NS)]
                    if SCAN_RR:
                        alive = [True] * NS
                        while any(alive):
                            for gi in range(NS):
                                if alive[gi]:
                                    try:
                                        next(gens[gi])
                                    except StopIteration:
                                        alive[gi] = False
                    else:
                        for g in gens:
                            for _ in g:
                                pass
                kb.dma("sync", onsp[b2 * T_SEQ + c * CH:b2 * T_SEQ + c * CH + CH, :],
                       ob[:], reads=[ob], writes=[onsp])
    kb.barrier()

    with ExitStack() as cctx:
        csb = lambda *a, **k: cctx.enter_context(nc.sbuf_tensor(*a, **k))
        wg_h = csb(f"{tag}_wg", [128, 16 * 1536], BF16)
        wo_h = csb(f"{tag}_wo", [128, 12 * D], BF16)
        xch = [csb(f"{tag}_xc{i}", [128, 16 * 128], BF16) for i in range(2)]
        onh = [csb(f"{tag}_oh{i}", [128, 1536], BF16) for i in range(2)]
        gsl = csb(f"{tag}_gsl", [128, 1536], F32)
        ogbf = csb(f"{tag}_ogb", [128, 1536], BF16)
        ogT = csb(f"{tag}_ogT", [128, 12 * 128], BF16)
        odr = csb(f"{tag}_odr", [128, 512], F32)
        resc = csb(f"{tag}_rsc", [128, 512], F32)
        for hf in range(2):
            kb.dma("gpsimd", wg_h[:],
                   APc(wts["Wg"].tensor, hf * 1536,
                       [[3072, 128], [128 * 3072, 16], [1, 1536]]),
                   reads=[wts["Wg"]], writes=[wg_h])
            kb.dma("gpsimd", wo_h[:], pack_rows(wts["Wo"], hf * 1536, 1536, D),
                   reads=[wts["Wo"]], writes=[wo_h])
            for b2 in range(B_LOC):
                for c in range(NCH):
                    cur = c % 2
                    xc, oh = xch[cur], onh[cur]
                    row0 = b2 * T_SEQ + c * CH
                    kb.dma("sync", xc[:],
                           APc(xsp.tensor, b2 * T_SEQ + c * CH,
                               [[16 * T_LOC, 128], [T_LOC, 16], [1, 128]]),
                           reads=[xsp], writes=[xc])
                    kb.dma("sync", oh[:],
                           onsp[row0:row0 + CH, hf * 1536:(hf + 1) * 1536],
                           reads=[onsp], writes=[oh])
                    for fs in range(3):
                        bank = psum_pool[fs]
                        for dc in range(16):
                            kb.op("tensor", lambda e, b=bank, dc=dc, fs=fs, xc=xc:
                                  e.matmul(b[:], xc[:, dc * 128:(dc + 1) * 128],
                                           wg_h[:, dc * 1536 + fs * 512:dc * 1536 + fs * 512 + 512],
                                           start=(dc == 0), stop=(dc == 15)),
                                  reads=[xc, wg_h], writes=[bank])
                        kb.op("scalar", lambda e, b=bank, fs=fs: e.activation(
                            out=gsl[:, fs * 512:(fs + 1) * 512], in_=b[:],
                            func=AF.Silu, bias=0.0, scale=1.0),
                            reads=[bank], writes=[gsl])
                    kb.op("vector", lambda e, oh=oh: e.tensor_tensor(
                        out=ogbf[:], in0=oh[:], in1=gsl[:], op=ALU.mult),
                        reads=[oh, gsl], writes=[ogbf])
                    for fc in range(12):
                        bank = psum_pool[3 + fc % 2]
                        pb_bf = bank[:, 0:64].bitcast(BF16)
                        kb.op("tensor", lambda e, pb=pb_bf, fc=fc: e.transpose(
                            pb, ogbf[:, fc * 128:(fc + 1) * 128], id_bf[:]),
                            reads=[ogbf, id_bf], writes=[bank])
                        kb.op("vector", lambda e, pb=pb_bf, fc=fc: e.tensor_copy(
                            out=ogT[:, fc * 128:(fc + 1) * 128], in_=pb),
                            reads=[bank], writes=[ogT])
                    for ds in range(4):
                        bank = psum_pool[5 + (ds % 3)]
                        for fc in range(12):
                            kb.op("tensor", lambda e, b=bank, fc=fc, ds=ds: e.matmul(
                                b[:], ogT[:, fc * 128:(fc + 1) * 128],
                                wo_h[:, fc * D + ds * 512:fc * D + ds * 512 + 512],
                                start=(fc == 0), stop=(fc == 11)),
                                reads=[ogT, wo_h], writes=[bank])
                        kb.op("scalar", lambda e, b=bank: e.copy(out=odr[:], in_=b[:]),
                              reads=[bank], writes=[odr])
                        src = xin if hf == 0 else xout
                        kb.dma("scalar", resc[:],
                               src[row0:row0 + CH, ds * 512:(ds + 1) * 512],
                               writes=[resc])
                        kb.op("vector", lambda e: e.tensor_tensor(
                            out=resc[:], in0=resc[:], in1=odr[:], op=ALU.add),
                            reads=[resc, odr], writes=[resc])
                        kb.dma("sync", xout[row0:row0 + CH, ds * 512:(ds + 1) * 512],
                               resc[:], reads=[resc])
    kb.barrier()


def cv_key_qbias(cv):
    return cv[QBIAS_VAL]


QBIAS_VAL = float(-0.5 * np.log(96.0))


def build_l0attn_stage():
    nc = bass.Bass("TRN2", target_bir_lowering=False)
    xin = nc.dram_tensor("xin", [T_LOC, D], F32, kind="ExternalInput").ap()
    xout = nc.dram_tensor("xout", [T_LOC, D], F32, kind="ExternalOutput").ap()
    wts = {}
    for name, shp in [("Wq", [2048, 1536]), ("Wk", [2048, 1536]), ("Wv", [2048, 3072]),
                      ("conv_q", [1536, 4]), ("conv_k", [1536, 4]), ("conv_v", [3072, 4]),
                      ("Wb", [2048, 16]), ("Wa", [2048, 16]), ("A_log", [16]),
                      ("dt_bias", [16]), ("Wg", [2048, 3072]),
                      ("norm_weight", [192]), ("Wo", [3072, 2048])]:
        wts[name] = nc.dram_tensor(name, shp, F32, kind="ExternalInput").ap()
    scr = dict(
        qsp=nc.dram_tensor("qsp", [2 * 12 * 128, T_SEQ], BF16, kind="Internal").ap(),
        ksp=nc.dram_tensor("ksp", [2 * 12 * 128, T_SEQ], BF16, kind="Internal").ap(),
        vsp=nc.dram_tensor("vsp", [2 * T_SEQ, 3072], BF16, kind="Internal").ap(),
        onsp=nc.dram_tensor("onsp", [2 * T_SEQ, 3072], BF16, kind="Internal").ap(),
        xsp=nc.dram_tensor("xsp", [128, 16 * T_LOC], BF16, kind="Internal").ap(),
    )
    with ExitStack() as ctx:
        kb = KB(nc, ctx)
        psum_pool = [ctx.enter_context(nc.psum_tensor(f"pp{i}", [128, 512], F32))
                     for i in range(8)]
        csts = declare_and_load_consts(kb, ctx, ["id_bf", "id_f32", "ones_f32",
                                                 "mask_lows", "mask_upi"])
        csts["mq_d"] = nc.dram_tensor("c_mq", [1536, 16], BF16, kind="ExternalInput").ap()
        csts["mqt_d"] = nc.dram_tensor("c_mqt", [16, 1536], BF16, kind="ExternalInput").ap()
        csts["sel16_d"] = nc.dram_tensor("c_sel16", [16, 2048], F32, kind="ExternalInput").ap()
        cvals = make_cvals(kb, ctx, [1.0, QBIAS_VAL])
        emit_l0attn(kb, ctx, xin, xout, wts, scr, csts, psum_pool, cvals)
        block = ctx.enter_context(nc.Block())
        kb.finalize(block)
    return nc



_CONSTS = None


def _consts():
    global _CONSTS
    if _CONSTS is None:
        _CONSTS = make_consts()
    return _CONSTS





def build_fused():
    nc = bass.Bass("TRN2", target_bir_lowering=False)
    X = nc.dram_tensor("X", [T_LOC, D], F32, kind="ExternalInput").ap()
    Y = nc.dram_tensor("Y", [T_LOC, D], F32, kind="ExternalOutput").ap()
    win = {}
    for name, shp in [("W1", [8192, 2048]), ("W3", [8192, 2048]), ("W2", [2048, 8192]),
                      ("WFC", [5120, 2048]), ("WPJ", [2048, 5120]),
                      ("Wq", [2048, 1536]), ("Wk", [2048, 1536]), ("Wv", [2048, 3072]),
                      ("conv_q", [1536, 4]), ("conv_k", [1536, 4]), ("conv_v", [3072, 4]),
                      ("Wb", [2048, 16]), ("Wa", [2048, 16]), ("A_log", [16]),
                      ("dt_bias", [16]), ("Wg", [2048, 3072]),
                      ("norm_weight", [192]), ("Wo", [3072, 2048]),
                      ("W1a", [2048, 6144]), ("b1a", [6144])]:
        win[name] = nc.dram_tensor(name, shp, F32, kind="ExternalInput").ap()
    dr = lambda n, shp, dt: nc.dram_tensor(n, shp, dt, kind="Internal").ap()
    xa = dr("xa", [T_LOC, D], F32)
    xb = dr("xb", [T_LOC, D], F32)
    w1tt = dr("w1tt", [64 * 2048, 128], BF16)
    w3tt = dr("w3tt", [64 * 2048, 128], BF16)
    w2pp = dr("w2pp", [4 * 64 * 128, 512], BF16)
    wfctt = dr("wfctt", [40 * 2048, 128], BF16)
    wpjtt = dr("wpjtt", [4 * 40 * 128, 512], BF16)
    w1bf = dr("w1bf", [8192, 2048], BF16)
    w3bf = dr("w3bf", [8192, 2048], BF16)
    w2bf = dr("w2bf", [2048, 8192], BF16)
    w2bfT = dr("w2bfT", [8192, 2048], BF16)
    wfcbf = dr("wfcbf", [5120, 2048], BF16)
    wpjbf = dr("wpjbf", [2048, 5120], BF16)
    scr = dict(
        qsp=dr("qsp", [2 * 12 * 128, T_SEQ], BF16),
        ksp=dr("ksp", [2 * 12 * 128, T_SEQ], BF16),
        vsp=dr("vsp", [2 * T_SEQ, 3072], BF16),
        onsp=dr("onsp", [2 * T_SEQ, 3072], BF16),
        xsp=dr("xsp", [128, 16 * T_LOC], BF16),
    )
    qsp1 = dr("qsp1", [2048, T_LOC], BF16)
    qsp2 = dr("qsp2", [2048, T_LOC], BF16)

    with ExitStack() as octx:
        psum_pool = [octx.enter_context(nc.psum_tensor(f"pp{i}", [128, 512], F32))
                     for i in range(8)]
        kbs = []
        dpool = DmaPool(nc, octx)

        def new_kb():
            if kbs:
                kbs[-1].seal()
            kb = KB(nc, octx, dpool)
            kbs.append(kb)
            return kb

        kb = new_kb()
        cst_names = ["id_bf", "id_f32", "ones_f32", "h64_bf", "h128_bf",
                     "mask_lows", "mask_upi"]
        csts = declare_and_load_consts(kb, octx, cst_names)
        csts["mq_d"] = nc.dram_tensor("c_mq", [1536, 16], BF16, kind="ExternalInput").ap()
        csts["mqt_d"] = nc.dram_tensor("c_mqt", [16, 1536], BF16, kind="ExternalInput").ap()
        csts["sel16_d"] = nc.dram_tensor("c_sel16", [16, 2048], F32, kind="ExternalInput").ap()
        cvals = make_cvals(kb, octx, [1.0, QBIAS_VAL])
        stile = octx.enter_context(nc.sbuf_tensor("sctile", [128, 1], F32))
        am_fc = octx.enter_context(nc.sbuf_tensor("am_fc", [1, 1], F32))
        am_pj = octx.enter_context(nc.sbuf_tensor("am_pj", [1, 1], F32))

        with ExitStack() as sctx:
            emit_cast_pass(kb, sctx, win["W1"], w1bf, 8192, 2048, "w1c")
            emit_transpose_tile_lhsT(kb, sctx, w1bf, w1tt, 8192, 2048, "w1t")
            emit_cast_pass(kb, sctx, win["W3"], w3bf, 8192, 2048, "w3c")
            emit_transpose_tile_lhsT(kb, sctx, w3bf, w3tt, 8192, 2048, "w3t")
            emit_cast_pass(kb, sctx, win["W2"], w2bf, 2048, 8192, "w2c",
                           rg_rows=256, bufs=1)
            emit_transpose_tile_rhsT_natural(kb, sctx, w2bf, w2bfT, "w2t")
            emit_fold_w2(kb, sctx, w2bfT, w2pp, csts, psum_pool)
        kb.barrier()
        with ExitStack() as sctx:
            emit_fold_h128(kb, sctx, w2pp, csts, psum_pool)
        kb.barrier()
        with ExitStack() as sctx:
            thr_fc = emit_absmean_thr(kb, sctx, win["WFC"], 5120, 2048, "amf",
                                      csts, psum_pool, rg_rows=256)
            emit_tern_cast_pass(kb, sctx, win["WFC"], wfcbf, 5120, 2048, "tfc",
                                thr_fc, rg_rows=256)
            emit_transpose_tile_lhsT(kb, sctx, wfcbf, wfctt, 5120, 2048, "wft")
            kb.op("vector", lambda e, t=thr_fc: e.tensor_copy(out=am_fc[:], in_=t[0:1, 2:3]),
                  reads=[thr_fc], writes=[am_fc])
        kb.barrier()
        with ExitStack() as sctx:
            thr_pj = emit_absmean_thr(kb, sctx, win["WPJ"], 2048, 5120, "amp",
                                      csts, psum_pool, rg_rows=128)
            emit_tern_cast_pass(kb, sctx, win["WPJ"], wpjbf, 2048, 5120, "tpj",
                                thr_pj, rg_rows=128)
            emit_transpose_tile_rhsT(kb, sctx, wpjbf, wpjtt, 2048, 5120, "wpt")
            kb.op("vector", lambda e, t=thr_pj: e.tensor_copy(out=am_pj[:], in_=t[0:1, 2:3]),
                  reads=[thr_pj], writes=[am_pj])
        kb.barrier()
        thr_fc, thr_pj = am_fc, am_pj
        kb.op("vector", lambda e: e.tensor_tensor(
            out=stile[0:1, :], in0=thr_fc[0:1, 0:1], in1=thr_fc[0:1, 0:1], op=ALU.mult),
            reads=[thr_fc], writes=[stile])
        kb.op("vector", lambda e: e.tensor_tensor(
            out=stile[0:1, :], in0=stile[0:1, :], in1=thr_pj[0:1, 0:1], op=ALU.mult),
            reads=[stile, thr_pj], writes=[stile])
        kb.op("vector", lambda e: e.tensor_scalar(
            out=stile[0:1, :], in0=stile[0:1, :], scalar1=0.5, scalar2=None,
            op0=ALU.mult), reads=[stile], writes=[stile])
        sc_d = dr("sc_d", [1, 1], F32)
        kb.dma("sync", sc_d, stile[0:1, :], reads=[stile])
        kb.dma("sync", stile[:], APc(sc_d.tensor, 0, [[0, 128], [1, 1]]), writes=[stile])

        kb = new_kb()
        with ExitStack() as sctx:
            emit_ffn(kb, sctx, X, xa, w1tt, w3tt, w2pp, 8192, csts, psum_pool,
                     tag="f0a", relu2=False)
        kb = new_kb()
        with ExitStack() as sctx:
            emit_l0attn(kb, sctx, xa, xb, win, scr, csts, psum_pool, cvals, tag="at0")
        kb = new_kb()
        with ExitStack() as sctx:
            emit_ffn(kb, sctx, xb, xa, w1tt, w3tt, w2pp, 8192, csts, psum_pool,
                     tag="f0b", relu2=False)
        kb = new_kb()
        with ExitStack() as sctx:
            emit_ffn(kb, sctx, xa, xb, wfctt, None, wpjtt, 5120, csts, psum_pool,
                     tag="f1a", relu2=True, scale_tile=stile)
        kb = new_kb()
        with ExitStack() as sctx:
            emit_l1attn(kb, sctx, xb, xa, win["W1a"], win["b1a"], qsp1, qsp2,
                        csts, psum_pool, tag="at1")
        kb = new_kb()
        with ExitStack() as sctx:
            emit_ffn(kb, sctx, xa, Y, wfctt, None, wpjtt, 5120, csts, psum_pool,
                     tag="f1b", relu2=True, scale_tile=stile)

        block = octx.enter_context(nc.Block())
        for kb in kbs:
            kb.finalize(block)
    return nc


def kernel(**inputs):
    cs = _consts()
    cores = list(range(N_CORES))
    x = np.ascontiguousarray(
        np.asarray(inputs["L0_attn_hidden_states"]).reshape(N_CORES, T_LOC, D))
    nc = build_fused()
    base = dict(
        W1=inputs["L0_ffn_W1"], W3=inputs["L0_ffn_W3"], W2=inputs["L0_ffn_W2"],
        WFC=inputs["L1_ffn_w_fc"], WPJ=inputs["L1_ffn_w_proj"],
        Wq=inputs["L0_attn_Wq"], Wk=inputs["L0_attn_Wk"], Wv=inputs["L0_attn_Wv"],
        conv_q=inputs["L0_attn_conv_q"], conv_k=inputs["L0_attn_conv_k"],
        conv_v=inputs["L0_attn_conv_v"], Wb=inputs["L0_attn_Wb"],
        Wa=inputs["L0_attn_Wa"], A_log=inputs["L0_attn_A_log"],
        dt_bias=inputs["L0_attn_dt_bias"], Wg=inputs["L0_attn_Wg"],
        norm_weight=inputs["L0_attn_norm_weight"], Wo=inputs["L0_attn_Wo"],
        W1a=inputs["L1_attn_W"], b1a=inputs["L1_attn_b"],
        c_id_bf=cs["id_bf"], c_id_f32=cs["id_f32"], c_ones_f32=cs["ones_f32"],
        c_h64_bf=cs["h64_bf"], c_h128_bf=cs["h128_bf"],
        c_mask_lows=cs["mask_lows"], c_mask_upi=cs["mask_upi"],
        c_mq=cs["mq"], c_mqt=cs["mqt"], c_sel16=cs["sel16"])
    in_maps = [dict(base, X=x[c]) for c in cores]
    res = run_bass_kernel_spmd(nc, in_maps, core_ids=cores)
    y = np.stack([np.asarray(res.results[c]["Y"]) for c in cores])
    return y.reshape(16, T_SEQ, D).astype(np.float32)


def kernel_m1(**inputs):
    cs = _consts()
    cores = list(range(N_CORES))
    x = np.ascontiguousarray(
        np.asarray(inputs["L0_attn_hidden_states"]).reshape(N_CORES, T_LOC, D))

    nc = build_prep(None)
    prep_in = dict(W1=inputs["L0_ffn_W1"], W3=inputs["L0_ffn_W3"],
                   W2=inputs["L0_ffn_W2"], WFC=inputs["L1_ffn_w_fc"],
                   WPJ=inputs["L1_ffn_w_proj"],
                   c_id_f32=cs["id_f32"], c_h64_bf=cs["h64_bf"])
    res = run_bass_kernel_spmd(nc, [prep_in] * N_CORES, core_ids=cores)
    pr = res.results[0]
    w1tt, w3tt, w2pp = pr["w1tt"], pr["w3tt"], pr["w2pp"]
    wfctt, wpjtt, scale = pr["wfctt"], pr["wpjtt"], pr["scale_o"]

    def shard_run(nc, mk_in):
        res = run_bass_kernel_spmd(nc, [mk_in(c) for c in cores], core_ids=cores)
        return np.stack([np.asarray(res.results[c]["xout"]) for c in cores])

    nc_f0 = build_ffn_stage(8192, relu2=False)
    x = shard_run(nc_f0, lambda c: dict(
        xin=x[c], w1tt=w1tt, w3tt=w3tt, w2pp=w2pp,
        c_id_bf=cs["id_bf"], c_h128_bf=cs["h128_bf"]))
    nc_a0 = build_l0attn_stage()
    wnames = dict(Wq="L0_attn_Wq", Wk="L0_attn_Wk", Wv="L0_attn_Wv",
                  conv_q="L0_attn_conv_q", conv_k="L0_attn_conv_k",
                  conv_v="L0_attn_conv_v", Wb="L0_attn_Wb", Wa="L0_attn_Wa",
                  A_log="L0_attn_A_log", dt_bias="L0_attn_dt_bias",
                  Wg="L0_attn_Wg", norm_weight="L0_attn_norm_weight",
                  Wo="L0_attn_Wo")
    attn_w = {k2: np.asarray(inputs[v]) for k2, v in wnames.items()}
    x = shard_run(nc_a0, lambda c: dict(
        xin=x[c], c_id_bf=cs["id_bf"], c_id_f32=cs["id_f32"],
        c_ones_f32=cs["ones_f32"], c_mask_lows=cs["mask_lows"],
        c_mask_upi=cs["mask_upi"], c_mq=cs["mq"], c_mqt=cs["mqt"], **attn_w))
    x = shard_run(nc_f0, lambda c: dict(
        xin=x[c], w1tt=w1tt, w3tt=w3tt, w2pp=w2pp,
        c_id_bf=cs["id_bf"], c_h128_bf=cs["h128_bf"]))
    nc_f1 = build_ffn_stage(5120, relu2=True)
    x = shard_run(nc_f1, lambda c: dict(
        xin=x[c], wfctt=wfctt, wpjtt=wpjtt, scale_i=scale, c_id_bf=cs["id_bf"]))
    nc_a1 = build_l1attn_stage()
    x = shard_run(nc_a1, lambda c: dict(
        xin=x[c], W=inputs["L1_attn_W"], b=inputs["L1_attn_b"],
        c_id_bf=cs["id_bf"], c_id_f32=cs["id_f32"]))
    x = shard_run(nc_f1, lambda c: dict(
        xin=x[c], wfctt=wfctt, wpjtt=wpjtt, scale_i=scale, c_id_bf=cs["id_bf"]))

    return x.reshape(16, T_SEQ, D).astype(np.float32)


# version 08 (final)
# speedup vs baseline: 1.1425x; 1.1425x over previous
import numpy as np
import ml_dtypes
from contextlib import ExitStack

import concourse.bass as bass
import concourse.mybir as mybir
from concourse.ap import AP as APc
from concourse.bass_utils import run_bass_kernel_spmd

F32 = mybir.dt.float32
BF16 = mybir.dt.bfloat16
AF = mybir.ActivationFunctionType
ALU = mybir.AluOpType
AX = mybir.AxisListType

N_CORES = 8
B_LOC = 2
T_SEQ = 2048
T_LOC = B_LOC * T_SEQ
D = 2048

ENGS = ("sync", "gpsimd", "vector", "scalar", "tensor")


def _rname(x):
    if hasattr(x, "tensor"):
        return x.tensor.name
    return x.name


_KB_ID = [0]

DMA_QUEUES = ("sync", "gpsimd", "scalar")
DMA_POOL_DEPTH = 6


class DmaPool:

    def __init__(self, nc, ctx):
        self.sems = {}
        self.cnt = {}
        self.idx = {q: 0 for q in DMA_QUEUES}
        for q in DMA_QUEUES:
            for i in range(DMA_POOL_DEPTH):
                key = f"dq_{q}{i}"
                self.sems[key] = ctx.enter_context(nc.semaphore(key))
                self.cnt[key] = 0

    def slot(self, q):
        i = self.idx[q] % DMA_POOL_DEPTH
        self.idx[q] += 1
        return f"dq_{q}{i}"

    def snapshot(self):
        return [(k, v) for k, v in self.cnt.items() if v > 0]


class KB:

    def __init__(self, nc, ctx, dpool=None):
        self.nc = nc
        _KB_ID[0] += 1
        kid = _KB_ID[0]
        self.sems = {e: ctx.enter_context(nc.semaphore(f"kb{kid}_{e}")) for e in ENGS}
        self.dpool = dpool if dpool is not None else DmaPool(nc, ctx)
        self.cnt = {e: 0 for e in ENGS}
        self.items = {e: [] for e in ENGS}
        self.waited = {e: {} for e in ENGS}
        self.res = {}
        self.bar = []
        self.finals = None

    def _sem(self, key):
        return self.sems[key] if key in self.sems else self.dpool.sems[key]

    def _resource(self, name):
        if name not in self.res:
            self.res[name] = {"w": None, "r": {}}
        return self.res[name]

    def barrier(self):
        self.bar = ([(e, self.cnt[e]) for e in ENGS if self.cnt[e] > 0]
                    + self.dpool.snapshot())

    def op(self, eng, fn, reads=(), writes=(), deps=(), inc=1, _dma_key=None):
        dep_list = list(deps) + self.bar
        rr = [self._resource(_rname(x)) for x in reads]
        ww = [self._resource(_rname(x)) for x in writes]
        for r in rr:
            if r["w"] is not None:
                dep_list.append(r["w"])
        for w in ww:
            if w["w"] is not None:
                dep_list.append(w["w"])
            for e2, v in w["r"].items():
                dep_list.append((e2, v))
        waits = []
        for (e2, v) in dep_list:
            if e2 == eng == "tensor":
                continue
            if v <= self.waited[eng].get(e2, 0):
                continue
            self.waited[eng][e2] = v
            waits.append((e2, v))
        if _dma_key is None:
            self.cnt[eng] += inc
            tok = (eng, self.cnt[eng])
            inc_sem = (eng, inc)
        else:
            self.dpool.cnt[_dma_key] += 16
            tok = (_dma_key, self.dpool.cnt[_dma_key])
            inc_sem = (_dma_key, 16)
        self.items[eng].append((waits, fn, inc_sem))
        for r in rr:
            if r["r"].get(tok[0], 0) < tok[1]:
                r["r"][tok[0]] = tok[1]
        for w in ww:
            w["w"] = tok
            w["r"] = {}
        return tok

    def dma(self, queue, out, in_, reads=(), writes=(), deps=(), **kw):
        key = self.dpool.slot(queue)
        slot_pred = [(key, self.dpool.cnt[key])] if self.dpool.cnt[key] > 0 else []
        return self.op(queue, lambda e: e.dma_start(out=out, in_=in_, **kw),
                       reads=list(reads) + [in_], writes=list(writes) + [out],
                       deps=list(deps) + slot_pred, _dma_key=key)

    def seal(self):
        self.finals = ([(e, self.cnt[e]) for e in ENGS if self.cnt[e] > 0]
                       + self.dpool.snapshot())

    def finalize(self, block):
        if self.finals is None:
            self.seal()
        finals = self.finals
        items = self.items
        ksems = self._sem

        def section(eng_name):
            def _(eng):
                for (waits, fn, inc_sem) in items[eng_name]:
                    for (k, v) in waits:
                        eng.wait_ge(ksems(k), v)
                    ins = fn(eng)
                    ins.then_inc(ksems(inc_sem[0]), inc_sem[1])
                for (k, v) in finals:
                    eng.wait_ge(ksems(k), v)
            return _
        block.sync(section("sync"))
        block.gpsimd(section("gpsimd"))
        block.vector(section("vector"))
        block.scalar(section("scalar"))
        block.tensor(section("tensor"))



def hadamard(n):
    H = np.array([[1.0]])
    while H.shape[0] < n:
        H = np.block([[H, H], [H, -H]])
    return H


CONST_DEFS = {
    "id_bf": ((128, 128), BF16),
    "id_f32": ((128, 128), F32),
    "h128_bf": ((128, 128), BF16),
    "h64_bf": ((128, 64), BF16),
    "ones_bf": ((128, 128), BF16),
    "ut_bf": ((128, 128), BF16),
    "mask_lows": ((128, 128), F32),
    "mask_upi": ((128, 128), F32),
    "mask_ups": ((128, 128), F32),
    "ones_f32": ((128, 128), F32),
    "mq": ((1536, 16), BF16),
    "mqt": ((16, 1536), BF16),
    "sel16": ((16, 2048), F32),
}


def make_consts():
    i = np.arange(128)
    return {
        "id_bf": np.eye(128, dtype=ml_dtypes.bfloat16),
        "id_f32": np.eye(128, dtype=np.float32),
        "h128_bf": (hadamard(128) / np.sqrt(128.0)).astype(ml_dtypes.bfloat16),
        "h64_bf": np.concatenate([hadamard(64) / 8.0] * 2, 0).astype(ml_dtypes.bfloat16),
        "ones_bf": np.ones((128, 128), dtype=ml_dtypes.bfloat16),
        "ut_bf": np.triu(np.ones((128, 128))).astype(ml_dtypes.bfloat16),
        "mask_lows": np.where(i[:, None] > i[None, :], 0.0, -1e30).astype(np.float32),
        "mask_upi": np.where(i[:, None] <= i[None, :], 0.0, -1e30).astype(np.float32),
        "mask_ups": np.where(i[:, None] < i[None, :], 0.0, -1e30).astype(np.float32),
        "ones_f32": np.ones((128, 128), dtype=np.float32),
        "mq": (np.arange(1536)[:, None] // 96 == np.arange(16)[None, :]).astype(ml_dtypes.bfloat16),
        "mqt": (np.arange(1536)[None, :] // 96 == np.arange(16)[:, None]).astype(ml_dtypes.bfloat16),
        "sel16": (np.arange(2048)[None, :] // 128 == np.arange(16)[:, None]).astype(np.float32),
    }


def make_cvals(kb, ctx, values):
    nc = kb.nc
    t = ctx.enter_context(nc.sbuf_tensor("cvals", [128, max(1, len(values))], F32))
    out = {}
    for i, v in enumerate(values):
        kb.op("vector", lambda e, i=i, v=v: e.memset(t[:, i:i + 1], v),
              writes=[t])
        out[v] = t[:, i:i + 1]
    out["_tile"] = t
    return out


def declare_and_load_consts(kb, ctx, names):
    nc = kb.nc
    out = {}
    for n in names:
        shp, dt = CONST_DEFS[n]
        d = nc.dram_tensor("c_" + n, list(shp), dt, kind="ExternalInput").ap()
        t = ctx.enter_context(nc.sbuf_tensor("sb_" + n, list(shp), dt))
        kb.dma("sync", t[:], d)
        out[n] = t
    return out




def pack_rows(ap, row0, n_rows, C):
    return APc(ap.tensor, row0 * C, [[C, 128], [128 * C, n_rows // 128], [1, C]])

def emit_cast_pass(kb, ctx, src, wbf, R, C, tag, rg_rows=512, bufs=2):
    nc = kb.nc
    a = rg_rows // 128
    cb = [ctx.enter_context(nc.sbuf_tensor(f"{tag}_cb{i}", [128, a * C], BF16))
          for i in range(bufs)]
    toks = []
    for rg in range(R // rg_rows):
        buf = cb[rg % bufs]
        sv = pack_rows(src, rg * rg_rows, rg_rows, C)
        dv = pack_rows(wbf, rg * rg_rows, rg_rows, C)
        t_ld = kb.dma("gpsimd", buf[:], sv, reads=[src], writes=[buf])
        toks.append(kb.dma("sync", dv, buf[:], reads=[buf], writes=[wbf]))
    return toks


def emit_tern_cast_pass(kb, ctx, src, wbf, R, C, tag, thr, rg_rows=512):
    nc = kb.nc
    a = rg_rows // 128
    tf = ctx.enter_context(nc.sbuf_tensor(f"{tag}_tf", [128, a * C], F32))
    tg = ctx.enter_context(nc.sbuf_tensor(f"{tag}_tg", [128, a * C], F32))
    cb = ctx.enter_context(nc.sbuf_tensor(f"{tag}_cb", [128, a * C], BF16))
    toks = []
    for rg in range(R // rg_rows):
        sv = pack_rows(src, rg * rg_rows, rg_rows, C)
        dv = pack_rows(wbf, rg * rg_rows, rg_rows, C)
        kb.dma("sync", tf[:], sv, reads=[src], writes=[tf])
        kb.op("vector", lambda e: e.tensor_scalar(
            out=tg[:], in0=tf[:], scalar1=thr[:, 0:1], scalar2=None, op0=ALU.is_gt),
            reads=[tf, thr], writes=[tg])
        kb.op("vector", lambda e: e.tensor_scalar(
            out=tf[:], in0=tf[:], scalar1=thr[:, 1:2], scalar2=None, op0=ALU.is_lt),
            reads=[tf, thr], writes=[tf])
        kb.op("vector", lambda e: e.tensor_tensor(
            out=cb[:], in0=tg[:], in1=tf[:], op=ALU.subtract),
            reads=[tg, tf], writes=[cb])
        toks.append(kb.dma("sync", dv, cb[:], reads=[cb], writes=[wbf]))
    return toks


def emit_transpose_tile_lhsT(kb, ctx, wbf, wtt, H, Dd, tag):
    nc = kb.nc
    nd = Dd // 128
    tb = [ctx.enter_context(nc.sbuf_tensor(f"{tag}_tb{i}", [128, H], BF16,
                                           is_transpose_dest=True)) for i in range(2)]
    toks = []
    for c in range(nd):
        buf = tb[c % 2]
        t_ld = kb.dma("sync", buf[:], wbf[:, c * 128:(c + 1) * 128], transpose=True)
        out_ap = APc(wtt.tensor, c * 128,
                     [[nd * 128, 128], [128 * nd * 128, H // 128], [1, 128]])
        toks.append(kb.dma("sync", out_ap, buf[:], writes=[wtt]))
    return toks


def lhsT_block_ap(wtt, hb, nd):
    return APc(wtt.tensor, hb * 128 * nd * 128, [[nd * 128, 128], [1, nd * 128]])


def emit_transpose_tile_rhsT(kb, ctx, wbf, wtt, Dd, Hf, tag, panel=512):
    nc = kb.nc
    HB = Hf // 128
    n_pan = Dd // panel
    tb = [ctx.enter_context(nc.sbuf_tensor(f"{tag}_tb{i}", [128, Dd], BF16,
                                           is_transpose_dest=True)) for i in range(2)]
    toks = []
    for hc in range(HB):
        buf = tb[hc % 2]
        t_ld = kb.dma("sync", buf[:], wbf[:, hc * 128:(hc + 1) * 128], transpose=True)
        out_ap = APc(wtt.tensor, hc * 128 * panel,
                     [[panel, 128], [HB * 128 * panel, n_pan], [1, panel]])
        toks.append(kb.dma("sync", out_ap, buf[:], writes=[wtt]))
    return toks


def emit_absmean_thr(kb, ctx, src, H, Dd, tag, csts, psum_pool, rg_rows=512):
    nc = kb.nc
    RG = rg_rows
    a = RG // 128
    n_rg = H // RG
    thr = ctx.enter_context(nc.sbuf_tensor(f"{tag}_thr", [128, 4], F32))
    kb.op("vector", lambda e: e.memset(thr[:], 0.0), writes=[thr])
    sctx = ctx
    tf = [sctx.enter_context(nc.sbuf_tensor(f"{tag}_af{i}", [128, a * Dd], F32))
          for i in range(2)]
    parts = sctx.enter_context(nc.sbuf_tensor(f"{tag}_parts", [128, n_rg], F32))
    red = sctx.enter_context(nc.sbuf_tensor(f"{tag}_red", [128, 1], F32))
    redT = sctx.enter_context(nc.sbuf_tensor(f"{tag}_redT", [1, 128], F32))
    thr_d = nc.dram_tensor(f"{tag}_thrd", [1, 4], F32, kind="Internal").ap()
    for rg in range(n_rg):
        buf = tf[rg % 2]
        kb.dma("sync", buf[:], pack_rows(src, rg * RG, RG, Dd),
               reads=[src], writes=[buf])
        kb.op("vector", lambda e, b=buf, i=rg: e.tensor_reduce(
            out=parts[:, i:i + 1], in_=b[:], axis=AX.X, op=ALU.add,
            apply_absolute_value=True), reads=[buf], writes=[parts])
    kb.op("vector", lambda e: e.tensor_reduce(
        out=red[:], in_=parts[:], axis=AX.X, op=ALU.add),
        reads=[parts], writes=[red])
    pT = psum_pool[0]
    kb.op("tensor", lambda e: e.transpose(pT[0:1, 0:128], red[:], csts["id_f32"][:]),
          reads=[red, csts["id_f32"]], writes=[pT])
    kb.op("vector", lambda e: e.tensor_copy(out=redT[:], in_=pT[0:1, 0:128]),
          reads=[pT], writes=[redT])
    kb.op("vector", lambda e: e.tensor_reduce(
        out=thr[0:1, 2:3], in_=redT[:], axis=AX.X, op=ALU.add),
        reads=[redT], writes=[thr])
    kb.op("vector", lambda e: e.tensor_scalar(
        out=thr[0:1, 2:3], in0=thr[0:1, 2:3], scalar1=1.0 / (H * Dd), scalar2=1e-5,
        op0=ALU.mult, op1=ALU.max), reads=[thr], writes=[thr])
    kb.op("vector", lambda e: e.tensor_scalar(
        out=thr[0:1, 0:1], in0=thr[0:1, 2:3], scalar1=0.7, scalar2=None,
        op0=ALU.mult), reads=[thr], writes=[thr])
    kb.op("vector", lambda e: e.tensor_scalar(
        out=thr[0:1, 1:2], in0=thr[0:1, 0:1], scalar1=-1.0, scalar2=None,
        op0=ALU.mult), reads=[thr], writes=[thr])
    kb.dma("sync", thr_d, thr[0:1, :])
    kb.dma("sync", thr[:], APc(thr_d.tensor, 0, [[0, 128], [1, 4]]))
    return thr


def emit_fold_h128(kb, ctx, w2pp, csts, psum_pool, tag="w2h"):
    nc = kb.nc
    h128 = csts["h128_bf"]
    bi = [ctx.enter_context(nc.sbuf_tensor(f"{tag}_i{i}", [128, 2048], BF16))
          for i in range(2)]
    bo = [ctx.enter_context(nc.sbuf_tensor(f"{tag}_o{i}", [128, 2048], BF16))
          for i in range(2)]
    for g in range(64):
        fi = bi[g % 2]
        fo = bo[g % 2]
        in_ap = APc(w2pp.tensor, g * 4 * 128 * 512, [[512, 128], [128 * 512, 4], [1, 512]])
        kb.dma("sync", fi[:], in_ap, reads=[w2pp], writes=[fi])
        for b4 in range(4):
            bank = psum_pool[4 + b4 % 2]
            kb.op("tensor", lambda e, b=bank, f=fi, b4=b4: e.matmul(
                b[:], h128[:], f[:, b4 * 512:(b4 + 1) * 512], start=True, stop=True),
                reads=[fi, h128], writes=[bank])
            eng = "vector" if b4 % 2 == 0 else "scalar"
            def drain(e, b=bank, f=fo, b4=b4, eng=eng):
                if eng == "vector":
                    return e.tensor_copy(out=f[:, b4 * 512:(b4 + 1) * 512], in_=b[:])
                return e.copy(out=f[:, b4 * 512:(b4 + 1) * 512], in_=b[:])
            kb.op(eng, drain, reads=[bank], writes=[fo])
        out_ap = APc(w2pp.tensor, g * 4 * 128 * 512, [[512, 128], [128 * 512, 4], [1, 512]])
        kb.dma("sync", out_ap, fo[:], reads=[fo], writes=[w2pp])


def emit_fold_w2(kb, ctx, w2bfT, w2pp, csts, psum_pool, tag="w2f"):
    nc = kb.nc
    Dd = 2048
    fin = [ctx.enter_context(nc.sbuf_tensor(f"{tag}_fi{i}", [128, Dd], BF16))
           for i in range(2)]
    fout = [ctx.enter_context(nc.sbuf_tensor(f"{tag}_fo{i}", [128, Dd], BF16))
            for i in range(2)]
    h64 = csts["h64_bf"]
    toks = []
    for j0 in range(64):
        fi = fin[j0 % 2]
        fo = fout[j0 % 2]
        kb.dma("sync", fi[0:64, :],
               APc(w2bfT.tensor, j0 * Dd, [[128 * Dd, 64], [1, Dd]]))
        kb.dma("sync", fi[64:128, :],
               APc(w2bfT.tensor, (j0 + 64) * Dd, [[128 * Dd, 64], [1, Dd]]))
        for half in range(2):
            for ds in range(4):
                bank = psum_pool[half * 4 + ds]
                kb.op("tensor", lambda e, b=bank, f=fi, h=half, d=ds: e.matmul(
                    b[0:64, :], h64[h * 64:(h + 1) * 64, 0:64],
                    f[h * 64:(h + 1) * 64, d * 512:(d + 1) * 512],
                    start=True, stop=True), reads=[fi, h64], writes=[bank])
                eng = "vector" if ds % 2 == 0 else "scalar"
                def drain(e, b=bank, f=fo, h=half, d=ds, eng=eng):
                    if eng == "vector":
                        return e.tensor_copy(
                            out=f[h * 64:(h + 1) * 64, d * 512:(d + 1) * 512],
                            in_=b[0:64, :])
                    return e.copy(
                        out=f[h * 64:(h + 1) * 64, d * 512:(d + 1) * 512],
                        in_=b[0:64, :])
                kb.op(eng, drain, reads=[bank], writes=[fo])
        sA = kb.dma("sync",
                    APc(w2pp.tensor, j0 * 512,
                        [[128 * 512, 64], [64 * 128 * 512, 4], [1, 512]]),
                    fo[0:64, :], writes=[w2pp])
        sB = kb.dma("sync",
                    APc(w2pp.tensor, (j0 + 64) * 512,
                        [[128 * 512, 64], [64 * 128 * 512, 4], [1, 512]]),
                    fo[64:128, :], writes=[w2pp])
        toks += [sA, sB]
    return toks



def emit_ln_rowtile(kb, bufs, xin_rows, dest_fn, csts, pbanks, par=0):
    ln_in = bufs["ln_in"][par] if isinstance(bufs["ln_in"], list) else bufs["ln_in"]
    lnbf = bufs["lnbf"][par] if isinstance(bufs["lnbf"], list) else bufs["lnbf"]
    dump = lnbf
    stat0 = bufs["stat"]
    stat = stat0[:, par * 8:par * 8 + 8] if isinstance(bufs["ln_in"], list) else stat0
    kb.dma("sync", ln_in[:], xin_rows)
    kb.op("vector", lambda e: e.tensor_reduce(
        out=stat[:, 0:1], in_=ln_in[:], axis=AX.X, op=ALU.add),
        reads=[ln_in], writes=[stat0])
    kb.op("vector", lambda e: e.scalar_tensor_tensor(
        out=dump[:], in0=ln_in[:], scalar=0.0, in1=ln_in[:],
        op0=ALU.add, op1=ALU.mult, accum_out=stat[:, 1:2]),
        reads=[ln_in], writes=[dump, stat0])
    kb.op("vector", lambda e: e.tensor_scalar(
        out=stat[:, 2:3], in0=stat[:, 0:1], scalar1=1.0 / D, scalar2=None,
        op0=ALU.mult), reads=[stat0], writes=[stat0])
    kb.op("vector", lambda e: e.tensor_tensor(
        out=stat[:, 3:4], in0=stat[:, 2:3], in1=stat[:, 2:3], op=ALU.mult),
        reads=[stat0], writes=[stat0])
    kb.op("vector", lambda e: e.tensor_scalar(
        out=stat[:, 4:5], in0=stat[:, 1:2], scalar1=1.0 / D, scalar2=None,
        op0=ALU.mult), reads=[stat0], writes=[stat0])
    kb.op("vector", lambda e: e.tensor_tensor(
        out=stat[:, 4:5], in0=stat[:, 4:5], in1=stat[:, 3:4], op=ALU.subtract),
        reads=[stat0], writes=[stat0])
    kb.op("vector", lambda e: e.tensor_scalar(
        out=stat[:, 4:5], in0=stat[:, 4:5], scalar1=1e-5, scalar2=None,
        op0=ALU.add), reads=[stat0], writes=[stat0])
    kb.op("scalar", lambda e: e.activation(
        out=stat[:, 5:6], in_=stat[:, 4:5], func=AF.Ln, bias=0.0, scale=1.0),
        reads=[stat0], writes=[stat0])
    kb.op("scalar", lambda e: e.activation(
        out=stat[:, 6:7], in_=stat[:, 5:6], func=AF.Exp, bias=0.0, scale=-0.5),
        reads=[stat0], writes=[stat0])
    kb.op("vector", lambda e: e.tensor_scalar(
        out=lnbf[:], in0=ln_in[:], scalar1=stat[:, 2:3], scalar2=stat[:, 6:7],
        op0=ALU.subtract, op1=ALU.mult), reads=[ln_in, stat0], writes=[lnbf])
    id_bf = csts["id_bf"]
    for k in range(16):
        pb = pbanks[k % 2]
        pb_bf = pb[:, 0:256].bitcast(BF16)[:, 0:128]
        kb.op("tensor", lambda e, k=k, pb_bf=pb_bf: e.transpose(
            pb_bf, lnbf[:, k * 128:(k + 1) * 128], id_bf[:]),
            reads=[lnbf, id_bf], writes=[pb])
        dst_t, col0 = dest_fn(k)
        kb.op("vector", lambda e, pb_bf=pb_bf, dst_t=dst_t, col0=col0: e.tensor_copy(
            out=dst_t[:, col0:col0 + 128], in_=pb_bf),
            reads=[pb], writes=[dst_t])



def emit_ffn(kb, ctx, xin, xout, wtt1, wtt3, wpp2, Hf, csts, psum_pool,
             tag="ffn", relu2=False, scale_tile=None):
    nc = kb.nc
    HB = Hf // 128
    TT = 1024
    sb = lambda *a, **k: ctx.enter_context(nc.sbuf_tensor(*a, **k))
    Xp = sb(f"{tag}_Xp", [128, 16 * TT], BF16)
    hid = sb(f"{tag}_hid", [128, HB * TT], BF16)
    _lnbf = sb(f"{tag}_lnb", [128, D], BF16)
    ln_bufs = dict(ln_in=[sb(f"{tag}_lni{i}", [128, D], F32) for i in range(2)],
                   stat=sb(f"{tag}_st", [128, 16], F32),
                   lnbf=[_lnbf, _lnbf])
    w1b = [sb(f"{tag}_w1b{i}", [128, D], BF16) for i in range(2)]
    _w3b = sb(f"{tag}_w3b0", [128, D], BF16) if wtt3 is not None else None
    w3b = [_w3b, _w3b] if wtt3 is not None else None
    w2b = [sb(f"{tag}_w2b{i}", [128, 2 * 512], BF16) for i in range(2)]
    sg = [sb(f"{tag}_sg{i}", [128, 512], F32) for i in range(2)]
    res = [sb(f"{tag}_res{i}", [128, 512], F32) for i in range(2)]

    for tile in range(T_LOC // TT):
        row0 = tile * TT
        for r in range(8):
            emit_ln_rowtile(kb, ln_bufs, xin[row0 + r * 128:row0 + (r + 1) * 128, :],
                            lambda k, r=r: (Xp, k * TT + r * 128), csts,
                            (psum_pool[0], psum_pool[1]), par=r % 2)
        for hb in range(HB):
            wb = w1b[hb % 2]
            kb.dma("sync", wb[:], lhsT_block_ap(wtt1, hb, 16), reads=[wtt1], writes=[wb])
            mats = [(0, wb)]
            if wtt3 is not None:
                w3 = w3b[hb % 2]
                kb.dma("sync", w3[:], lhsT_block_ap(wtt3, hb, 16), reads=[wtt3], writes=[w3])
                mats.append((1, w3))
            for (mi, wtile) in mats:
                for chj in range(TT // 512):
                    bank = psum_pool[mi * 2 + chj]
                    for c in range(16):
                        kb.op("tensor", lambda e, b=bank, w=wtile, c=c, chj=chj:
                              e.matmul(b[:], w[:, c * 128:(c + 1) * 128],
                                       Xp[:, c * TT + chj * 512:c * TT + chj * 512 + 512],
                                       start=(c == 0), stop=(c == 15)),
                              reads=[wtile, Xp], writes=[bank])
            for chj in range(TT // 512):
                bankg = psum_pool[0 + chj]
                sgb = sg[chj]
                col = hb * TT + chj * 512
                if relu2:
                    kb.op("scalar", lambda e, b=bankg, s=sgb: e.activation(
                        out=s[:], in_=b[:], func=AF.Relu, bias=0.0, scale=1.0),
                        reads=[bankg], writes=[sgb])
                    kb.op("vector", lambda e, s=sgb, col=col: e.tensor_tensor(
                        out=hid[:, col:col + 512], in0=s[:], in1=s[:], op=ALU.mult),
                        reads=[sgb], writes=[hid])
                else:
                    kb.op("scalar", lambda e, b=bankg, s=sgb: e.activation(
                        out=s[:], in_=b[:], func=AF.Silu, bias=0.0, scale=1.0),
                        reads=[bankg], writes=[sgb])
                    bankl = psum_pool[2 + chj]
                    kb.op("vector", lambda e, s=sgb, b=bankl, col=col: e.tensor_tensor(
                        out=hid[:, col:col + 512], in0=s[:], in1=b[:], op=ALU.mult),
                        reads=[sgb, bankl], writes=[hid])
        for pn in range(D // 512):
            for hbg in range((HB + 1) // 2):
                nb4 = min(2, HB - hbg * 2)
                wb = w2b[hbg % 2]
                in_ap = APc(wpp2.tensor, (pn * HB + hbg * 2) * 128 * 512,
                            [[512, 128], [128 * 512, nb4], [1, 512]])
                kb.dma("sync", wb[:, 0:nb4 * 512], in_ap, reads=[wpp2], writes=[wb])
                for b4 in range(nb4):
                    hb = hbg * 2 + b4
                    for tg in range(TT // 128):
                        bank = psum_pool[tg]
                        kb.op("tensor", lambda e, b=bank, hb=hb, tg=tg, b4=b4, w=wb:
                              e.matmul(b[:], hid[:, hb * TT + tg * 128: hb * TT + tg * 128 + 128],
                                       w[:, b4 * 512:(b4 + 1) * 512],
                                       start=(hb == 0), stop=(hb == HB - 1)),
                              reads=[wb, hid], writes=[bank])
            for tg in range(TT // 128):
                bank = psum_pool[tg]
                rs = res[tg % 2]
                kb.dma("scalar", rs[:],
                       xin[row0 + tg * 128:row0 + (tg + 1) * 128,
                           pn * 512:(pn + 1) * 512], writes=[rs])
                if relu2 and scale_tile is not None:
                    kb.op("vector", lambda e, b=bank, rs=rs: e.scalar_tensor_tensor(
                        out=rs[:], in0=b[:], scalar=scale_tile[:, 0:1], in1=rs[:],
                        op0=ALU.mult, op1=ALU.add),
                        reads=[bank, scale_tile, rs], writes=[rs])
                else:
                    kb.op("vector", lambda e, b=bank, rs=rs: e.scalar_tensor_tensor(
                        out=rs[:], in0=b[:], scalar=0.5, in1=rs[:],
                        op0=ALU.mult, op1=ALU.add), reads=[bank, rs], writes=[rs])
                kb.dma("sync",
                       xout[row0 + tg * 128:row0 + (tg + 1) * 128,
                            pn * 512:(pn + 1) * 512], rs[:], reads=[rs])



def build_prep(nprep):
    nc = bass.Bass("TRN2", target_bir_lowering=False)
    W1 = nc.dram_tensor("W1", [8192, 2048], F32, kind="ExternalInput").ap()
    W3 = nc.dram_tensor("W3", [8192, 2048], F32, kind="ExternalInput").ap()
    W2 = nc.dram_tensor("W2", [2048, 8192], F32, kind="ExternalInput").ap()
    WFC = nc.dram_tensor("WFC", [5120, 2048], F32, kind="ExternalInput").ap()
    WPJ = nc.dram_tensor("WPJ", [2048, 5120], F32, kind="ExternalInput").ap()

    w1tt = nc.dram_tensor("w1tt", [64 * 2048, 128], BF16, kind="ExternalOutput").ap()
    w3tt = nc.dram_tensor("w3tt", [64 * 2048, 128], BF16, kind="ExternalOutput").ap()
    w2pp = nc.dram_tensor("w2pp", [4 * 64 * 128, 512], BF16, kind="ExternalOutput").ap()
    wfctt = nc.dram_tensor("wfctt", [40 * 2048, 128], BF16, kind="ExternalOutput").ap()
    wpjtt = nc.dram_tensor("wpjtt", [4 * 40 * 128, 512], BF16, kind="ExternalOutput").ap()
    scale_o = nc.dram_tensor("scale_o", [1, 1], F32, kind="ExternalOutput").ap()

    w1bf = nc.dram_tensor("w1bf", [8192, 2048], BF16, kind="Internal").ap()
    w3bf = nc.dram_tensor("w3bf", [8192, 2048], BF16, kind="Internal").ap()
    w2bf = nc.dram_tensor("w2bf", [2048, 8192], BF16, kind="Internal").ap()
    w2bfT = nc.dram_tensor("w2bfT", [8192, 2048], BF16, kind="Internal").ap()
    wfcbf = nc.dram_tensor("wfcbf", [5120, 2048], BF16, kind="Internal").ap()
    wpjbf = nc.dram_tensor("wpjbf", [2048, 5120], BF16, kind="Internal").ap()

    with ExitStack() as ctx:
        kb = KB(nc, ctx)
        psum_pool = [ctx.enter_context(nc.psum_tensor(f"pp{i}", [128, 512], F32))
                     for i in range(8)]
        csts = declare_and_load_consts(kb, ctx, ["id_f32", "h64_bf"])

        stile_am = [None, None]
        with ExitStack() as sctx:
            emit_cast_pass(kb, sctx, W1, w1bf, 8192, 2048, "w1c")
            emit_transpose_tile_lhsT(kb, sctx, w1bf, w1tt, 8192, 2048, "w1t")
            emit_cast_pass(kb, sctx, W3, w3bf, 8192, 2048, "w3c")
            emit_transpose_tile_lhsT(kb, sctx, w3bf, w3tt, 8192, 2048, "w3t")
            emit_cast_pass(kb, sctx, W2, w2bf, 2048, 8192, "w2c", rg_rows=256, bufs=1)
            emit_transpose_tile_rhsT_natural(kb, sctx, w2bf, w2bfT, "w2t")
            emit_fold_w2(kb, sctx, w2bfT, w2pp, csts, psum_pool)
        kb.barrier()
        am_fc = ctx.enter_context(nc.sbuf_tensor("pam_fc", [1, 1], F32))
        am_pj = ctx.enter_context(nc.sbuf_tensor("pam_pj", [1, 1], F32))
        with ExitStack() as sctx:
            thr_fc = emit_absmean_thr(kb, sctx, WFC, 5120, 2048, "amf", csts,
                                      psum_pool, rg_rows=256)
            emit_tern_cast_pass(kb, sctx, WFC, wfcbf, 5120, 2048, "tfc", thr_fc,
                                rg_rows=256)
            emit_transpose_tile_lhsT(kb, sctx, wfcbf, wfctt, 5120, 2048, "wft")
            kb.op("vector", lambda e, t=thr_fc: e.tensor_copy(out=am_fc[:], in_=t[0:1, 2:3]),
                  reads=[thr_fc], writes=[am_fc])
        kb.barrier()
        with ExitStack() as sctx:
            thr_pj = emit_absmean_thr(kb, sctx, WPJ, 2048, 5120, "amp", csts,
                                      psum_pool, rg_rows=128)
            emit_tern_cast_pass(kb, sctx, WPJ, wpjbf, 2048, 5120, "tpj", thr_pj,
                                rg_rows=128)
            emit_transpose_tile_rhsT(kb, sctx, wpjbf, wpjtt, 2048, 5120, "wpt")
            kb.op("vector", lambda e, t=thr_pj: e.tensor_copy(out=am_pj[:], in_=t[0:1, 2:3]),
                  reads=[thr_pj], writes=[am_pj])
        kb.barrier()
        thr_fc, thr_pj = am_fc, am_pj
        stile = ctx.enter_context(nc.sbuf_tensor("stile", [1, 1], F32))
        kb.op("vector", lambda e, t=thr_fc: e.tensor_tensor(
            out=stile[:], in0=t[0:1, 0:1], in1=t[0:1, 0:1], op=ALU.mult),
            reads=[thr_fc], writes=[stile])
        kb.op("vector", lambda e, t=thr_pj: e.tensor_tensor(
            out=stile[:], in0=stile[:], in1=t[0:1, 0:1], op=ALU.mult),
            reads=[stile, thr_pj], writes=[stile])
        kb.dma("sync", scale_o, stile[:], reads=[stile])

        block = ctx.enter_context(nc.Block())
        kb.finalize(block)
    return nc


def emit_transpose_tile_rhsT_natural(kb, ctx, wbf, wT, tag):
    nc = kb.nc
    tb = [ctx.enter_context(nc.sbuf_tensor(f"{tag}_tb{i}", [128, 2048], BF16,
                                           is_transpose_dest=True)) for i in range(2)]
    for hc in range(64):
        buf = tb[hc % 2]
        kb.dma("sync", buf[:], wbf[:, hc * 128:(hc + 1) * 128], transpose=True)
        kb.dma("sync", wT[hc * 128:(hc + 1) * 128, :], buf[:])


def build_ffn_stage(Hf, relu2):
    nc = bass.Bass("TRN2", target_bir_lowering=False)
    HB = Hf // 128
    xin = nc.dram_tensor("xin", [T_LOC, D], F32, kind="ExternalInput").ap()
    xout = nc.dram_tensor("xout", [T_LOC, D], F32, kind="ExternalOutput").ap()
    if relu2:
        wtt1 = nc.dram_tensor("wfctt", [HB * 2048, 128], BF16, kind="ExternalInput").ap()
        wpp2 = nc.dram_tensor("wpjtt", [4 * HB * 128, 512], BF16, kind="ExternalInput").ap()
        scale_i = nc.dram_tensor("scale_i", [1, 1], F32, kind="ExternalInput").ap()
        wtt3 = None
    else:
        wtt1 = nc.dram_tensor("w1tt", [HB * 2048, 128], BF16, kind="ExternalInput").ap()
        wtt3 = nc.dram_tensor("w3tt", [HB * 2048, 128], BF16, kind="ExternalInput").ap()
        wpp2 = nc.dram_tensor("w2pp", [4 * HB * 128, 512], BF16, kind="ExternalInput").ap()
    with ExitStack() as ctx:
        kb = KB(nc, ctx)
        psum_pool = [ctx.enter_context(nc.psum_tensor(f"pp{i}", [128, 512], F32))
                     for i in range(8)]
        names = ["id_bf"] + ([] if relu2 else ["h128_bf"])
        csts = declare_and_load_consts(kb, ctx, names)
        scale_tile = None
        if relu2:
            scale_tile = ctx.enter_context(nc.sbuf_tensor("sct", [128, 1], F32))
            kb.dma("sync", scale_tile[:], APc(scale_i.tensor, 0, [[0, 128], [1, 1]]))
        emit_ffn(kb, ctx, xin, xout, wtt1, wtt3, wpp2, Hf, csts, psum_pool,
                 tag="f", relu2=relu2, scale_tile=scale_tile)
        block = ctx.enter_context(nc.Block())
        kb.finalize(block)
    return nc





def emit_l1attn(kb, ctx, xin, xout, W, bvec, qsp, qsp2, csts, psum_pool, tag="a1"):
    nc = kb.nc
    sb = lambda *a, **k: ctx.enter_context(nc.sbuf_tensor(*a, **k))
    TT = T_LOC
    Xp = sb(f"{tag}_Xp", [128, 16 * TT], BF16)
    ln_bufs = dict(ln_in=[sb(f"{tag}_lni{i}", [128, D], F32) for i in range(2)],
                   stat=sb(f"{tag}_st", [128, 16], F32),
                   lnbf=[sb(f"{tag}_lnb{i}", [128, D], BF16) for i in range(2)])
    for r in range(32):
        emit_ln_rowtile(kb, ln_bufs, xin[r * 128:(r + 1) * 128, :],
                        lambda k, r=r: (Xp, k * TT + r * 128), csts,
                        (psum_pool[6], psum_pool[7]), par=r % 2)

    brow = sb(f"{tag}_brow", [48, 128], F32)
    ball = sb(f"{tag}_ball", [128, 48], F32)
    kb.dma("sync", brow[:], APc(bvec.tensor, 0, [[128, 48], [1, 128]]))
    kb.op("tensor", lambda e: e.transpose(psum_pool[5][0:128, 0:48], brow[:],
                                          csts["id_f32"][0:48, 0:48]),
          reads=[brow, csts["id_f32"]], writes=[psum_pool[5]])
    kb.op("vector", lambda e: e.tensor_copy(out=ball[:], in_=psum_pool[5][0:128, 0:48]),
          reads=[psum_pool[5]], writes=[ball])

    wq = [sb(f"{tag}_wq{i}", [128, 16 * 128], BF16) for i in range(2)]
    wv = [sb(f"{tag}_wv{i}", [128, 16 * 128], BF16) for i in range(2)]
    qf = sb(f"{tag}_qf", [128, 512], F32)
    vf = sb(f"{tag}_vf", [128, 512], F32)
    dmp = sb(f"{tag}_dq", [128, 512], BF16)
    qfull = sb(f"{tag}_qfl", [128, TT], BF16)
    parts = sb(f"{tag}_pts", [128, 24], F32)
    sqq = sb(f"{tag}_sqq", [128, 32], F32)
    sqk = sb(f"{tag}_sqk", [128, 32], F32)
    skv = sb(f"{tag}_skv", [128, 32], F32)
    s_all = sb(f"{tag}_sal", [128, 32], F32)

    def wload(buf, ft):
        in_ap = APc(W.tensor, ft * 128, [[6144, 128], [6144 * 128, 16], [1, 128]])
        return kb.dma("gpsimd", buf[:], in_ap, reads=[W], writes=[buf])

    def project(buf, ft_local, tc, bank):
        for c in range(16):
            kb.op("tensor", lambda e, b=bank, w=buf, c=c, tc=tc:
                  e.matmul(b[:], w[:, c * 128:(c + 1) * 128],
                           Xp[:, c * TT + tc * 512:c * TT + tc * 512 + 512],
                           start=(c == 0), stop=(c == 15)),
                  reads=[buf, Xp], writes=[bank])

    for ft in range(16):
        buf = wq[ft % 2]
        wload(buf, ft)
        for tc in range(8):
            bank = psum_pool[tc % 2]
            project(buf, ft, tc, bank)
            kb.op("vector", lambda e, b=bank, ft=ft: e.tensor_scalar(
                out=qf[:], in0=b[:], scalar1=ball[:, ft:ft + 1], scalar2=None,
                op0=ALU.add), reads=[bank, ball], writes=[qf])
            kb.op("vector", lambda e, tc=tc: e.scalar_tensor_tensor(
                out=dmp[:], in0=qf[:], scalar=0.0, in1=qf[:], op0=ALU.add,
                op1=ALU.mult, accum_out=parts[:, tc:tc + 1]),
                reads=[qf], writes=[dmp, parts])
            kb.op("scalar", lambda e, tc=tc: e.copy(
                out=qfull[:, tc * 512:(tc + 1) * 512], in_=qf[:]),
                reads=[qf], writes=[qfull])
        for b2 in range(2):
            kb.op("vector", lambda e, ft=ft, b2=b2: e.tensor_reduce(
                out=sqq[:, ft * 2 + b2:ft * 2 + b2 + 1], in_=parts[:, b2 * 4:b2 * 4 + 4],
                axis=AX.X, op=ALU.add), reads=[parts], writes=[sqq])
        kb.dma("sync", qsp[ft * 128:(ft + 1) * 128, :], qfull[:],
               reads=[qfull], writes=[qsp])

    for ft in range(16):
        bufk = wq[ft % 2]
        bufv = wv[ft % 2]
        wload(bufk, 16 + ft)
        wload(bufv, 32 + ft)
        for tc in range(8):
            bank = psum_pool[tc % 2]
            bank2 = psum_pool[2 + tc % 2]
            project(bufk, ft, tc, bank)
            project(bufv, ft, tc, bank2)
            kb.op("vector", lambda e, b=bank, ft=ft: e.tensor_scalar(
                out=qf[:], in0=b[:], scalar1=ball[:, 16 + ft:17 + ft], scalar2=None,
                op0=ALU.add), reads=[bank, ball], writes=[qf])
            kb.op("vector", lambda e, b=bank2, ft=ft: e.tensor_scalar(
                out=vf[:], in0=b[:], scalar1=ball[:, 32 + ft:33 + ft], scalar2=None,
                op0=ALU.add), reads=[bank2, ball], writes=[vf])
            kb.op("vector", lambda e, tc=tc: e.scalar_tensor_tensor(
                out=dmp[:], in0=qf[:], scalar=0.0, in1=qf[:], op0=ALU.add,
                op1=ALU.mult, accum_out=parts[:, 8 + tc:9 + tc]),
                reads=[qf], writes=[dmp, parts])
            kb.op("vector", lambda e, tc=tc: e.scalar_tensor_tensor(
                out=dmp[:], in0=qf[:], scalar=0.0, in1=vf[:], op0=ALU.add,
                op1=ALU.mult, accum_out=parts[:, 16 + tc:17 + tc]),
                reads=[qf, vf], writes=[dmp, parts])
        for b2 in range(2):
            kb.op("vector", lambda e, ft=ft, b2=b2: e.tensor_reduce(
                out=sqk[:, ft * 2 + b2:ft * 2 + b2 + 1],
                in_=parts[:, 8 + b2 * 4:8 + b2 * 4 + 4],
                axis=AX.X, op=ALU.add), reads=[parts], writes=[sqk])
            kb.op("vector", lambda e, ft=ft, b2=b2: e.tensor_reduce(
                out=skv[:, ft * 2 + b2:ft * 2 + b2 + 1],
                in_=parts[:, 16 + b2 * 4:16 + b2 * 4 + 4],
                axis=AX.X, op=ALU.add), reads=[parts], writes=[skv])

    kb.op("vector", lambda e: e.tensor_tensor(
        out=s_all[:], in0=sqk[:], in1=sqq[:], op=ALU.mult),
        reads=[sqk, sqq], writes=[s_all])
    kb.op("scalar", lambda e: e.activation(
        out=s_all[:], in_=s_all[:], func=AF.Ln, bias=0.0, scale=1.0),
        reads=[s_all], writes=[s_all])
    kb.op("scalar", lambda e: e.activation(
        out=s_all[:], in_=s_all[:], func=AF.Exp, bias=0.0, scale=-0.5),
        reads=[s_all], writes=[s_all])
    kb.op("vector", lambda e: e.tensor_tensor(
        out=s_all[:], in0=s_all[:], in1=skv[:], op=ALU.mult),
        reads=[s_all, skv], writes=[s_all])

    for ft in range(16):
        kb.dma("sync", qfull[:], qsp[ft * 128:(ft + 1) * 128, :],
               reads=[qsp], writes=[qfull])
        for b2 in range(2):
            kb.op("vector", lambda e, ft=ft, b2=b2: e.tensor_scalar(
                out=qfull[:, b2 * T_SEQ:(b2 + 1) * T_SEQ],
                in0=qfull[:, b2 * T_SEQ:(b2 + 1) * T_SEQ],
                scalar1=s_all[:, ft * 2 + b2:ft * 2 + b2 + 1], scalar2=None,
                op0=ALU.mult), reads=[qfull, s_all], writes=[qfull])
        kb.dma("sync", qsp2[ft * 128:(ft + 1) * 128, :], qfull[:],
               reads=[qfull], writes=[qsp2])

    qt = [sb(f"{tag}_qt{i}", [128, D], BF16, is_transpose_dest=True) for i in range(2)]
    _rs = sb(f"{tag}_rs0", [128, D], F32)
    rs = [_rs, _rs]
    for rt in range(32):
        tq = qt[rt % 2]
        tr = rs[rt % 2]
        kb.dma("sync", tq[:], qsp2[:, rt * 128:(rt + 1) * 128],
               reads=[qsp2], writes=[tq], transpose=True)
        kb.dma("scalar", tr[:], xin[rt * 128:(rt + 1) * 128, :], writes=[tr])
        kb.op("vector", lambda e, tq=tq, tr=tr: e.tensor_tensor(
            out=tr[:], in0=tr[:], in1=tq[:], op=ALU.add),
            reads=[tq, tr], writes=[tr])
        kb.dma("sync", xout[rt * 128:(rt + 1) * 128, :], tr[:], reads=[tr])


def build_l1attn_stage():
    nc = bass.Bass("TRN2", target_bir_lowering=False)
    xin = nc.dram_tensor("xin", [T_LOC, D], F32, kind="ExternalInput").ap()
    W = nc.dram_tensor("W", [2048, 6144], F32, kind="ExternalInput").ap()
    bvec = nc.dram_tensor("b", [6144], F32, kind="ExternalInput").ap()
    xout = nc.dram_tensor("xout", [T_LOC, D], F32, kind="ExternalOutput").ap()
    qsp = nc.dram_tensor("qsp", [2048, T_LOC], BF16, kind="Internal").ap()
    qsp2 = nc.dram_tensor("qsp2", [2048, T_LOC], BF16, kind="Internal").ap()
    with ExitStack() as ctx:
        kb = KB(nc, ctx)
        psum_pool = [ctx.enter_context(nc.psum_tensor(f"pp{i}", [128, 512], F32))
                     for i in range(8)]
        csts = declare_and_load_consts(kb, ctx, ["id_bf", "id_f32"])
        emit_l1attn(kb, ctx, xin, xout, W, bvec, qsp, qsp2, csts, psum_pool)
        block = ctx.enter_context(nc.Block())
        kb.finalize(block)
    return nc


H_HEADS = 16
SCAN_SLOTS = 2
SCAN_RR = True
SCAN_PART = 0
SCAN_PACK = False
DK = 96
DV = 192
CH = 128
NCH = T_SEQ // CH


def emit_l0attn(kb, ctx, xin, xout, wts, scr, csts, psum_pool, cv, tag="a0"):
    nc = kb.nc
    sb = lambda *a, **k: ctx.enter_context(nc.sbuf_tensor(*a, **k))
    qsp, ksp, vsp, onsp, xsp = scr["qsp"], scr["ksp"], scr["vsp"], scr["onsp"], scr["xsp"]

    cnv = sb(f"{tag}_cnv", [128, 192], F32)
    mq_sb = sb(f"{tag}_mq", [128, 12 * 16], BF16)
    mqt_sb = sb(f"{tag}_mqt", [16, 12 * 128], BF16)
    dt_sb = sb(f"{tag}_dt", [16, 1], F32)
    nae = sb(f"{tag}_nae", [16, 1], F32)
    nw_bc = sb(f"{tag}_nw", [128, DV], F32)
    Gt_all = sb(f"{tag}_Gt", [128, 512], F32)
    EGt_all = sb(f"{tag}_EGt", [128, 512], F32)
    NBt_all = sb(f"{tag}_NBt", [128, 512], F32)
    PBt_all = sb(f"{tag}_PBt", [128, 512], F32)

    kb.dma("sync", cnv[:, 0:48],
           APc(wts["conv_q"].tensor, 0, [[4, 128], [128 * 4, 12], [1, 4]]))
    kb.dma("sync", cnv[:, 48:96],
           APc(wts["conv_k"].tensor, 0, [[4, 128], [128 * 4, 12], [1, 4]]))
    kb.dma("sync", cnv[:, 96:192],
           APc(wts["conv_v"].tensor, 0, [[4, 128], [128 * 4, 24], [1, 4]]))
    kb.dma("sync", mq_sb[:], APc(csts["mq_d"].tensor, 0, [[16, 128], [128 * 16, 12], [1, 16]]))
    kb.dma("sync", mqt_sb[:], csts["mqt_d"])
    kb.dma("sync", dt_sb[:], APc(wts["dt_bias"].tensor, 0, [[1, 16], [1, 1]]))
    kb.dma("sync", nae[:], APc(wts["A_log"].tensor, 0, [[1, 16], [1, 1]]))
    kb.op("scalar", lambda e: e.activation(out=nae[:], in_=nae[:], func=AF.Exp,
                                           bias=0.0, scale=1.0),
          reads=[nae], writes=[nae])
    kb.op("vector", lambda e: e.tensor_scalar(out=nae[:], in0=nae[:], scalar1=-1.0,
                                              scalar2=None, op0=ALU.mult),
          reads=[nae], writes=[nae])
    kb.dma("sync", nw_bc[:], APc(wts["norm_weight"].tensor, 0, [[0, 128], [1, DV]]))

    id_bf, id_f32, ones_f32 = csts["id_bf"], csts["id_f32"], csts["ones_f32"]
    mask_lows, mask_upi = csts["mask_lows"], csts["mask_upi"]

    with ExitStack() as actx:
        asb = lambda *a, **k: actx.enter_context(nc.sbuf_tensor(*a, **k))
        Xp = asb(f"{tag}_Xp", [128, 16 * T_LOC], BF16)
        with ExitStack() as lctx:
            lsb = lambda *a, **k: lctx.enter_context(nc.sbuf_tensor(*a, **k))
            ln_bufs = dict(ln_in=[lsb(f"{tag}_lni{i}", [128, D], F32) for i in range(2)],
                           stat=lsb(f"{tag}_lst", [128, 16], F32),
                           lnbf=[lsb(f"{tag}_lnb{i}", [128, D], BF16) for i in range(2)])
            for r in range(32):
                emit_ln_rowtile(kb, ln_bufs, xin[r * 128:(r + 1) * 128, :],
                                lambda k, r=r: (Xp, k * T_LOC + r * 128), csts,
                                (psum_pool[6], psum_pool[7]), par=r % 2)
        kb.barrier()

        wbq = asb(f"{tag}_wbq", [128, 16 * 16], BF16)
        wba = asb(f"{tag}_wba", [128, 16 * 16], BF16)
        kb.dma("gpsimd", wbq[:], APc(wts["Wb"].tensor, 0, [[16, 128], [128 * 16, 16], [1, 16]]),
               reads=[wts["Wb"]], writes=[wbq])
        kb.dma("gpsimd", wba[:], APc(wts["Wa"].tensor, 0, [[16, 128], [128 * 16, 16], [1, 16]]),
               reads=[wts["Wa"]], writes=[wba])
        nbfull = asb(f"{tag}_nbf", [16, T_SEQ], F32)
        gts = asb(f"{tag}_gts", [16, T_SEQ], F32)
        onesT = asb(f"{tag}_onT", [16, CH], F32)
        kb.op("vector", lambda e: e.memset(onesT[:], 1.0), writes=[onesT])
        wring = [asb(f"{tag}_wr{i}", [128, 16 * 128], BF16) for i in range(2)]
        praw = asb(f"{tag}_praw", [128, T_SEQ], F32)
        pcv = asb(f"{tag}_pcv", [128, T_SEQ], F32)
        sqacc = asb(f"{tag}_sqa", [16, T_SEQ], F32)
        tmp16 = sqacc
        qsqbf = asb(f"{tag}_qsq", [128, T_SEQ], BF16)
        rstdbf = asb(f"{tag}_rsb", [16, T_SEQ], BF16)
        spl = asb(f"{tag}_spl", [128, T_SEQ], BF16)

        def beta_g(b2):
            for (wtile, outt) in ((wbq, nbfull), (wba, gts)):
                for tc in range(T_SEQ // 512):
                    bank = psum_pool[tc % 2]
                    for c in range(16):
                        col = c * T_LOC + b2 * T_SEQ + tc * 512
                        kb.op("tensor", lambda e, b=bank, w=wtile, c=c, col=col:
                              e.matmul(b[0:16, :], w[:, c * 16:(c + 1) * 16],
                                       Xp[:, col:col + 512],
                                       start=(c == 0), stop=(c == 15)),
                              reads=[wtile, Xp], writes=[bank])
                    kb.op("scalar", lambda e, b=bank, o=outt, tc=tc: e.copy(
                        out=o[0:16, tc * 512:(tc + 1) * 512], in_=b[0:16, :]),
                        reads=[bank], writes=[outt])
            kb.op("scalar", lambda e: e.activation(out=nbfull[0:16, :], in_=nbfull[0:16, :],
                                                   func=AF.Sigmoid, bias=0.0, scale=1.0),
                  reads=[nbfull], writes=[nbfull])
            kb.op("vector", lambda e: e.tensor_scalar(out=nbfull[0:16, :], in0=nbfull[0:16, :],
                                                      scalar1=-2.0, scalar2=None, op0=ALU.mult),
                  reads=[nbfull], writes=[nbfull])
            kb.op("vector", lambda e: e.tensor_scalar(out=gts[0:16, :], in0=gts[0:16, :],
                                                      scalar1=dt_sb[:, 0:1], scalar2=None,
                                                      op0=ALU.add),
                  reads=[gts, dt_sb], writes=[gts])
            kb.op("scalar", lambda e: e.activation(out=tmp16[0:16, :], in_=gts[0:16, :],
                                                   func=AF.Abs, bias=0.0, scale=1.0),
                  reads=[gts], writes=[tmp16])
            kb.op("scalar", lambda e: e.activation(out=tmp16[0:16, :], in_=tmp16[0:16, :],
                                                   func=AF.Exp, bias=0.0, scale=-1.0),
                  reads=[tmp16], writes=[tmp16])
            kb.op("scalar", lambda e: e.activation(out=tmp16[0:16, :], in_=tmp16[0:16, :],
                                                   func=AF.Ln, bias=cv[1.0][0:16, :], scale=1.0),
                  reads=[tmp16, cv["_tile"]], writes=[tmp16])
            kb.op("vector", lambda e: e.tensor_scalar(out=gts[0:16, :], in0=gts[0:16, :],
                                                      scalar1=0.0, scalar2=None, op0=ALU.max),
                  reads=[gts], writes=[gts])
            kb.op("vector", lambda e: e.tensor_tensor(out=gts[0:16, :], in0=gts[0:16, :],
                                                      in1=tmp16[0:16, :], op=ALU.add),
                  reads=[gts, tmp16], writes=[gts])
            kb.op("vector", lambda e: e.tensor_scalar(out=gts[0:16, :], in0=gts[0:16, :],
                                                      scalar1=nae[:, 0:1], scalar2=None,
                                                      op0=ALU.mult),
                  reads=[gts, nae], writes=[gts])
            for c in range(NCH):
                colb = (b2 * NCH + c) * 16
                t0 = c * CH
                kb.op("vector", lambda e, t0=t0: e.tensor_tensor_scan(
                    out=tmp16[0:16, 0:CH], data0=onesT[0:16, :],
                    data1=gts[0:16, t0:t0 + CH], initial=0.0,
                    op0=ALU.mult, op1=ALU.add),
                    reads=[onesT, gts], writes=[tmp16])
                bank = psum_pool[c % 2]
                kb.op("tensor", lambda e, b=bank: e.transpose(
                    b[0:128, 0:16], tmp16[0:16, 0:CH], id_f32[0:16, 0:16]),
                    reads=[tmp16, id_f32], writes=[bank])
                kb.op("vector", lambda e, b=bank, colb=colb: e.tensor_copy(
                    out=Gt_all[:, colb:colb + 16], in_=b[0:128, 0:16]),
                    reads=[bank], writes=[Gt_all])
                bank2 = psum_pool[2 + c % 2]
                kb.op("tensor", lambda e, b=bank2, t0=t0: e.transpose(
                    b[0:128, 0:16], nbfull[0:16, t0:t0 + CH], id_f32[0:16, 0:16]),
                    reads=[nbfull, id_f32], writes=[bank2])
                kb.op("vector", lambda e, b=bank2, colb=colb: e.tensor_copy(
                    out=NBt_all[:, colb:colb + 16], in_=b[0:128, 0:16]),
                    reads=[bank2], writes=[NBt_all])

        def proj_conv_silu(b2, ft, Wsrc, Hf, tap0):
            buf = wring[ft % 2]
            kb.dma("gpsimd", buf[:],
                   APc(Wsrc.tensor, ft * 128, [[Hf, 128], [Hf * 128, 16], [1, 128]]),
                   reads=[Wsrc], writes=[buf])
            for tc in range(4):
                bank = psum_pool[4 + tc % 2]
                for c in range(16):
                    col = c * T_LOC + b2 * T_SEQ + tc * 512
                    kb.op("tensor", lambda e, b=bank, w=buf, c=c, col=col:
                          e.matmul(b[:], w[:, c * 128:(c + 1) * 128],
                                   Xp[:, col:col + 512], start=(c == 0), stop=(c == 15)),
                          reads=[buf, Xp], writes=[bank])
                kb.op("scalar", lambda e, b=bank, tc=tc: e.copy(
                    out=praw[:, tc * 512:(tc + 1) * 512], in_=b[:]),
                    reads=[bank], writes=[praw])
            kb.op("vector", lambda e: e.tensor_scalar(
                out=pcv[:], in0=praw[:], scalar1=cnv[:, tap0 + 3:tap0 + 4],
                scalar2=None, op0=ALU.mult), reads=[praw, cnv], writes=[pcv])
            for tau in (2, 1, 0):
                sh = 3 - tau
                kb.op("vector", lambda e, tau=tau, sh=sh: e.scalar_tensor_tensor(
                    out=pcv[:, sh:T_SEQ], in0=praw[:, 0:T_SEQ - sh],
                    scalar=cnv[:, tap0 + tau:tap0 + tau + 1],
                    in1=pcv[:, sh:T_SEQ], op0=ALU.mult, op1=ALU.add),
                    reads=[praw, cnv, pcv], writes=[pcv])
            kb.op("scalar", lambda e: e.activation(
                out=praw[:], in_=pcv[:], func=AF.Silu, bias=0.0, scale=1.0),
                reads=[pcv], writes=[praw])

        def qk_pipe(b2, Wsrc, convoff, spdst, scale_bias):
            for ft in range(12):
                proj_conv_silu(b2, ft, Wsrc, 1536, convoff + ft * 4)
                kb.op("vector", lambda e: e.tensor_tensor(
                    out=qsqbf[:], in0=praw[:], in1=praw[:], op=ALU.mult),
                    reads=[praw], writes=[qsqbf])
                kb.op("vector", lambda e: e.tensor_copy(out=spl[:], in_=praw[:]),
                      reads=[praw], writes=[spl])
                kb.dma("sync", spdst[(b2 * 12 + ft) * 128:(b2 * 12 + ft) * 128 + 128, :],
                       spl[:], reads=[spl], writes=[spdst])
                for tc in range(4):
                    bank = psum_pool[6 + tc % 2]
                    kb.op("tensor", lambda e, b=bank, ft=ft, tc=tc: e.matmul(
                        b[0:16, :], mq_sb[:, ft * 16:(ft + 1) * 16],
                        qsqbf[:, tc * 512:(tc + 1) * 512], start=True, stop=True),
                        reads=[mq_sb, qsqbf], writes=[bank])
                    if ft == 0:
                        kb.op("vector", lambda e, b=bank, tc=tc: e.tensor_copy(
                            out=sqacc[0:16, tc * 512:(tc + 1) * 512], in_=b[0:16, :]),
                            reads=[bank], writes=[sqacc])
                    else:
                        kb.op("vector", lambda e, b=bank, tc=tc: e.tensor_tensor(
                            out=sqacc[0:16, tc * 512:(tc + 1) * 512],
                            in0=sqacc[0:16, tc * 512:(tc + 1) * 512],
                            in1=b[0:16, :], op=ALU.add),
                            reads=[bank, sqacc], writes=[sqacc])
            kb.op("vector", lambda e: e.tensor_scalar(
                out=sqacc[0:16, :], in0=sqacc[0:16, :], scalar1=1e-6, scalar2=None,
                op0=ALU.add), reads=[sqacc], writes=[sqacc])
            kb.op("scalar", lambda e: e.activation(
                out=sqacc[0:16, :], in_=sqacc[0:16, :], func=AF.Ln, bias=0.0, scale=1.0),
                reads=[sqacc], writes=[sqacc])
            if scale_bias is None:
                kb.op("scalar", lambda e: e.activation(
                    out=rstdbf[0:16, :], in_=sqacc[0:16, :], func=AF.Exp, bias=0.0,
                    scale=-0.5), reads=[sqacc], writes=[rstdbf])
            else:
                kb.op("scalar", lambda e: e.activation(
                    out=rstdbf[0:16, :], in_=sqacc[0:16, :], func=AF.Exp,
                    bias=scale_bias[0:16, :], scale=-0.5),
                    reads=[sqacc, cv["_tile"]], writes=[rstdbf])
            for ft in range(12):
                kb.dma("sync", spl[:],
                       spdst[(b2 * 12 + ft) * 128:(b2 * 12 + ft) * 128 + 128, :],
                       reads=[spdst], writes=[spl])
                for tc in range(4):
                    bank = psum_pool[4 + tc % 2]
                    kb.op("tensor", lambda e, b=bank, ft=ft, tc=tc: e.matmul(
                        b[:], mqt_sb[0:16, ft * 128:(ft + 1) * 128],
                        rstdbf[0:16, tc * 512:(tc + 1) * 512], start=True, stop=True),
                        reads=[mqt_sb, rstdbf], writes=[bank])
                    kb.op("vector", lambda e, b=bank, tc=tc: e.tensor_tensor(
                        out=spl[:, tc * 512:(tc + 1) * 512],
                        in0=spl[:, tc * 512:(tc + 1) * 512],
                        in1=b[:], op=ALU.mult),
                        reads=[bank, spl], writes=[spl])
                kb.dma("sync", spdst[(b2 * 12 + ft) * 128:(b2 * 12 + ft) * 128 + 128, :],
                       spl[:], reads=[spl], writes=[spdst])

        def v_pipe(b2):
            vtt = qsqbf
            for ft in range(24):
                proj_conv_silu(b2, ft, wts["Wv"], 3072, 96 + ft * 4)
                kb.op("vector", lambda e: e.tensor_copy(out=spl[:], in_=praw[:]),
                      reads=[praw], writes=[spl])
                for tc in range(16):
                    bank = psum_pool[6 + tc % 2]
                    pb_bf = bank[:, 0:64].bitcast(BF16)
                    kb.op("tensor", lambda e, pb=pb_bf, tc=tc: e.transpose(
                        pb, spl[:, tc * 128:(tc + 1) * 128], id_bf[:]),
                        reads=[spl, id_bf], writes=[bank])
                    kb.op("vector", lambda e, pb=pb_bf, tc=tc: e.tensor_copy(
                        out=vtt[:, tc * 128:(tc + 1) * 128], in_=pb),
                        reads=[bank], writes=[vtt])
                out_ap = APc(vsp.tensor, (b2 * T_SEQ) * 3072 + ft * 128,
                             [[3072, 128], [128 * 3072, 16], [1, 128]])
                kb.dma("sync", out_ap, vtt[:], reads=[vtt], writes=[vsp])

        for b2 in range(B_LOC):
            beta_g(b2)
            qk_pipe(b2, wts["Wq"], 0, qsp, cv_key_qbias(cv))
            qk_pipe(b2, wts["Wk"], 48, ksp, None)
            v_pipe(b2)
        kb.op("scalar", lambda e: e.activation(out=EGt_all[:], in_=Gt_all[:],
                                               func=AF.Exp, bias=0.0, scale=1.0),
              reads=[Gt_all], writes=[EGt_all])
        kb.op("vector", lambda e: e.tensor_scalar(out=PBt_all[:], in0=NBt_all[:],
                                                  scalar1=-1.0, scalar2=None, op0=ALU.mult),
              reads=[NBt_all], writes=[PBt_all])
        kb.dma("sync", xsp, Xp[:], reads=[Xp], writes=[xsp])
    kb.barrier()

    with ExitStack() as bctx:
        bsb = lambda *a, **k: bctx.enter_context(nc.sbuf_tensor(*a, **k))
        sel16 = bsb(f"{tag}_sel", [16, 2048], F32)
        kb.dma("sync", sel16[:], csts["sel16_d"], writes=[sel16])
        qsb = [bsb(f"{tag}_qsb{i}", [96, 16 * CH], BF16) for i in range(2)]
        ksb = [bsb(f"{tag}_ksb{i}", [96, 16 * CH], BF16) for i in range(2)]
        vsb = [bsb(f"{tag}_vsb{i}", [128, 16 * DV], BF16) for i in range(2)]
        osb = [bsb(f"{tag}_osb{i}", [128, 16 * DV], BF16) for i in range(2)]
        S32 = [bsb(f"{tag}_S32{i}", [96, 4 * DV], F32) for i in range(4)]
        Sbf = [bsb(f"{tag}_Sbf{i}", [96, 4 * DV], BF16) for i in range(4)]
        Slo = [bsb(f"{tag}_Slo{i}", [96, 4 * DV], BF16) for i in range(4)]
        grow16 = bsb(f"{tag}_g16", [16, 128], F32)

        class Slot:
            def __init__(self, i):
                self.Dneg = bsb(f"{tag}_Dng{i}", [128, 128], F32)
                self.DS = bsb(f"{tag}_DS{i}", [128, 128], F32)
                self.DUi = bsb(f"{tag}_DUi{i}", [128, 128], F32)
                self.P0 = bsb(f"{tag}_P0{i}", [128, 128], BF16)
                self.P0T = bsb(f"{tag}_P0T{i}", [128, 128], BF16)
                self.Pn = [bsb(f"{tag}_Pn{i}{j}", [128, 128], BF16) for j in range(2)]
                self.PnT = [bsb(f"{tag}_PnT{i}{j}", [128, 128], BF16) for j in range(2)]
                self.Lt = bsb(f"{tag}_Lt{i}", [128, 128], BF16)
                self.y32 = bsb(f"{tag}_y32{i}", [128, DV], F32)
                self.ybf = bsb(f"{tag}_ybf{i}", [128, DV], BF16)
                self.o_sb = bsb(f"{tag}_oSB{i}", [128, DV], F32)
                self.o2sb = bsb(f"{tag}_o2{i}", [128, DV], F32)
                self.scl = bsb(f"{tag}_scl{i}", [128, 4], F32)
                self.egl = bsb(f"{tag}_egl{i}", [96, 1], F32)
                self.kr = bsb(f"{tag}_kr{i}", [128, 96], BF16)
                self.dmp = bsb(f"{tag}_dmp{i}", [128, DV], BF16)
        slots = [Slot(i) for i in range(4)]

        def step_gen(b2, c, h, kbuf, qb, vb, ob):
            slot = h % SCAN_SLOTS
            B = slots[slot]
            col = (b2 * NCH + c) * 16 + h
            Gcol = Gt_all[:, col:col + 1]
            EGcol = EGt_all[:, col:col + 1]
            NBcol = NBt_all[:, col:col + 1]
            PBcol = PBt_all[:, col:col + 1]
            Kt = kbuf[0:96, h * CH:(h + 1) * CH]
            Qt = qb[0:96, h * CH:(h + 1) * CH]
            Vt = vb[:, h * DV:(h + 1) * DV]
            sslot, hs = h % 4, h // 4
            S32s = S32[sslot][0:96, hs * DV:(hs + 1) * DV]
            Sbfs = Sbf[sslot][0:96, hs * DV:(hs + 1) * DV]
            Slos = Slo[sslot][0:96, hs * DV:(hs + 1) * DV]
            if SCAN_PACK:
                ba, bb = psum_pool[slot * 2], psum_pool[slot * 2 + 1]
                ap_kq = ba[:, 128:256]
                ap_raw = bb[:, 128:128 + DV]
                ap_sq2 = bb[:, 128:256]
                ap_o2 = bb[:, 0:DV]
                t_kq, t_raw, t_sq2, t_o2 = ba, bb, bb, bb
            else:
                ba, bb = psum_pool[slot * 4], psum_pool[slot * 4 + 1]
                bc, bd = psum_pool[slot * 4 + 2], psum_pool[slot * 4 + 3]
                ap_kq = bc[:, 0:128]
                ap_raw = bd[:, 0:DV]
                ap_sq2 = bc[:, 0:128]
                ap_o2 = bc[:, 0:DV]
                t_kq, t_raw, t_sq2, t_o2 = bc, bd, bc, bc
            kb.op("tensor", lambda e: e.matmul(
                ba[:, 0:128], sel16[0:16, h * 128:(h + 1) * 128], grow16[0:16, :],
                start=True, stop=True), reads=[sel16, grow16], writes=[ba])
            kb.op("tensor", lambda e: e.matmul(
                bb[:, 0:128], Kt, Kt, start=True, stop=True),
                reads=[kbuf], writes=[bb])
            kb.op("tensor", lambda e: e.matmul(
                ap_kq, Kt, Qt, start=True, stop=True),
                reads=[kbuf, qb], writes=[t_kq])
            kb.op("tensor", lambda e: e.matmul(
                ap_raw, Kt, Sbfs, start=True, stop=False),
                reads=[kbuf, Sbf[sslot]], writes=[t_raw])
            kb.op("tensor", lambda e: e.matmul(
                ap_raw, Kt, Slos, start=False, stop=True),
                reads=[kbuf, Slo[sslot]], writes=[t_raw])
            kb.op("vector", lambda e: e.tensor_scalar(
                out=B.Dneg[:], in0=ba[:, 0:128], scalar1=Gcol, scalar2=None,
                op0=ALU.subtract), reads=[ba, Gt_all], writes=[B.Dneg])
            kb.op("vector", lambda e: e.scalar_tensor_tensor(
                out=B.DS[:], in0=B.Dneg[:], scalar=-1.0, in1=mask_lows[:],
                op0=ALU.mult, op1=ALU.add),
                reads=[B.Dneg, mask_lows], writes=[B.DS])
            kb.op("scalar", lambda e: e.activation(
                out=B.DS[:], in_=B.DS[:], func=AF.Exp, bias=0.0, scale=1.0),
                reads=[B.DS], writes=[B.DS])
            kb.op("vector", lambda e: e.tensor_tensor(
                out=B.DUi[:], in0=B.Dneg[:], in1=mask_upi[:], op=ALU.add),
                reads=[B.Dneg, mask_upi], writes=[B.DUi])
            kb.op("scalar", lambda e: e.activation(
                out=B.DUi[:], in_=B.DUi[:], func=AF.Exp, bias=0.0, scale=1.0),
                reads=[B.DUi], writes=[B.DUi])
            kb.op("vector", lambda e: e.scalar_tensor_tensor(
                out=B.P0[:], in0=B.DS[:], scalar=NBcol, in1=bb[:, 0:128],
                op0=ALU.mult, op1=ALU.mult),
                reads=[B.DS, NBt_all, bb], writes=[B.P0])
            pa_bf = ba[:, 0:64].bitcast(BF16)
            kb.op("tensor", lambda e: e.transpose(
                pa_bf, B.P0[:], id_bf[:]), reads=[B.P0, id_bf], writes=[ba])
            kb.op("scalar", lambda e: e.copy(out=B.P0T[:], in_=pa_bf),
                  reads=[ba], writes=[B.P0T])
            kb.op("vector", lambda e: e.tensor_tensor(
                out=B.Lt[:], in0=B.DUi[:], in1=ap_kq, op=ALU.mult),
                reads=[B.DUi, t_kq], writes=[B.Lt])
            kb.op("vector", lambda e: e.tensor_tensor(
                out=B.scl[:, 0:1], in0=NBcol, in1=EGcol, op=ALU.mult),
                reads=[NBt_all, EGt_all], writes=[B.scl])
            kb.op("vector", lambda e: e.tensor_scalar(
                out=B.y32[:], in0=Vt, scalar1=PBcol, scalar2=None, op0=ALU.mult),
                reads=[vb, PBt_all], writes=[B.y32])
            kb.op("vector", lambda e: e.scalar_tensor_tensor(
                out=B.y32[:], in0=ap_raw, scalar=B.scl[:, 0:1], in1=B.y32[:],
                op0=ALU.mult, op1=ALU.add),
                reads=[t_raw, B.scl, B.y32], writes=[B.y32])
            kb.op("scalar", lambda e: e.copy(out=B.ybf[:], in_=B.y32[:]),
                  reads=[B.y32], writes=[B.ybf])
            kb.op("tensor", lambda e: e.matmul(
                ba[0:96, 0:1], sel16[0:16, h * 128:h * 128 + 96], grow16[0:16, 127:128],
                start=True, stop=True), reads=[sel16, grow16], writes=[ba])
            kb.op("scalar", lambda e: e.activation(
                out=B.egl[0:96, :], in_=ba[0:96, 0:1], func=AF.Exp, bias=0.0,
                scale=1.0), reads=[ba], writes=[B.egl])
            if SCAN_PART == 1:
                return
            yield
            for it in range(7):
                Pc = B.P0 if it == 0 else B.Pn[(it - 1) % 2]
                PcT = B.P0T if it == 0 else B.PnT[(it - 1) % 2]
                kb.op("tensor", lambda e, PT=PcT: e.matmul(
                    ap_raw, PT[:], B.ybf[:], start=True, stop=True),
                    reads=[PcT, B.ybf], writes=[t_raw])
                if it < 6:
                    kb.op("tensor", lambda e, P=Pc, PT=PcT: e.matmul(
                        bb[:, 0:128], PT[:], P[:], start=True, stop=True),
                        reads=[Pc, PcT], writes=[bb])
                    kb.op("tensor", lambda e, P=Pc, PT=PcT: e.matmul(
                        ap_sq2, P[:], PT[:], start=True, stop=True),
                        reads=[Pc, PcT], writes=[t_sq2])
                kb.op("vector", lambda e: e.tensor_tensor(
                    out=B.y32[:], in0=B.y32[:], in1=ap_raw, op=ALU.add),
                    reads=[B.y32, t_raw], writes=[B.y32])
                kb.op("scalar", lambda e: e.copy(out=B.ybf[:], in_=B.y32[:]),
                      reads=[B.y32], writes=[B.ybf])
                if it < 6:
                    t_pn = kb.op("vector", lambda e, P=B.Pn[it % 2]: e.tensor_copy(
                        out=P[:], in_=bb[:, 0:128]), reads=[bb], writes=[B.Pn[it % 2]])
                    kb.op("scalar", lambda e, P=B.PnT[it % 2]: e.copy(
                        out=P[:], in_=ap_sq2), reads=[t_sq2],
                        writes=[B.PnT[it % 2]], deps=[t_pn])
                yield
            if SCAN_PART == 2:
                return
            kb.op("tensor", lambda e: e.matmul(
                ba[:, 0:DV], Qt, Sbfs, start=True, stop=False),
                reads=[qb, Sbf[sslot]], writes=[ba])
            kb.op("tensor", lambda e: e.matmul(
                ba[:, 0:DV], Qt, Slos, start=False, stop=True),
                reads=[qb, Slo[sslot]], writes=[ba])
            kb.op("tensor", lambda e: e.matmul(
                ap_o2, B.Lt[:], B.ybf[:], start=True, stop=True),
                reads=[B.Lt, B.ybf], writes=[t_o2])
            kb.op("scalar", lambda e: e.copy(out=B.o2sb[:], in_=ap_o2),
                  reads=[t_o2], writes=[B.o2sb])
            kb.op("vector", lambda e: e.scalar_tensor_tensor(
                out=B.o_sb[:], in0=ba[:, 0:DV], scalar=EGcol, in1=B.o2sb[:],
                op0=ALU.mult, op1=ALU.add),
                reads=[ba, EGt_all, B.o2sb], writes=[B.o_sb])
            kb.op("vector", lambda e: e.scalar_tensor_tensor(
                out=B.dmp[:], in0=B.o_sb[:], scalar=0.0, in1=B.o_sb[:],
                op0=ALU.add, op1=ALU.mult, accum_out=B.scl[:, 1:2]),
                reads=[B.o_sb], writes=[B.dmp, B.scl])
            kb.op("vector", lambda e: e.tensor_scalar(
                out=B.scl[:, 1:2], in0=B.scl[:, 1:2], scalar1=1.0 / DV,
                scalar2=1e-6, op0=ALU.mult, op1=ALU.add),
                reads=[B.scl], writes=[B.scl])
            kb.op("scalar", lambda e: e.activation(
                out=B.scl[:, 2:3], in_=B.scl[:, 1:2], func=AF.Ln, bias=0.0,
                scale=1.0), reads=[B.scl], writes=[B.scl])
            kb.op("scalar", lambda e: e.activation(
                out=B.scl[:, 2:3], in_=B.scl[:, 2:3], func=AF.Exp, bias=0.0,
                scale=-0.5), reads=[B.scl], writes=[B.scl])
            kb.op("vector", lambda e: e.scalar_tensor_tensor(
                out=ob[:, h * DV:(h + 1) * DV], in0=B.o_sb[:], scalar=B.scl[:, 2:3],
                in1=nw_bc[:], op0=ALU.mult, op1=ALU.mult),
                reads=[B.o_sb, B.scl, nw_bc], writes=[ob])
            yield
            if SCAN_PART == 3:
                return
            pd_bf = ba[:, 0:64].bitcast(BF16)[:, 0:96]
            kb.op("tensor", lambda e: e.transpose(
                pd_bf, Kt, id_bf[0:96, 0:96]), reads=[kbuf, id_bf], writes=[ba])
            kb.op("vector", lambda e: e.tensor_scalar(
                out=B.kr[:], in0=pd_bf, scalar1=B.DUi[:, 127:128], scalar2=None,
                op0=ALU.mult), reads=[ba, B.DUi], writes=[B.kr])
            kb.op("tensor", lambda e: e.matmul(
                bb[0:96, 0:DV], B.kr[:], B.ybf[:], start=True, stop=True),
                reads=[B.kr, B.ybf], writes=[bb])
            kb.op("vector", lambda e: e.scalar_tensor_tensor(
                out=S32s, in0=S32s, scalar=B.egl[:, 0:1], in1=bb[0:96, 0:DV],
                op0=ALU.mult, op1=ALU.add),
                reads=[S32[sslot], B.egl, bb], writes=[S32[sslot]])
            kb.op("scalar", lambda e: e.copy(out=Sbfs, in_=S32s),
                  reads=[S32[sslot]], writes=[Sbf[sslot]])
            kb.op("vector", lambda e: e.tensor_tensor(
                out=Slos, in0=S32s, in1=Sbfs, op=ALU.subtract),
                reads=[S32[sslot], Sbf[sslot]], writes=[Slo[sslot]])

        for b2 in range(B_LOC):
            for sl in range(4):
                kb.op("vector", lambda e, sl=sl: e.memset(S32[sl][0:96, :], 0.0),
                      writes=[S32[sl]])
                kb.op("vector", lambda e, sl=sl: e.memset(Sbf[sl][0:96, :], 0.0),
                      writes=[Sbf[sl]])
                kb.op("vector", lambda e, sl=sl: e.memset(Slo[sl][0:96, :], 0.0),
                      writes=[Slo[sl]])
            for c in range(NCH):
                cur = c % 2
                qb, kbuf, vb, ob = qsb[cur], ksb[cur], vsb[cur], osb[cur]
                kb.dma("sync", qb[0:96, :],
                       APc(qsp.tensor, (b2 * 12 * 128) * T_SEQ + c * CH,
                           [[T_SEQ, 96], [96 * T_SEQ, 16], [1, CH]]),
                       reads=[qsp], writes=[qb])
                kb.dma("sync", kbuf[0:96, :],
                       APc(ksp.tensor, (b2 * 12 * 128) * T_SEQ + c * CH,
                           [[T_SEQ, 96], [96 * T_SEQ, 16], [1, CH]]),
                       reads=[ksp], writes=[kbuf])
                kb.dma("sync", vb[:],
                       vsp[b2 * T_SEQ + c * CH:b2 * T_SEQ + c * CH + CH, :],
                       reads=[vsp], writes=[vb])
                colb = (b2 * NCH + c) * 16
                bg = psum_pool[0]
                kb.op("tensor", lambda e, colb=colb: e.transpose(
                    bg[0:16, 0:128], Gt_all[:, colb:colb + 16], id_f32[:]),
                    reads=[Gt_all, id_f32], writes=[bg])
                kb.op("vector", lambda e: e.tensor_copy(
                    out=grow16[:], in_=bg[0:16, 0:128]),
                    reads=[bg], writes=[grow16])
                NS = SCAN_SLOTS
                for hq in range(H_HEADS // NS):
                    gens = [step_gen(b2, c, NS * hq + i, kbuf, qb, vb, ob)
                            for i in range(NS)]
                    if SCAN_RR:
                        alive = [True] * NS
                        while any(alive):
                            for gi in range(NS):
                                if alive[gi]:
                                    try:
                                        next(gens[gi])
                                    except StopIteration:
                                        alive[gi] = False
                    else:
                        for g in gens:
                            for _ in g:
                                pass
                kb.dma("sync", onsp[b2 * T_SEQ + c * CH:b2 * T_SEQ + c * CH + CH, :],
                       ob[:], reads=[ob], writes=[onsp])
    kb.barrier()

    with ExitStack() as cctx:
        csb = lambda *a, **k: cctx.enter_context(nc.sbuf_tensor(*a, **k))
        wg_h = csb(f"{tag}_wg", [128, 16 * 1536], BF16)
        wo_h = csb(f"{tag}_wo", [128, 12 * D], BF16)
        xch = [csb(f"{tag}_xc{i}", [128, 16 * 128], BF16) for i in range(2)]
        onh = [csb(f"{tag}_oh{i}", [128, 1536], BF16) for i in range(2)]
        gsl = csb(f"{tag}_gsl", [128, 1536], F32)
        ogbf = csb(f"{tag}_ogb", [128, 1536], BF16)
        ogT = csb(f"{tag}_ogT", [128, 12 * 128], BF16)
        odr = csb(f"{tag}_odr", [128, 512], F32)
        resc = csb(f"{tag}_rsc", [128, 512], F32)
        for hf in range(2):
            kb.dma("gpsimd", wg_h[:],
                   APc(wts["Wg"].tensor, hf * 1536,
                       [[3072, 128], [128 * 3072, 16], [1, 1536]]),
                   reads=[wts["Wg"]], writes=[wg_h])
            kb.dma("gpsimd", wo_h[:], pack_rows(wts["Wo"], hf * 1536, 1536, D),
                   reads=[wts["Wo"]], writes=[wo_h])
            for b2 in range(B_LOC):
                for c in range(NCH):
                    cur = c % 2
                    xc, oh = xch[cur], onh[cur]
                    row0 = b2 * T_SEQ + c * CH
                    kb.dma("sync", xc[:],
                           APc(xsp.tensor, b2 * T_SEQ + c * CH,
                               [[16 * T_LOC, 128], [T_LOC, 16], [1, 128]]),
                           reads=[xsp], writes=[xc])
                    kb.dma("sync", oh[:],
                           onsp[row0:row0 + CH, hf * 1536:(hf + 1) * 1536],
                           reads=[onsp], writes=[oh])
                    for fs in range(3):
                        bank = psum_pool[fs]
                        for dc in range(16):
                            kb.op("tensor", lambda e, b=bank, dc=dc, fs=fs, xc=xc:
                                  e.matmul(b[:], xc[:, dc * 128:(dc + 1) * 128],
                                           wg_h[:, dc * 1536 + fs * 512:dc * 1536 + fs * 512 + 512],
                                           start=(dc == 0), stop=(dc == 15)),
                                  reads=[xc, wg_h], writes=[bank])
                        kb.op("scalar", lambda e, b=bank, fs=fs: e.activation(
                            out=gsl[:, fs * 512:(fs + 1) * 512], in_=b[:],
                            func=AF.Silu, bias=0.0, scale=1.0),
                            reads=[bank], writes=[gsl])
                    kb.op("vector", lambda e, oh=oh: e.tensor_tensor(
                        out=ogbf[:], in0=oh[:], in1=gsl[:], op=ALU.mult),
                        reads=[oh, gsl], writes=[ogbf])
                    for fc in range(12):
                        bank = psum_pool[3 + fc % 2]
                        pb_bf = bank[:, 0:64].bitcast(BF16)
                        kb.op("tensor", lambda e, pb=pb_bf, fc=fc: e.transpose(
                            pb, ogbf[:, fc * 128:(fc + 1) * 128], id_bf[:]),
                            reads=[ogbf, id_bf], writes=[bank])
                        kb.op("vector", lambda e, pb=pb_bf, fc=fc: e.tensor_copy(
                            out=ogT[:, fc * 128:(fc + 1) * 128], in_=pb),
                            reads=[bank], writes=[ogT])
                    for ds in range(4):
                        bank = psum_pool[5 + (ds % 3)]
                        for fc in range(12):
                            kb.op("tensor", lambda e, b=bank, fc=fc, ds=ds: e.matmul(
                                b[:], ogT[:, fc * 128:(fc + 1) * 128],
                                wo_h[:, fc * D + ds * 512:fc * D + ds * 512 + 512],
                                start=(fc == 0), stop=(fc == 11)),
                                reads=[ogT, wo_h], writes=[bank])
                        kb.op("scalar", lambda e, b=bank: e.copy(out=odr[:], in_=b[:]),
                              reads=[bank], writes=[odr])
                        src = xin if hf == 0 else xout
                        kb.dma("scalar", resc[:],
                               src[row0:row0 + CH, ds * 512:(ds + 1) * 512],
                               writes=[resc])
                        kb.op("vector", lambda e: e.tensor_tensor(
                            out=resc[:], in0=resc[:], in1=odr[:], op=ALU.add),
                            reads=[resc, odr], writes=[resc])
                        kb.dma("sync", xout[row0:row0 + CH, ds * 512:(ds + 1) * 512],
                               resc[:], reads=[resc])
    kb.barrier()


def cv_key_qbias(cv):
    return cv[QBIAS_VAL]


QBIAS_VAL = float(-0.5 * np.log(96.0))


def build_l0attn_stage():
    nc = bass.Bass("TRN2", target_bir_lowering=False)
    xin = nc.dram_tensor("xin", [T_LOC, D], F32, kind="ExternalInput").ap()
    xout = nc.dram_tensor("xout", [T_LOC, D], F32, kind="ExternalOutput").ap()
    wts = {}
    for name, shp in [("Wq", [2048, 1536]), ("Wk", [2048, 1536]), ("Wv", [2048, 3072]),
                      ("conv_q", [1536, 4]), ("conv_k", [1536, 4]), ("conv_v", [3072, 4]),
                      ("Wb", [2048, 16]), ("Wa", [2048, 16]), ("A_log", [16]),
                      ("dt_bias", [16]), ("Wg", [2048, 3072]),
                      ("norm_weight", [192]), ("Wo", [3072, 2048])]:
        wts[name] = nc.dram_tensor(name, shp, F32, kind="ExternalInput").ap()
    scr = dict(
        qsp=nc.dram_tensor("qsp", [2 * 12 * 128, T_SEQ], BF16, kind="Internal").ap(),
        ksp=nc.dram_tensor("ksp", [2 * 12 * 128, T_SEQ], BF16, kind="Internal").ap(),
        vsp=nc.dram_tensor("vsp", [2 * T_SEQ, 3072], BF16, kind="Internal").ap(),
        onsp=nc.dram_tensor("onsp", [2 * T_SEQ, 3072], BF16, kind="Internal").ap(),
        xsp=nc.dram_tensor("xsp", [128, 16 * T_LOC], BF16, kind="Internal").ap(),
    )
    with ExitStack() as ctx:
        kb = KB(nc, ctx)
        psum_pool = [ctx.enter_context(nc.psum_tensor(f"pp{i}", [128, 512], F32))
                     for i in range(8)]
        csts = declare_and_load_consts(kb, ctx, ["id_bf", "id_f32", "ones_f32",
                                                 "mask_lows", "mask_upi"])
        csts["mq_d"] = nc.dram_tensor("c_mq", [1536, 16], BF16, kind="ExternalInput").ap()
        csts["mqt_d"] = nc.dram_tensor("c_mqt", [16, 1536], BF16, kind="ExternalInput").ap()
        csts["sel16_d"] = nc.dram_tensor("c_sel16", [16, 2048], F32, kind="ExternalInput").ap()
        cvals = make_cvals(kb, ctx, [1.0, QBIAS_VAL])
        emit_l0attn(kb, ctx, xin, xout, wts, scr, csts, psum_pool, cvals)
        block = ctx.enter_context(nc.Block())
        kb.finalize(block)
    return nc



_CONSTS = None


def _consts():
    global _CONSTS
    if _CONSTS is None:
        _CONSTS = make_consts()
    return _CONSTS





def build_fused():
    nc = bass.Bass("TRN2", target_bir_lowering=False)
    X = nc.dram_tensor("X", [T_LOC, D], F32, kind="ExternalInput").ap()
    Y = nc.dram_tensor("Y", [T_LOC, D], F32, kind="ExternalOutput").ap()
    win = {}
    for name, shp in [("W1", [8192, 2048]), ("W3", [8192, 2048]), ("W2", [2048, 8192]),
                      ("WFC", [5120, 2048]), ("WPJ", [2048, 5120]),
                      ("Wq", [2048, 1536]), ("Wk", [2048, 1536]), ("Wv", [2048, 3072]),
                      ("conv_q", [1536, 4]), ("conv_k", [1536, 4]), ("conv_v", [3072, 4]),
                      ("Wb", [2048, 16]), ("Wa", [2048, 16]), ("A_log", [16]),
                      ("dt_bias", [16]), ("Wg", [2048, 3072]),
                      ("norm_weight", [192]), ("Wo", [3072, 2048]),
                      ("W1a", [2048, 6144]), ("b1a", [6144])]:
        win[name] = nc.dram_tensor(name, shp, F32, kind="ExternalInput").ap()
    dr = lambda n, shp, dt: nc.dram_tensor(n, shp, dt, kind="Internal").ap()
    xa = dr("xa", [T_LOC, D], F32)
    xb = dr("xb", [T_LOC, D], F32)
    w1tt = dr("w1tt", [64 * 2048, 128], BF16)
    w3tt = dr("w3tt", [64 * 2048, 128], BF16)
    w2pp = dr("w2pp", [4 * 64 * 128, 512], BF16)
    wfctt = dr("wfctt", [40 * 2048, 128], BF16)
    wpjtt = dr("wpjtt", [4 * 40 * 128, 512], BF16)
    w1bf = dr("w1bf", [8192, 2048], BF16)
    w3bf = dr("w3bf", [8192, 2048], BF16)
    w2bf = dr("w2bf", [2048, 8192], BF16)
    w2bfT = dr("w2bfT", [8192, 2048], BF16)
    wfcbf = dr("wfcbf", [5120, 2048], BF16)
    wpjbf = dr("wpjbf", [2048, 5120], BF16)
    scr = dict(
        qsp=dr("qsp", [2 * 12 * 128, T_SEQ], BF16),
        ksp=dr("ksp", [2 * 12 * 128, T_SEQ], BF16),
        vsp=dr("vsp", [2 * T_SEQ, 3072], BF16),
        onsp=dr("onsp", [2 * T_SEQ, 3072], BF16),
        xsp=dr("xsp", [128, 16 * T_LOC], BF16),
    )
    qsp1 = dr("qsp1", [2048, T_LOC], BF16)
    qsp2 = dr("qsp2", [2048, T_LOC], BF16)

    with ExitStack() as octx:
        psum_pool = [octx.enter_context(nc.psum_tensor(f"pp{i}", [128, 512], F32))
                     for i in range(8)]
        kbs = []
        dpool = DmaPool(nc, octx)

        def new_kb():
            if kbs:
                kbs[-1].seal()
            kb = KB(nc, octx, dpool)
            kbs.append(kb)
            return kb

        kb = new_kb()
        cst_names = ["id_bf", "id_f32", "ones_f32", "h64_bf", "h128_bf",
                     "mask_lows", "mask_upi"]
        csts = declare_and_load_consts(kb, octx, cst_names)
        csts["mq_d"] = nc.dram_tensor("c_mq", [1536, 16], BF16, kind="ExternalInput").ap()
        csts["mqt_d"] = nc.dram_tensor("c_mqt", [16, 1536], BF16, kind="ExternalInput").ap()
        csts["sel16_d"] = nc.dram_tensor("c_sel16", [16, 2048], F32, kind="ExternalInput").ap()
        cvals = make_cvals(kb, octx, [1.0, QBIAS_VAL])
        stile = octx.enter_context(nc.sbuf_tensor("sctile", [128, 1], F32))
        am_fc = octx.enter_context(nc.sbuf_tensor("am_fc", [1, 1], F32))
        am_pj = octx.enter_context(nc.sbuf_tensor("am_pj", [1, 1], F32))

        with ExitStack() as sctx:
            emit_cast_pass(kb, sctx, win["W1"], w1bf, 8192, 2048, "w1c")
            emit_transpose_tile_lhsT(kb, sctx, w1bf, w1tt, 8192, 2048, "w1t")
            emit_cast_pass(kb, sctx, win["W3"], w3bf, 8192, 2048, "w3c")
            emit_transpose_tile_lhsT(kb, sctx, w3bf, w3tt, 8192, 2048, "w3t")
            emit_cast_pass(kb, sctx, win["W2"], w2bf, 2048, 8192, "w2c",
                           rg_rows=256, bufs=1)
            emit_transpose_tile_rhsT_natural(kb, sctx, w2bf, w2bfT, "w2t")
            emit_fold_w2(kb, sctx, w2bfT, w2pp, csts, psum_pool)
        kb.barrier()
        with ExitStack() as sctx:
            emit_fold_h128(kb, sctx, w2pp, csts, psum_pool)
        kb.barrier()
        with ExitStack() as sctx:
            thr_fc = emit_absmean_thr(kb, sctx, win["WFC"], 5120, 2048, "amf",
                                      csts, psum_pool, rg_rows=256)
            emit_tern_cast_pass(kb, sctx, win["WFC"], wfcbf, 5120, 2048, "tfc",
                                thr_fc, rg_rows=256)
            emit_transpose_tile_lhsT(kb, sctx, wfcbf, wfctt, 5120, 2048, "wft")
            kb.op("vector", lambda e, t=thr_fc: e.tensor_copy(out=am_fc[:], in_=t[0:1, 2:3]),
                  reads=[thr_fc], writes=[am_fc])
        kb.barrier()
        with ExitStack() as sctx:
            thr_pj = emit_absmean_thr(kb, sctx, win["WPJ"], 2048, 5120, "amp",
                                      csts, psum_pool, rg_rows=128)
            emit_tern_cast_pass(kb, sctx, win["WPJ"], wpjbf, 2048, 5120, "tpj",
                                thr_pj, rg_rows=128)
            emit_transpose_tile_rhsT(kb, sctx, wpjbf, wpjtt, 2048, 5120, "wpt")
            kb.op("vector", lambda e, t=thr_pj: e.tensor_copy(out=am_pj[:], in_=t[0:1, 2:3]),
                  reads=[thr_pj], writes=[am_pj])
        kb.barrier()
        thr_fc, thr_pj = am_fc, am_pj
        kb.op("vector", lambda e: e.tensor_tensor(
            out=stile[0:1, :], in0=thr_fc[0:1, 0:1], in1=thr_fc[0:1, 0:1], op=ALU.mult),
            reads=[thr_fc], writes=[stile])
        kb.op("vector", lambda e: e.tensor_tensor(
            out=stile[0:1, :], in0=stile[0:1, :], in1=thr_pj[0:1, 0:1], op=ALU.mult),
            reads=[stile, thr_pj], writes=[stile])
        kb.op("vector", lambda e: e.tensor_scalar(
            out=stile[0:1, :], in0=stile[0:1, :], scalar1=0.5, scalar2=None,
            op0=ALU.mult), reads=[stile], writes=[stile])
        sc_d = dr("sc_d", [1, 1], F32)
        kb.dma("sync", sc_d, stile[0:1, :], reads=[stile])
        kb.dma("sync", stile[:], APc(sc_d.tensor, 0, [[0, 128], [1, 1]]), writes=[stile])

        kb = new_kb()
        with ExitStack() as sctx:
            emit_ffn(kb, sctx, X, xa, w1tt, w3tt, w2pp, 8192, csts, psum_pool,
                     tag="f0a", relu2=False)
        kb = new_kb()
        with ExitStack() as sctx:
            emit_l0attn(kb, sctx, xa, xb, win, scr, csts, psum_pool, cvals, tag="at0")
        kb = new_kb()
        with ExitStack() as sctx:
            emit_ffn(kb, sctx, xb, xa, w1tt, w3tt, w2pp, 8192, csts, psum_pool,
                     tag="f0b", relu2=False)
        kb = new_kb()
        with ExitStack() as sctx:
            emit_ffn(kb, sctx, xa, xb, wfctt, None, wpjtt, 5120, csts, psum_pool,
                     tag="f1a", relu2=True, scale_tile=stile)
        kb = new_kb()
        with ExitStack() as sctx:
            emit_l1attn(kb, sctx, xb, xa, win["W1a"], win["b1a"], qsp1, qsp2,
                        csts, psum_pool, tag="at1")
        kb = new_kb()
        with ExitStack() as sctx:
            emit_ffn(kb, sctx, xa, Y, wfctt, None, wpjtt, 5120, csts, psum_pool,
                     tag="f1b", relu2=True, scale_tile=stile)

        block = octx.enter_context(nc.Block())
        for kb in kbs:
            kb.finalize(block)
    return nc


def kernel(**inputs):
    cs = _consts()
    cores = list(range(N_CORES))
    x = np.ascontiguousarray(
        np.asarray(inputs["L0_attn_hidden_states"]).reshape(N_CORES, T_LOC, D))
    nc = build_fused()
    base = dict(
        W1=inputs["L0_ffn_W1"], W3=inputs["L0_ffn_W3"], W2=inputs["L0_ffn_W2"],
        WFC=inputs["L1_ffn_w_fc"], WPJ=inputs["L1_ffn_w_proj"],
        Wq=inputs["L0_attn_Wq"], Wk=inputs["L0_attn_Wk"], Wv=inputs["L0_attn_Wv"],
        conv_q=inputs["L0_attn_conv_q"], conv_k=inputs["L0_attn_conv_k"],
        conv_v=inputs["L0_attn_conv_v"], Wb=inputs["L0_attn_Wb"],
        Wa=inputs["L0_attn_Wa"], A_log=inputs["L0_attn_A_log"],
        dt_bias=inputs["L0_attn_dt_bias"], Wg=inputs["L0_attn_Wg"],
        norm_weight=inputs["L0_attn_norm_weight"], Wo=inputs["L0_attn_Wo"],
        W1a=inputs["L1_attn_W"], b1a=inputs["L1_attn_b"],
        c_id_bf=cs["id_bf"], c_id_f32=cs["id_f32"], c_ones_f32=cs["ones_f32"],
        c_h64_bf=cs["h64_bf"], c_h128_bf=cs["h128_bf"],
        c_mask_lows=cs["mask_lows"], c_mask_upi=cs["mask_upi"],
        c_mq=cs["mq"], c_mqt=cs["mqt"], c_sel16=cs["sel16"])
    in_maps = [dict(base, X=x[c]) for c in cores]
    res = run_bass_kernel_spmd(nc, in_maps, core_ids=cores)
    y = np.stack([np.asarray(res.results[c]["Y"]) for c in cores])
    return y.reshape(16, T_SEQ, D).astype(np.float32)


def kernel_m1(**inputs):
    cs = _consts()
    cores = list(range(N_CORES))
    x = np.ascontiguousarray(
        np.asarray(inputs["L0_attn_hidden_states"]).reshape(N_CORES, T_LOC, D))

    nc = build_prep(None)
    prep_in = dict(W1=inputs["L0_ffn_W1"], W3=inputs["L0_ffn_W3"],
                   W2=inputs["L0_ffn_W2"], WFC=inputs["L1_ffn_w_fc"],
                   WPJ=inputs["L1_ffn_w_proj"],
                   c_id_f32=cs["id_f32"], c_h64_bf=cs["h64_bf"])
    res = run_bass_kernel_spmd(nc, [prep_in] * N_CORES, core_ids=cores)
    pr = res.results[0]
    w1tt, w3tt, w2pp = pr["w1tt"], pr["w3tt"], pr["w2pp"]
    wfctt, wpjtt, scale = pr["wfctt"], pr["wpjtt"], pr["scale_o"]

    def shard_run(nc, mk_in):
        res = run_bass_kernel_spmd(nc, [mk_in(c) for c in cores], core_ids=cores)
        return np.stack([np.asarray(res.results[c]["xout"]) for c in cores])

    nc_f0 = build_ffn_stage(8192, relu2=False)
    x = shard_run(nc_f0, lambda c: dict(
        xin=x[c], w1tt=w1tt, w3tt=w3tt, w2pp=w2pp,
        c_id_bf=cs["id_bf"], c_h128_bf=cs["h128_bf"]))
    nc_a0 = build_l0attn_stage()
    wnames = dict(Wq="L0_attn_Wq", Wk="L0_attn_Wk", Wv="L0_attn_Wv",
                  conv_q="L0_attn_conv_q", conv_k="L0_attn_conv_k",
                  conv_v="L0_attn_conv_v", Wb="L0_attn_Wb", Wa="L0_attn_Wa",
                  A_log="L0_attn_A_log", dt_bias="L0_attn_dt_bias",
                  Wg="L0_attn_Wg", norm_weight="L0_attn_norm_weight",
                  Wo="L0_attn_Wo")
    attn_w = {k2: np.asarray(inputs[v]) for k2, v in wnames.items()}
    x = shard_run(nc_a0, lambda c: dict(
        xin=x[c], c_id_bf=cs["id_bf"], c_id_f32=cs["id_f32"],
        c_ones_f32=cs["ones_f32"], c_mask_lows=cs["mask_lows"],
        c_mask_upi=cs["mask_upi"], c_mq=cs["mq"], c_mqt=cs["mqt"], **attn_w))
    x = shard_run(nc_f0, lambda c: dict(
        xin=x[c], w1tt=w1tt, w3tt=w3tt, w2pp=w2pp,
        c_id_bf=cs["id_bf"], c_h128_bf=cs["h128_bf"]))
    nc_f1 = build_ffn_stage(5120, relu2=True)
    x = shard_run(nc_f1, lambda c: dict(
        xin=x[c], wfctt=wfctt, wpjtt=wpjtt, scale_i=scale, c_id_bf=cs["id_bf"]))
    nc_a1 = build_l1attn_stage()
    x = shard_run(nc_a1, lambda c: dict(
        xin=x[c], W=inputs["L1_attn_W"], b=inputs["L1_attn_b"],
        c_id_bf=cs["id_bf"], c_id_f32=cs["id_f32"]))
    x = shard_run(nc_f1, lambda c: dict(
        xin=x[c], wfctt=wfctt, wpjtt=wpjtt, scale_i=scale, c_id_bf=cs["id_bf"]))

    return x.reshape(16, T_SEQ, D).astype(np.float32)
